# version 02
# baseline (speedup 1.0000x reference)
import numpy as np
import ml_dtypes

import concourse.bass as bass
import concourse.mybir as mybir
from concourse.bass_utils import run_bass_kernel_spmd

F32 = mybir.dt.float32
BF16 = mybir.dt.bfloat16
U32 = mybir.dt.uint32
AF = mybir.ActivationFunctionType
ALU = mybir.AluOpType
AX = mybir.AxisListType
IOA = bass.IndirectOffsetOnAxis

L = 2048
DM = 2048
DIN = 4096
HEADS = 64
HD = 64
STATE = 128
KCONV = 4
CD = DIN + 2 * STATE
PROJ = DIN + CD + HEADS
NMT = 67
CH = 256
NCH = L // CH
EPS = 1e-5
NE = 32
TOPK = 8
SSEL = 4
THR = 0.5
DFF = 768
CAP = 1024
NDISP = NE * CAP
NTT = L // 128
NKT = DM // 128
MASKNEG = -1e30



from contextlib import contextmanager, ExitStack


@contextmanager
def _multi(*cms):
    with ExitStack() as _st:
        yield [_st.enter_context(cm) for cm in cms]


class Sem:
    __slots__ = ("h", "n")

    def __init__(self, stack, nc, name):
        self.h = stack.enter_context(nc.semaphore(name))
        self.n = 0

    def inc(self, instr, k=1):
        instr.then_inc(self.h, k)
        self.n += k
        return instr


def build_nc(dbg=None, upto=99, serial_b=False):
    nc = bass.Bass("TRN2", target_bir_lowering=False)

    def dram(name, shape, dt, kind):
        return nc.dram_tensor(name, shape, dt, kind=kind).ap()

    x_in = dram("x", [L, DM], F32, "ExternalInput")
    w_in_t = dram("w_in_t", [NMT, DM, 128], F32, "ExternalInput")
    conv_w_t = dram("conv_w_t", [128, 34, KCONV], F32, "ExternalInput")
    conv_b_t = dram("conv_b_t", [128, 34], F32, "ExternalInput")
    dt_bias_c = dram("dt_bias_c", [64, 1], F32, "ExternalInput")
    a_log_c = dram("a_log_c", [64, 1], F32, "ExternalInput")
    d_row_i = dram("d_row", [1, 64], F32, "ExternalInput")
    norm_w_t = dram("norm_w_t", [128, 32], F32, "ExternalInput")
    w_out_t = dram("w_out_t", [32, 128, DM], F32, "ExternalInput")
    w_g_t = dram("w_g_t", [NKT, 128, NE], F32, "ExternalInput")
    w_gu_t = dram("w_gu_t", [NE, 12, DM, 128], F32, "ExternalInput")
    w_d_t = dram("w_d_t", [NE, DFF, DM], F32, "ExternalInput")
    sim_i = dram("sim", [NE, NE], F32, "ExternalInput")
    ident_bf_i = dram("ident_bf", [128, 128], BF16, "ExternalInput")
    ident_f_i = dram("ident_f", [128, 128], F32, "ExternalInput")
    iota32_i = dram("iota32", [128, NE], F32, "ExternalInput")
    trash_i = dram("trash_c", [128, 1], F32, "ExternalInput")
    mask_i = dram("mask_st", [2, 128, CH], F32, "ExternalInput")
    ustrict_i = dram("ustrict", [128, 128], F32, "ExternalInput")
    ones_r1_i = dram("ones_r1", [1, 128], F32, "ExternalInput")
    ones_c_i = dram("ones_c", [128, 1], F32, "ExternalInput")
    bcsel_i = dram("bcsel", [64, HEADS, 128], BF16, "ExternalInput")
    noti_i = dram("noti", [NE, NE], F32, "ExternalInput")
    i32_i = dram("i32d", [NE, NE], F32, "ExternalInput")
    misc_i = dram("misc_c", [128, 8], F32, "ExternalInput")

    x_tm_d = dram("x_tm_d", [L, DIN], BF16, "Internal")
    sgate_d = dram("sgate_d", [L, DIN], BF16, "Internal")
    hgn_d = dram("hgn_d", [32, 128, L], BF16, "Internal")
    ln2_d = dram("ln2_d", [L, DM], BF16, "Internal")
    x2_d = dram("x2_d", [L, DM], F32, "Internal")
    xdisp_d = dram("xdisp_d", [NDISP + 128, DM], BF16, "Internal")
    ydisp_d = dram("ydisp_d", [NDISP + 128, DM], BF16, "Internal")

    out_d = dram("out", [L, DM], F32, "ExternalOutput")
    dbg_outs = {}
    if dbg:
        for name, shape, dt in dbg:
            dbg_outs[name] = dram("dbg_" + name, shape, dt, "ExternalOutput")

    es = ExitStack()
    with es:
        def sb(name, shape, dt):
            return es.enter_context(nc.sbuf_tensor(name, shape, dt))

        identb = sb("identb", [128, 128], BF16)
        identf = sb("identf", [128, 128], F32)
        iota32 = sb("iota32s", [128, NE], F32)
        trash = sb("trashs", [128, 1], F32)
        maskst = sb("masksts", [128, 2, CH], F32)
        ustrict = sb("ustricts", [128, 128], F32)
        ones_r1 = sb("ones_r1s", [1, 128], F32)
        ones_c = sb("ones_cs", [128, 1], F32)
        convw = sb("convws", [128, 34, KCONV], F32)
        convb = sb("convbs", [128, 34], F32)
        dtbias = sb("dtbiass", [64, 1], F32)
        alogc = sb("alogcs", [64, 1], F32)
        acol = sb("acols", [64, 1], F32)
        drow = sb("drows", [1, 64], F32)
        dbc = sb("dbcs", [128, 64], F32)
        normw = sb("normws", [128, 32], F32)
        simd = sb("simds", [NE, NE], F32)
        simraw = sb("simraws", [NE, NE], F32)
        noti = sb("notis", [NE, NE], F32)
        i32d = sb("i32ds", [NE, NE], F32)
        miscc = sb("misccs", [128, 8], F32)
        wg_hi = sb("wg_hi", [128, NKT, NE], BF16)
        wg_lo = sb("wg_lo", [128, NKT, NE], BF16)
        wg_f = sb("wg_f", [128, NKT, NE], F32)
        bcsel = sb("bcsels", [64, HEADS, 128], BF16)

        logits_tm = sb("logits_tm", [128, NTT, NE], F32)
        w8_all = sb("w8_all", [128, NTT, TOPK], F32)
        idx8_all = sb("idx8_all", [128, NTT, TOPK], F32)
        posu_all = sb("posu_all", [128, TOPK * NTT], U32)

        names = ["qc", "qa", "qw", "qs", "qi", "pe", "dv", "dw", "ev", "ac", "gp"]
        S = {k: Sem(es, nc, "sem_" + k) for k in names}
        qc, qa, qw, qs, qi, pe, dv, dw, ev, ac, gp = (S[k] for k in names)

        blk = es.enter_context(nc.Block())


        def phase_barrier():
            bar = [(dv, dv.n), (dw, dw.n), (ev, ev.n), (ac, ac.n), (gp, gp.n),
                   (pe, pe.n), (qs, qs.n), (qi, qi.n), (qa, qa.n), (qw, qw.n),
                   (qc, qc.n)]

            def mk(eng):
                for s2, v2 in bar:
                    eng.wait_ge(s2.h, v2)
            blk.tensor(mk)
            blk.vector(mk)
            blk.scalar(mk)
            blk.gpsimd(mk)
            blk.sync(mk)

        def p0_sync(s):
            for dst, src in [
                (identb, ident_bf_i), (identf, ident_f_i), (iota32, iota32_i),
                (trash, trash_i), (ustrict, ustrict_i),
                (ones_r1, ones_r1_i), (ones_c, ones_c_i), (bcsel, bcsel_i),
                (dtbias, dt_bias_c), (alogc, a_log_c), (drow, d_row_i),
                (simraw, sim_i), (noti, noti_i), (i32d, i32_i), (miscc, misc_i),
            ]:
                qc.inc(s.dma_start(out=dst[:], in_=src[:]), 16)
            qc.inc(s.dma_start(out=maskst[:], in_=mask_i.rearrange("a p c -> p a c")), 16)
            qc.inc(s.dma_start(out=convw[:], in_=conv_w_t[:]), 16)
            qc.inc(s.dma_start(out=convb[:], in_=conv_b_t[:]), 16)
            qc.inc(s.dma_start(out=normw[:], in_=norm_w_t[:]), 16)
            qc.inc(s.dma_start(out=wg_f[:], in_=w_g_t.rearrange("a p c -> p a c")), 16)
        blk.sync(p0_sync)
        qc_all = qc.n

        p0ps_cm = nc.psum_tensor("p0ps", [128, 64], F32)
        p0ps = p0ps_cm.__enter__()

        def p0_pe(t):
            t.wait_ge(qc.h, qc_all)
            pe.inc(t.matmul(p0ps[:], ones_r1[:], drow[:], start=True, stop=True))
        blk.tensor(p0_pe)
        snap_p0pe = pe.n

        def p0_dv(v):
            v.wait_ge(qc.h, qc_all)
            v.wait_ge(pe.h, snap_p0pe)
            v.tensor_copy(out=dbc[:], in_=p0ps[:])
            v.tensor_tensor(out=simd[:], in0=simraw[:], in1=noti[:], op=ALU.mult)
            v.tensor_tensor(out=simd[:], in0=simd[:], in1=i32d[:], op=ALU.add)
            v.tensor_copy(out=wg_hi[:], in_=wg_f[:])
            dv.inc(v.tensor_tensor(out=wg_lo[:], in0=wg_f[:], in1=wg_hi[:], op=ALU.subtract))
        blk.vector(p0_dv)
        snap_p0dv = dv.n

        def p0_ac(s):
            s.wait_ge(qc.h, qc_all)
            s.activation(out=acol[:], in_=alogc[:], func=AF.Exp)
            ac.inc(s.mul(acol[:], acol[:], -1.0))
        blk.scalar(p0_ac)
        snap_p0ac = ac.n
        p0ps_cm.__exit__(None, None, None)

        mres_cm = [
            nc.sbuf_tensor("b_fm", [128, L], BF16),
            nc.sbuf_tensor("c_fm", [128, L], BF16),
            nc.sbuf_tensor("b_tm", [128, NTT, 128], BF16),
            nc.sbuf_tensor("dt_fm", [64, L], F32),
            nc.sbuf_tensor("da_fm", [64, L], F32),
            nc.sbuf_tensor("dt_tm", [128, NTT, 64], F32),
            nc.sbuf_tensor("s_all", [128, HEADS * HD], F32),
            nc.sbuf_tensor("s_bf", [128, HEADS * HD], BF16),
        ]
        b_fm, c_fm, b_tm, dt_fm, da_fm, dt_tm, s_all, s_bf = [cm.__enter__() for cm in mres_cm]

        if upto >= 1:
            ln1_cm = nc.sbuf_tensor("ln1xT", [128, NKT, L], BF16)
            ln1xT = ln1_cm.__enter__()
            with _multi(
                nc.sbuf_tensor("axt", [128, 2, DM], F32),
                nc.sbuf_tensor("axc", [128, 2, DM], F32),
                nc.sbuf_tensor("acl", [128, 8], F32),
                nc.sbuf_tensor("alnb", [128, 2, DM], BF16),
                nc.psum_tensor("aptb", [128, 2, 1024], BF16),
            ) as (axt, axc, acl, alnb, aptb):
                qa_a0 = qa.n
                snapA1 = [None] * NTT
                snapAT = [None] * NTT
                snapApt = {}
                snapApc = {}
                snapApc = {}
                for i in range(NTT):
                    p = i % 2

                    def a_sy(s, i=i, p=p):
                        if i >= 2:
                            s.wait_ge(dv.h, snapA1[i - 2])
                        qa.inc(s.dma_start(out=axt[:, p, :], in_=x_in[i * 128:(i + 1) * 128, :]), 16)
                    blk.sync(a_sy)

                    def a_dv1(v, i=i, p=p):
                        v.wait_ge(qa.h, qa_a0 + 16 * (i + 1))
                        v.tensor_reduce(out=acl[:, 0:1], in_=axt[:, p, :], axis=AX.X, op=ALU.add)
                        v.drain()
                        v.tensor_scalar(out=acl[:, 1:2], in0=acl[:, 0:1], scalar1=1.0 / DM,
                                        scalar2=None, op0=ALU.mult)
                        v.drain()
                        dv.inc(v.tensor_scalar(out=axc[:, p, :], in0=axt[:, p, :], scalar1=acl[:, 1:2],
                                               scalar2=None, op0=ALU.subtract))
                    blk.vector(a_dv1)
                    snapA1[i] = dv.n

                    def a_ac(s, i=i, p=p):
                        s.wait_ge(dv.h, snapA1[i])
                        if i >= 2:
                            s.wait_ge(pe.h, snapAT[i - 2])
                        s.activation(out=alnb[:, p, :], in_=axc[:, p, :], func=AF.Square,
                                     accum_out=acl[:, 4:5])
                        ac.inc(s.activation(out=acl[:, 5:6], in_=acl[:, 4:5], func=AF.Sqrt,
                                            scale=miscc[:, 2:3], bias=miscc[:, 1:2]))
                    blk.scalar(a_ac)
                    snapAac = ac.n

                    def a_dv2(v, i=i, p=p, snapAac=snapAac):
                        v.wait_ge(ac.h, snapAac)
                        if i >= 2:
                            v.wait_ge(pe.h, snapAT[i - 2])
                        v.reciprocal(out=acl[:, 6:7], in_=acl[:, 5:6])
                        v.drain()
                        dv.inc(v.tensor_scalar(out=alnb[:, p, :], in0=axc[:, p, :], scalar1=acl[:, 6:7],
                                               scalar2=None, op0=ALU.mult))
                    blk.vector(a_dv2)
                    snapA2 = dv.n

                    for grp in range(2):
                        G = i * 2 + grp
                        bk = G % 2
                        dw_g2 = snapApc.get(G - 2)

                        def a_pe(t, i=i, p=p, grp=grp, bk=bk, snapA2=snapA2, dw_g2=dw_g2):
                            if grp == 0:
                                t.wait_ge(dv.h, snapA2)
                            if dw_g2 is not None:
                                t.wait_ge(dw.h, dw_g2)
                            for kk in range(8):
                                kt = grp * 8 + kk
                                sl = aptb[:, bk, kk * 128:kk * 128 + 128]
                                pe.inc(t.transpose(sl, alnb[:, p, kt * 128:(kt + 1) * 128], identb[:]))
                        blk.tensor(a_pe)
                        snapApt[G] = pe.n

                        def a_dv3(v, i=i, p=p, grp=grp, bk=bk, G=G):
                            v.wait_ge(pe.h, snapApt[G])
                            for kk in range(8):
                                kt = grp * 8 + kk
                                dw.inc(v.tensor_copy(out=ln1xT[:, kt, i * 128:(i + 1) * 128],
                                                     in_=aptb[:, bk, kk * 128:kk * 128 + 128]))
                        blk.vector(a_dv3)
                        snapApc[G] = dw.n
                    snapAT[i] = pe.n
                snapA_dv = dv.n
                snapA_dw = dw.n
                if dbg and any(n == "ln1xT" for n, _, _ in dbg):
                    def a_dump(s):
                        s.wait_ge(dw.h, dw.n)
                        s.wait_ge(dv.h, dv.n)
                        s.wait_ge(pe.h, pe.n)
                        qs.inc(s.dma_start(out=dbg_outs["ln1xT"][:],
                                           in_=ln1xT.rearrange("p a b -> p (a b)")), 16)
                        s.wait_ge(qs.h, qs.n)
                    blk.sync(a_dump)

            phase_barrier()
        if upto >= 2:
            with _multi(
                nc.sbuf_tensor("wbuf", [128, 2, NKT, 128], BF16),
                nc.sbuf_tensor("gcx", [128, DM], BF16),
                nc.sbuf_tensor("stage", [128, 2, NKT, 128], BF16),
                nc.sbuf_tensor("convin", [128, 3 + L], F32),
                nc.sbuf_tensor("cacc", [128, L], F32),
                nc.psum_tensor("bps", [128, 4, 512], F32),
                nc.psum_tensor("bptb", [128, 2, 1024], BF16),
                nc.psum_tensor("bptf", [128, 2, 512], F32),
            ) as (wbuf, gcx, stage, convin, cacc, bps, bptb, bptf):
                dtr = convin[0:64, 3:3 + L]
                dtt1 = cacc[0:64, 0:L]

                def b_init(v):
                    dv.inc(v.memset(convin[:, 0:3], 0.0))
                blk.vector(b_init)
                snapBz = dv.n

                qw_b0 = qw.n
                qwsnapB = {}
                bgrp_ctr = [0]
                snapBgt = {}
                snapBgc = {}
                ev_b0 = ev.n
                evB_before = {}
                snapBmm = [None] * NMT
                snapBsilu = {}
                snapBT = {}
                snapBC = {}
                last_stage_dma = [None, None]
                last_gcx_T = [None]

                def sec_of(m):
                    return "gate" if m < 32 else ("xbc" if m < 66 else "dt")

                for m0 in range(2):
                    def b_gp0(g, m0=m0):
                        if m0 == 0:
                            g.wait_ge(qc.h, qc_all)
                        qw.inc(g.dma_start(out=wbuf[:, m0 % 2, :, :],
                                           in_=w_in_t[m0].rearrange("(kt kp) m -> kp kt m", kp=128)), 16)
                        qwsnapB[m0] = qw.n
                    blk.gpsimd(b_gp0)

                for m in range(NMT):
                    if serial_b:
                        phase_barrier()
                    sec = sec_of(m)
                    evB_before[m] = ev.n

                    def b_pe(t, m=m):
                        t.wait_ge(qw.h, qwsnapB[m])
                        if m == 0:
                            t.wait_ge(dv.h, snapA_dv)
                            t.wait_ge(dw.h, snapA_dw)
                        for nt in range(4):
                            if m >= 1:
                                t.wait_ge(ev.h, evB_before[m - 1] + nt + 1)
                            for kt in range(NKT):
                                li = t.matmul(bps[:, nt, :], wbuf[:, m % 2, kt, :],
                                              ln1xT[:, kt, nt * 512:(nt + 1) * 512],
                                              start=(kt == 0), stop=(kt == NKT - 1))
                        pe.inc(li)
                    blk.tensor(b_pe)
                    snapBmm[m] = pe.n

                    if m + 2 < NMT:
                        def b_gp(g, m=m):
                            g.wait_ge(pe.h, snapBmm[m])
                            qw.inc(g.dma_start(out=wbuf[:, (m + 2) % 2, :, :],
                                               in_=w_in_t[m + 2].rearrange("(kt kp) m -> kp kt m",
                                                                           kp=128)), 16)
                            qwsnapB[m + 2] = qw.n
                        blk.gpsimd(b_gp)

                    if sec == "gate":
                        def b_ac(s, m=m):
                            s.wait_ge(pe.h, snapBmm[m])
                            if last_gcx_T[0] is not None:
                                s.wait_ge(pe.h, last_gcx_T[0])
                            for nt in range(4):
                                i2 = s.activation(out=gcx[:, nt * 512:(nt + 1) * 512],
                                                  in_=bps[:, nt, :], func=AF.Silu)
                                ev.inc(i2)
                        blk.scalar(b_ac)
                        snapBsilu_ev = ev.n

                        for grp in range(2):
                            G = bgrp_ctr[0]
                            bgrp_ctr[0] += 1
                            bk = G % 2
                            dw_g2 = snapBgc.get(G - 2)

                            def b_pet(t, m=m, grp=grp, bk=bk, dw_g2=dw_g2, snapBsilu_ev=snapBsilu_ev):
                                if grp == 0:
                                    t.wait_ge(ev.h, snapBsilu_ev)
                                    if last_stage_dma[m % 2] is not None:
                                        t.wait_ge(qs.h, last_stage_dma[m % 2])
                                if dw_g2 is not None:
                                    t.wait_ge(dw.h, dw_g2)
                                for kk in range(8):
                                    tt = grp * 8 + kk
                                    sl = bptb[:, bk, kk * 128:kk * 128 + 128]
                                    pe.inc(t.transpose(sl, gcx[:, tt * 128:(tt + 1) * 128], identb[:]))
                            blk.tensor(b_pet)
                            snapBgt[G] = pe.n

                            def b_dvc(v, m=m, grp=grp, bk=bk, G=G):
                                v.wait_ge(pe.h, snapBgt[G])
                                for kk in range(8):
                                    tt = grp * 8 + kk
                                    dw.inc(v.tensor_copy(out=stage[:, m % 2, tt, :],
                                                         in_=bptb[:, bk, kk * 128:kk * 128 + 128]))
                            blk.vector(b_dvc)
                            snapBgc[G] = dw.n
                        last_gcx_T[0] = pe.n
                        snapBcp = dw.n

                        def b_sy(s, m=m, snapBcp=snapBcp):
                            s.wait_ge(dw.h, snapBcp)
                            od = sgate_d.rearrange("(tt p) f -> p tt f", p=128)
                            qs.inc(s.dma_start(out=od[:, :, m * 128:(m + 1) * 128],
                                               in_=stage[:, m % 2, :, :]), 16)
                        blk.sync(b_sy)
                        last_stage_dma[m % 2] = qs.n

                    elif sec == "xbc":
                        ct = m - 32

                        def b_dv(v, m=m, ct=ct):
                            v.wait_ge(pe.h, snapBmm[m])
                            if m == 32:
                                v.wait_ge(dv.h, snapBz)
                            for nt in range(4):
                                ev.inc(v.tensor_copy(out=convin[:, 3 + nt * 512:3 + (nt + 1) * 512],
                                                     in_=bps[:, nt, :]))
                            if ct > 0:
                                v.wait_ge(ac.h, snapBsilu[m - 1])
                            v.tensor_scalar(out=cacc[:], in0=convin[:, 0:L],
                                            scalar1=convw[:, ct, 0:1], scalar2=convb[:, ct:ct + 1],
                                            op0=ALU.mult, op1=ALU.add)
                            for kk in range(1, KCONV):
                                li = v.scalar_tensor_tensor(out=cacc[:], in0=convin[:, kk:kk + L],
                                                            scalar=convw[:, ct, kk:kk + 1], in1=cacc[:],
                                                            op0=ALU.mult, op1=ALU.add)
                            dv.inc(li)
                        blk.vector(b_dv)
                        snapBcv = dv.n

                        def b_ac(s, m=m, ct=ct, snapBcv=snapBcv):
                            s.wait_ge(dv.h, snapBcv)
                            if ct < 32 and last_gcx_T[0] is not None:
                                s.wait_ge(pe.h, last_gcx_T[0])
                            dst = b_fm if ct == 32 else (c_fm if ct == 33 else gcx)
                            ac.inc(s.activation(out=dst[:], in_=cacc[:], func=AF.Silu))
                        blk.scalar(b_ac)
                        snapBsilu[m] = ac.n

                        if ct <= 32:
                            for grp in range(2):
                                G = bgrp_ctr[0]
                                bgrp_ctr[0] += 1
                                bk = G % 2
                                dw_g2 = snapBgc.get(G - 2)

                                def b_pet(t, m=m, ct=ct, grp=grp, bk=bk, dw_g2=dw_g2):
                                    if grp == 0:
                                        t.wait_ge(ac.h, snapBsilu[m])
                                        if ct < 32 and last_stage_dma[m % 2] is not None:
                                            t.wait_ge(qs.h, last_stage_dma[m % 2])
                                    if dw_g2 is not None:
                                        t.wait_ge(dw.h, dw_g2)
                                    src = b_fm if ct == 32 else gcx
                                    for kk in range(8):
                                        tt = grp * 8 + kk
                                        sl = bptb[:, bk, kk * 128:kk * 128 + 128]
                                        pe.inc(t.transpose(sl, src[:, tt * 128:(tt + 1) * 128],
                                                           identb[:]))
                                blk.tensor(b_pet)
                                snapBgt[G] = pe.n

                                def b_dvc(v, m=m, ct=ct, grp=grp, bk=bk, G=G):
                                    v.wait_ge(pe.h, snapBgt[G])
                                    for kk in range(8):
                                        tt = grp * 8 + kk
                                        dst = b_tm[:, tt, :] if ct == 32 else stage[:, m % 2, tt, :]
                                        dw.inc(v.tensor_copy(
                                            out=dst, in_=bptb[:, bk, kk * 128:kk * 128 + 128]))
                                blk.vector(b_dvc)
                                snapBgc[G] = dw.n
                            if ct < 32:
                                last_gcx_T[0] = pe.n
                            snapBcp = dw.n

                            if ct < 32:
                                def b_sy(s, m=m, ct=ct, snapBcp=snapBcp):
                                    s.wait_ge(dw.h, snapBcp)
                                    od = x_tm_d.rearrange("(tt p) f -> p tt f", p=128)
                                    qs.inc(s.dma_start(out=od[:, :, ct * 128:(ct + 1) * 128],
                                                       in_=stage[:, m % 2, :, :]), 16)
                                blk.sync(b_sy)
                                last_stage_dma[m % 2] = qs.n
                    else:
                        def b_dv(v, m=m):
                            v.wait_ge(pe.h, snapBmm[m])
                            v.wait_ge(ac.h, snapBsilu[65])
                            for nt in range(4):
                                ev.inc(v.tensor_copy(out=dtr[:, nt * 512:(nt + 1) * 512],
                                                     in_=bps[0:64, nt, :]))
                            dv.inc(v.tensor_scalar(out=dtr[:], in0=dtr[:], scalar1=dtbias[:],
                                                   scalar2=None, op0=ALU.add))
                        blk.vector(b_dv)
                        snapBdt1 = dv.n

                        def b_ac(s, m=m, snapBdt1=snapBdt1):
                            s.wait_ge(dv.h, snapBdt1)
                            s.activation(out=dtt1[:], in_=dtr[:], func=AF.Abs)
                            s.activation(out=dtt1[:], in_=dtt1[:], func=AF.Exp, scale=miscc[0:64, 4:5])
                            ac.inc(s.activation(out=dtt1[:], in_=dtt1[:], func=AF.Ln,
                                                bias=miscc[0:64, 0:1]))
                        blk.scalar(b_ac)
                        snapBdt2 = ac.n

                        def b_dv2(v, m=m, snapBdt2=snapBdt2):
                            v.wait_ge(ac.h, snapBdt2)
                            v.wait_ge(ac.h, snap_p0ac)
                            v.tensor_scalar(out=dt_fm[:], in0=dtr[:], scalar1=0.0, scalar2=None,
                                            op0=ALU.max)
                            v.tensor_tensor(out=dt_fm[:], in0=dt_fm[:], in1=dtt1[:], op=ALU.add)
                            dv.inc(v.tensor_scalar(out=da_fm[:], in0=dt_fm[:], scalar1=acol[:],
                                                   scalar2=None, op0=ALU.mult))
                        blk.vector(b_dv2)
                        snapBdt3 = dv.n

                        for grp in range(2):
                            bk = grp % 2

                            def b_pe2(t, m=m, grp=grp, bk=bk, snapBdt3=snapBdt3):
                                if grp == 0:
                                    t.wait_ge(dv.h, snapBdt3)
                                for kk in range(8):
                                    tt = grp * 8 + kk
                                    pe.inc(t.transpose(bptf[:, bk, kk * 64:kk * 64 + 64],
                                                       dt_fm[:, tt * 128:(tt + 1) * 128],
                                                       identf[0:64, 0:64]))
                            blk.tensor(b_pe2)
                            sBt = pe.n

                            def b_dv3(v, m=m, grp=grp, bk=bk, sBt=sBt):
                                v.wait_ge(pe.h, sBt)
                                for kk in range(8):
                                    tt = grp * 8 + kk
                                    dw.inc(v.tensor_copy(out=dt_tm[:, tt, :],
                                                         in_=bptf[:, bk, kk * 64:kk * 64 + 64]))
                            blk.vector(b_dv3)

                snapB_dv = dv.n
                snapB_dw = dw.n
                snapB_ac = ac.n
                qs_b_end = qs.n
            ln1_cm.__exit__(None, None, None)

            phase_barrier()
        if upto >= 3:
            with _multi(
                nc.sbuf_tensor("xch", [128, 2, 2, DIN], BF16),
                nc.sbuf_tensor("sgch", [128, 2, DIN], BF16),
                nc.sbuf_tensor("dacum", [64, CH], F32),
                nc.sbuf_tensor("dachi", [64, CH], BF16),
                nc.sbuf_tensor("daclo", [64, CH], BF16),
                nc.sbuf_tensor("dacum_tm", [128, 2, 64], F32),
                nc.sbuf_tensor("expda_tm", [128, 2, 64], F32),
                nc.sbuf_tensor("zeros64", [64, CH], F32),
                nc.sbuf_tensor("segT", [128, 2, 2, CH], F32),
                nc.sbuf_tensor("mts", [128, 2, 2, CH], BF16),
                nc.sbuf_tensor("xdt", [128, 2, 2, HD], BF16),
                nc.sbuf_tensor("xdec", [128, 2, 2, HD], BF16),
                nc.sbuf_tensor("ydc", [128, 2, 2, HD], F32),
                nc.sbuf_tensor("xdd", [128, 2, 2, HD], F32),
                nc.sbuf_tensor("y1c", [128, 2, 2, HD], F32),
                nc.sbuf_tensor("ych", [128, 2, DIN], BF16),
                nc.sbuf_tensor("ecol", [128, 64], F32),
                nc.sbuf_tensor("erow", [1, 64], F32),
                nc.sbuf_tensor("sqs", [128, DIN], BF16),
                nc.sbuf_tensor("rcol", [128, 8], F32),
                nc.sbuf_tensor("hstage", [128, 32, 128], BF16),
                nc.psum_tensor("cpsS", [128, 2, CH], F32),
                nc.psum_tensor("cpA", [128, 2, 512], F32),
                nc.psum_tensor("cpY", [128, 2, 512], F32),
                nc.psum_tensor("cpM", [128, 512], F32),
                nc.psum_tensor("cptb", [128, 2, 1024], BF16),
            ) as (xch, sgch, dacum, dachi, daclo, dacum_tm, expda_tm, zeros64, segT, mts, xdt, xdec, ydc, xdd, y1c, ych, ecol, erow, sqs, rcol, hstage, cpsS, cpA, cpY, cpM, cptb):
                def c_init(v):
                    v.wait_ge(qs.h, qs_b_end)
                    v.memset(s_all[:], 0.0)
                    v.memset(s_bf[:], 0.0)
                    dv.inc(v.memset(zeros64[:], 0.0))
                blk.vector(c_init)
                snapCini = dv.n

                qa_c0 = qa.n
                snapCgpend = [None] * NCH
                snapChgmul = [None] * NCH
                snapCsbf = [None] * NCH
                snapCh2last = [None] * NCH
                snapCecl = [None] * NCH
                snapCerow = [None] * NCH
                snapChgT = [None] * NCH
                snapCcprT = [None] * NCH
                last_hstage_dma = [None]
                cgrp_ctr = [0]
                snapCgt = {}
                snapCgc = {}

                xv = x_tm_d.rearrange("(c tt p) f -> p c tt f", p=128, tt=2)
                gv = sgate_d.rearrange("(c tt p) f -> p c tt f", p=128, tt=2)
                for c in range(NCH):
                    cp = c % 2

                    def c_sy(s, c=c, cp=cp):
                        if c == 0:
                            s.wait_ge(dv.h, snapCini)
                        if c >= 2:
                            s.wait_ge(gp.h, snapCgpend[c - 2])
                            s.wait_ge(dv.h, snapChgmul[c - 2])
                        qa.inc(s.dma_start(out=xch[:, cp, :, :], in_=xv[:, c, :, :]), 16)
                        if c >= 1:
                            s.wait_ge(dv.h, snapChgmul[c - 1])
                            s.wait_ge(pe.h, snapChgT[c - 1])
                        qa.inc(s.dma_start(out=sgch[:], in_=gv[:, c, :, :]), 16)
                    blk.sync(c_sy)

                    def c_scan(v, c=c):
                        v.wait_ge(qa.h, qa_c0 + 32 * (c + 1))
                        if c >= 1:
                            v.wait_ge(pe.h, snapCcprT[c - 1])
                        v.tensor_tensor_scan(out=dacum[:], data0=da_fm[:, c * CH:(c + 1) * CH],
                                             data1=zeros64[:], initial=0.0, op0=ALU.add, op1=ALU.add)
                        v.tensor_copy(out=dachi[:], in_=dacum[:])
                        dv.inc(v.tensor_tensor(out=daclo[:], in0=dacum[:], in1=dachi[:],
                                               op=ALU.subtract))
                    blk.vector(c_scan)
                    snapCscan = dv.n

                    def c_pe0(t, c=c, snapCscan=snapCscan):
                        t.wait_ge(dv.h, snapCscan)
                        if c == 0:
                            t.wait_ge(dv.h, snapB_dv)
                            t.wait_ge(dw.h, snapB_dw)
                            t.wait_ge(ac.h, snapB_ac)
                        if c >= 1:
                            t.wait_ge(ac.h, snapCerow[c - 1])
                        pe.inc(t.transpose(cpM[0:1, 128:192], dacum[:, CH - 1:CH], identf[0:64, 0:64]))
                    blk.tensor(c_pe0)
                    snapCcpr = pe.n

                    def c_ac0(s, c=c, snapCcpr=snapCcpr):
                        s.wait_ge(pe.h, snapCcpr)
                        ac.inc(s.activation(out=erow[:], in_=cpM[0:1, 128:192], func=AF.Exp))
                    blk.scalar(c_ac0)
                    snapCerow[c] = ac.n

                    def c_pe0b(t, c=c):
                        t.wait_ge(ac.h, snapCerow[c])
                        if c >= 1:
                            t.wait_ge(dv.h, snapCecl[c - 1])
                        pe.inc(t.matmul(cpM[:, 0:64], ones_r1[:], erow[:], start=True, stop=True))
                    blk.tensor(c_pe0b)
                    snapCpec = pe.n

                    def c_dve(v, c=c, snapCpec=snapCpec):
                        v.wait_ge(pe.h, snapCpec)
                        dv.inc(v.tensor_copy(out=ecol[:], in_=cpM[:, 0:64]))
                    blk.vector(c_dve)
                    snapCecl_pre = dv.n

                    for st in range(2):
                        dv_now = dv.n

                        def c_pt(t, c=c, st=st, snapCscan=snapCscan, dv_now=dv_now):
                            if st == 0:
                                t.wait_ge(dv.h, snapCscan)
                            t.wait_ge(dv.h, dv_now)
                            pe.inc(t.transpose(cpM[:, 64:128], dacum[:, st * 128:(st + 1) * 128],
                                               identf[0:64, 0:64]))
                        blk.tensor(c_pt)
                        sPT = pe.n

                        def c_ptc(v, c=c, st=st, sPT=sPT):
                            v.wait_ge(pe.h, sPT)
                            dv.inc(v.tensor_copy(out=dacum_tm[:, st, :], in_=cpM[:, 64:128]))
                        blk.vector(c_ptc)
                    snapCdat = dv.n

                    def c_act(s, c=c, snapCdat=snapCdat):
                        s.wait_ge(dv.h, snapCdat)
                        ac.inc(s.activation(out=expda_tm.rearrange("p a b -> p (a b)"),
                                            in_=dacum_tm.rearrange("p a b -> p (a b)"), func=AF.Exp))
                    blk.scalar(c_act)
                    snapCexp = ac.n

                    def c_psc(t, c=c, snapCdat=snapCdat):
                        t.wait_ge(dv.h, snapCdat)
                        if c >= 1:
                            t.wait_ge(dw.h, snapCh2last[c - 1])
                        for st in range(2):
                            li = t.matmul(cpsS[:, st, :],
                                          b_fm[:, c * CH + st * 128:c * CH + st * 128 + 128],
                                          c_fm[:, c * CH:(c + 1) * CH], start=True, stop=True)
                        pe.inc(li)
                    blk.tensor(c_psc)
                    snapCsc = pe.n

                    snapCh1 = {}
                    snapCh2 = {}
                    snapCh3 = {}
                    snapCpR = {}
                    snapCpY = {}
                    snapCgph = {}
                    snapCeseg = {}
                    for h in range(HEADS):
                        hp = h % 2

                        def c_pe1(t, c=c, h=h, hp=hp):
                            if h == 0:
                                t.wait_ge(ac.h, snapCexp)
                                t.wait_ge(pe.h, snapCsc)
                                if c >= 1:
                                    t.wait_ge(dv.h, snapCsbf[c - 1])
                            if h >= 2:
                                t.wait_ge(dv.h, snapCh1[h - 2])
                            t.matmul(cpA[:, hp, 0:CH], bcsel[:, h, :], dachi[:], start=True, stop=False)
                            pe.inc(t.matmul(cpA[:, hp, 0:CH], bcsel[:, h, :], daclo[:],
                                            start=False, stop=True))
                        blk.tensor(c_pe1)
                        snapCpR[h] = pe.n

                        def c_dv1(v, c=c, h=h, hp=hp):
                            v.wait_ge(pe.h, snapCpR[h])
                            if h >= 2:
                                v.wait_ge(ac.h, snapCeseg[h - 2])
                                v.wait_ge(pe.h, snapCpY[h - 2])
                            if h == 0:
                                v.drain()
                            for st in range(2):
                                v.scalar_tensor_tensor(out=segT[:, hp, st, :], in0=maskst[:, st, :],
                                                       scalar=dacum_tm[:, st, h:h + 1],
                                                       in1=cpA[:, hp, 0:CH],
                                                       op0=ALU.subtract, op1=ALU.add)
                            for st in range(2):
                                li = v.tensor_scalar(out=xdt[:, hp, st, :],
                                                     in0=xch[:, cp, st, h * HD:(h + 1) * HD],
                                                     scalar1=dt_tm[:, 2 * c + st, h:h + 1],
                                                     scalar2=None, op0=ALU.mult)
                            dv.inc(li)
                        blk.vector(c_dv1)
                        snapCh1[h] = dv.n

                        def c_ac1(s, c=c, h=h, hp=hp):
                            s.wait_ge(dv.h, snapCh1[h])
                            if h >= 2:
                                s.wait_ge(dw.h, snapCh2[h - 2])
                                s.wait_ge(gp.h, snapCgph[h - 2])
                            for st in range(2):
                                s.activation(out=xdd[:, hp, st, :],
                                             in_=xch[:, cp, st, h * HD:(h + 1) * HD],
                                             func=AF.Copy, scale=dbc[:, h:h + 1], bias=0.0)
                            for st in range(2):
                                li = s.activation(out=segT[:, hp, st, :], in_=segT[:, hp, st, :],
                                                  func=AF.Exp)
                            ac.inc(li)
                        blk.scalar(c_ac1)
                        snapCeseg[h] = ac.n

                        def c_dv2(v, c=c, h=h, hp=hp):
                            v.wait_ge(ac.h, snapCeseg[h])
                            if h >= 2:
                                v.wait_ge(pe.h, snapCpY[h - 2])
                            for st in range(2):
                                v.tensor_tensor(out=mts[:, hp, st, :], in0=segT[:, hp, st, :],
                                                in1=cpsS[:, st, :], op=ALU.mult)
                            for st in range(2):
                                li = v.tensor_scalar(out=xdec[:, hp, st, :], in0=xdt[:, hp, st, :],
                                                     scalar1=segT[:, hp, st, CH - 1:CH], scalar2=None,
                                                     op0=ALU.mult)
                            dw.inc(li)
                        blk.vector(c_dv2)
                        snapCh2[h] = dw.n

                        def c_pe2(t, c=c, h=h, hp=hp):
                            t.wait_ge(dw.h, snapCh2[h])
                            if h >= 2:
                                t.wait_ge(dv.h, snapCh3[h - 2])
                            t.matmul(cpY[:, hp, 0:HD], mts[:, hp, 0, 0:128], xdt[:, hp, 0, :],
                                     start=True, stop=True)
                            t.matmul(cpY[:, hp, HD:2 * HD], mts[:, hp, 0, 128:CH], xdt[:, hp, 0, :],
                                     start=True, stop=False)
                            t.matmul(cpY[:, hp, HD:2 * HD], mts[:, hp, 1, 128:CH], xdt[:, hp, 1, :],
                                     start=False, stop=True)
                            t.matmul(cpY[:, hp, 2 * HD:3 * HD], c_fm[:, c * CH:c * CH + 128],
                                     s_bf[:, h * HD:(h + 1) * HD], start=True, stop=True)
                            t.matmul(cpY[:, hp, 3 * HD:4 * HD], c_fm[:, c * CH + 128:(c + 1) * CH],
                                     s_bf[:, h * HD:(h + 1) * HD], start=True, stop=True)
                            t.matmul(cpA[:, hp, CH:CH + HD], b_tm[:, 2 * c, :], xdec[:, hp, 0, :],
                                     start=True, stop=False)
                            pe.inc(t.matmul(cpA[:, hp, CH:CH + HD], b_tm[:, 2 * c + 1, :], xdec[:, hp, 1, :],
                                            start=False, stop=True))
                        blk.tensor(c_pe2)
                        snapCpY[h] = pe.n

                        def c_dv3(v, c=c, h=h, hp=hp):
                            v.wait_ge(pe.h, snapCpY[h])
                            if h >= 2:
                                v.wait_ge(gp.h, snapCgph[h - 2])
                            for st in range(2):
                                v.tensor_copy(out=ydc[:, hp, st, :], in_=cpY[:, hp, st * HD:(st + 1) * HD])
                            for st in range(2):
                                v.scalar_tensor_tensor(out=y1c[:, hp, st, :], in0=cpY[:, hp, (2 + st) * HD:(3 + st) * HD],
                                                       scalar=expda_tm[:, st, h:h + 1],
                                                       in1=ydc[:, hp, st, :],
                                                       op0=ALU.mult, op1=ALU.add)
                            v.scalar_tensor_tensor(out=s_all[:, h * HD:(h + 1) * HD],
                                                   in0=s_all[:, h * HD:(h + 1) * HD],
                                                   scalar=ecol[:, h:h + 1], in1=cpA[:, hp, CH:CH + HD],
                                                   op0=ALU.mult, op1=ALU.add)
                            dv.inc(v.tensor_copy(out=s_bf[:, h * HD:(h + 1) * HD],
                                                 in_=s_all[:, h * HD:(h + 1) * HD]))
                        blk.vector(c_dv3)
                        snapCh3[h] = dv.n

                        def c_gp(g, c=c, h=h, hp=hp):
                            g.wait_ge(dv.h, snapCh3[h])
                            g.wait_ge(ac.h, snapCeseg[h])
                            if h == 0 and c >= 1:
                                g.wait_ge(dv.h, snapChgmul[c - 1])
                            for st in range(2):
                                li = g.tensor_tensor(out=ych[:, st, h * HD:(h + 1) * HD],
                                                     in0=xdd[:, hp, st, :],
                                                     in1=y1c[:, hp, st, :], op=ALU.add)
                            gp.inc(li)
                        blk.gpsimd(c_gp)
                        snapCgph[h] = gp.n
                    snapCh2last[c] = snapCh2[HEADS - 1]
                    snapCgpend[c] = gp.n
                    snapCsbf[c] = snapCh3[HEADS - 1]
                    snapCecl[c] = snapCecl_pre
                    snapCcprT[c] = snapCpR[HEADS - 1]

                    def c_hg(v, c=c, cp=cp):
                        v.wait_ge(gp.h, snapCgpend[c])
                        for st in range(2):
                            li = v.tensor_tensor(out=ych[:, st, :], in0=ych[:, st, :],
                                                 in1=sgch[:, st, :], op=ALU.mult)
                        dv.inc(li)
                    blk.vector(c_hg)
                    snapChgmul[c] = dv.n

                    def c_hga(s, c=c):
                        s.wait_ge(dv.h, snapChgmul[c])
                        for st in range(2):
                            s.activation(out=sqs[:], in_=ych[:, st, :], func=AF.Square,
                                         accum_out=rcol[:, st:st + 1])
                        for st in range(2):
                            li = s.activation(out=rcol[:, 2 + st:3 + st], in_=rcol[:, st:st + 1],
                                              func=AF.Sqrt, scale=miscc[:, 3:4], bias=miscc[:, 1:2])
                        ac.inc(li)
                    blk.scalar(c_hga)
                    snapChga = ac.n

                    def c_hg2(v, c=c, snapChga=snapChga):
                        v.wait_ge(ac.h, snapChga)
                        for st in range(2):
                            v.reciprocal(out=rcol[:, 4 + st:5 + st], in_=rcol[:, 2 + st:3 + st])
                        v.drain()
                        for st in range(2):
                            li = v.tensor_scalar(out=sgch[:, st, :], in0=ych[:, st, :],
                                                 scalar1=rcol[:, 4 + st:5 + st], scalar2=None,
                                                 op0=ALU.mult)
                        dv.inc(li)
                    blk.vector(c_hg2)
                    snapChgn = dv.n

                    snapCTg = {}
                    for st in range(2):
                        for grp in range(4):
                            G = cgrp_ctr[0]
                            cgrp_ctr[0] += 1
                            bk = G % 2
                            dw_g2 = snapCgc.get(G - 2)

                            def c_pet2(t, c=c, st=st, grp=grp, bk=bk, dw_g2=dw_g2, snapChgn=snapChgn):
                                if grp == 0:
                                    if st == 0:
                                        t.wait_ge(dv.h, snapChgn)
                                    if last_hstage_dma[0] is not None:
                                        t.wait_ge(qs.h, last_hstage_dma[0])
                                if dw_g2 is not None:
                                    t.wait_ge(dw.h, dw_g2)
                                for kk in range(8):
                                    ft = grp * 8 + kk
                                    sl = cptb[:, bk, kk * 128:kk * 128 + 128]
                                    pe.inc(t.transpose(sl, sgch[:, st, ft * 128:(ft + 1) * 128],
                                                       identb[:]))
                            blk.tensor(c_pet2)
                            snapCgt[G] = pe.n

                            def c_dvc2(v, c=c, st=st, grp=grp, bk=bk, G=G):
                                v.wait_ge(pe.h, snapCgt[G])
                                for kk in range(8):
                                    ft = grp * 8 + kk
                                    dw.inc(v.tensor_scalar(
                                        out=hstage[:, ft, :],
                                        in0=cptb[:, bk, kk * 128:kk * 128 + 128],
                                        scalar1=normw[:, ft:ft + 1], scalar2=None, op0=ALU.mult))
                            blk.vector(c_dvc2)
                            snapCgc[G] = dw.n
                        if st == 1:
                            snapChgT[c] = pe.n
                        snapChst = dw.n

                        def c_sy2(s, c=c, st=st, snapChst=snapChst):
                            s.wait_ge(dw.h, snapChst)
                            t0 = c * CH + st * 128
                            qs.inc(s.dma_start(out=hgn_d[:, :, t0:t0 + 128].rearrange("ft p t -> p ft t"),
                                               in_=hstage[:]), 16)
                        blk.sync(c_sy2)
                        last_hstage_dma[0] = qs.n
                qs_c_end = qs.n
            for cm in reversed(mres_cm):
                cm.__exit__(None, None, None)

            phase_barrier()
        if upto >= 4:
            hv = hgn_d.rearrange("ft p t -> p ft t")
            with _multi(
                nc.sbuf_tensor("wo1", [128, 32, 1024], BF16),
                nc.sbuf_tensor("hgt1", [128, 2, 32, 128], BF16),
                nc.sbuf_tensor("xr1", [128, 2, 1024], F32),
                nc.sbuf_tensor("o1t", [128, 2, 1024], F32),
                nc.psum_tensor("p1ps", [128, 2, 2, 512], F32),
            ) as (wo1, hgt1, xr1, o1t, p1ps):
                def p1_w(g):
                    g.wait_ge(qs.h, qs_c_end)
                    for gq in range(4):
                        qw.inc(g.dma_start(
                            out=wo1[:, gq * 8:(gq + 1) * 8, :],
                            in_=w_out_t[gq * 8:(gq + 1) * 8, :, 0:1024].rearrange("ft p d -> p ft d")), 16)
                blk.gpsimd(p1_w)
                qw_p1 = qw.n

                qa_p1 = qa.n
                snapP1mm = [None] * NTT
                snapP1ev = [None] * NTT
                snapP1st = [None] * NTT
                for i in range(NTT):
                    p = i % 2

                    def p1_sy(s, i=i, p=p):
                        if i == 0:
                            s.wait_ge(qs.h, qs_c_end)
                        if i >= 2:
                            s.wait_ge(pe.h, snapP1mm[i - 2])
                            s.wait_ge(dv.h, snapP1ev[i - 2])
                            s.wait_ge(qs.h, snapP1st[i - 2])
                        qa.inc(s.dma_start(out=hgt1[:, p, :, :], in_=hv[:, :, i * 128:(i + 1) * 128]), 16)
                        qa.inc(s.dma_start(out=xr1[:, p, :],
                                           in_=x_in[i * 128:(i + 1) * 128, 0:1024]), 16)
                    blk.sync(p1_sy)

                    def p1_pe(t, i=i, p=p):
                        t.wait_ge(qa.h, qa_p1 + 32 * i + 16)
                        if i == 0:
                            t.wait_ge(qw.h, qw_p1)
                        if i >= 2:
                            t.wait_ge(dv.h, snapP1ev[i - 2])
                        for nd in range(2):
                            for ft in range(32):
                                li = t.matmul(p1ps[:, p, nd, :], hgt1[:, p, ft, :],
                                              wo1[:, ft, nd * 512:(nd + 1) * 512],
                                              start=(ft == 0), stop=(ft == 31))
                        pe.inc(li)
                    blk.tensor(p1_pe)
                    snapP1mm[i] = pe.n

                    def p1_dv(v, i=i, p=p):
                        v.wait_ge(pe.h, snapP1mm[i])
                        v.wait_ge(qa.h, qa_p1 + 32 * (i + 1))
                        for nd in range(2):
                            li = v.tensor_tensor(out=o1t[:, p, nd * 512:(nd + 1) * 512],
                                                 in0=p1ps[:, p, nd, :],
                                                 in1=xr1[:, p, nd * 512:(nd + 1) * 512], op=ALU.add)
                        dv.inc(li)
                    blk.vector(p1_dv)
                    snapP1ev[i] = dv.n

                    def p1_st(s, i=i, p=p):
                        s.wait_ge(dv.h, snapP1ev[i])
                        qs.inc(s.dma_start(out=x2_d[i * 128:(i + 1) * 128, 0:1024], in_=o1t[:, p, :]), 16)
                    blk.sync(p1_st)
                    snapP1st[i] = qs.n
                qs_p1_end = qs.n

            phase_barrier()
        if upto >= 5:
            with _multi(
                nc.sbuf_tensor("wo2", [128, 32, 1024], BF16),
                nc.sbuf_tensor("hgt2", [128, 2, 32, 128], BF16),
                nc.sbuf_tensor("xr2", [128, 2, 1024], F32),
                nc.sbuf_tensor("x2t", [128, 2, DM], F32),
                nc.sbuf_tensor("ln2f", [128, DM], F32),
                nc.sbuf_tensor("ln2h", [128, 2, DM], BF16),
                nc.sbuf_tensor("ln2lo", [128, 2, DM], BF16),
                nc.sbuf_tensor("lnT", [128, 2, NKT, 128], BF16),
                nc.sbuf_tensor("dcl", [128, 8], F32),
                nc.sbuf_tensor("lgT", [32, 128], F32),
                nc.psum_tensor("p2ps", [128, 2, 2, 512], F32),
                nc.psum_tensor("dptb", [128, 2, 1024], BF16),
                nc.psum_tensor("dplg", [32, 128], F32),
                nc.psum_tensor("dplt", [128, 32], F32),
            ) as (wo2, hgt2, xr2, x2t, ln2f, ln2h, ln2lo, lnT, dcl, lgT, p2ps, dptb, dplg, dplt):
                def p2_w(g):
                    g.wait_ge(qs.h, qs_p1_end)
                    for gq in range(4):
                        qw.inc(g.dma_start(
                            out=wo2[:, gq * 8:(gq + 1) * 8, :],
                            in_=w_out_t[gq * 8:(gq + 1) * 8, :, 1024:2048].rearrange(
                                "ft p d -> p ft d")), 16)
                blk.gpsimd(p2_w)
                qw_p2 = qw.n

                qa_p2 = qa.n
                snapP2mm = [None] * NTT
                snapP2x2 = [None] * NTT
                snapP2b = [None] * NTT
                snapP2st = [None] * NTT
                snapP2T = [None] * NTT
                snapP2r = [None] * NTT
                snapP2lg = [None] * NTT
                snapP2t2 = [None] * NTT
                snapP2l5 = [None] * NTT
                snapP2gt = {}
                snapP2gc = {}
                for i in range(NTT):
                    p = i % 2

                    def p2_sy(s, i=i, p=p):
                        if i == 0:
                            s.wait_ge(qs.h, qs_p1_end)
                        if i >= 2:
                            s.wait_ge(pe.h, snapP2mm[i - 2])
                            s.wait_ge(dv.h, snapP2x2[i - 2])
                            s.wait_ge(qs.h, snapP2st[i - 2])
                        qa.inc(s.dma_start(out=hgt2[:, p, :, :], in_=hv[:, :, i * 128:(i + 1) * 128]), 16)
                        qa.inc(s.dma_start(out=xr2[:, p, :],
                                           in_=x_in[i * 128:(i + 1) * 128, 1024:2048]), 16)
                        qa.inc(s.dma_start(out=x2t[:, p, 0:1024],
                                           in_=x2_d[i * 128:(i + 1) * 128, 0:1024]), 16)
                    blk.sync(p2_sy)

                    def p2_pe(t, i=i, p=p):
                        t.wait_ge(qa.h, qa_p2 + 48 * i + 16)
                        if i == 0:
                            t.wait_ge(qw.h, qw_p2)
                        if i >= 2:
                            t.wait_ge(dv.h, snapP2x2[i - 2])
                        for nd in range(2):
                            for ft in range(32):
                                li = t.matmul(p2ps[:, p, nd, :], hgt2[:, p, ft, :],
                                              wo2[:, ft, nd * 512:(nd + 1) * 512],
                                              start=(ft == 0), stop=(ft == 31))
                        pe.inc(li)
                    blk.tensor(p2_pe)
                    snapP2mm[i] = pe.n

                    def p2_dv1(v, i=i, p=p):
                        v.wait_ge(pe.h, snapP2mm[i])
                        v.wait_ge(qa.h, qa_p2 + 48 * (i + 1))
                        for nd in range(2):
                            v.tensor_tensor(out=x2t[:, p, 1024 + nd * 512:1024 + (nd + 1) * 512],
                                            in0=p2ps[:, p, nd, :],
                                            in1=xr2[:, p, nd * 512:(nd + 1) * 512], op=ALU.add)
                        v.tensor_reduce(out=dcl[:, 0:1], in_=x2t[:, p, :], axis=AX.X, op=ALU.add)
                        v.drain()
                        v.tensor_scalar(out=dcl[:, 1:2], in0=dcl[:, 0:1], scalar1=1.0 / DM,
                                        scalar2=None, op0=ALU.mult)
                        v.drain()
                        dv.inc(v.tensor_scalar(out=ln2f[:], in0=x2t[:, p, :], scalar1=dcl[:, 1:2],
                                               scalar2=None, op0=ALU.subtract))
                    blk.vector(p2_dv1)
                    snapP2x2[i] = dv.n

                    def p2_ac(s, i=i, p=p):
                        s.wait_ge(dv.h, snapP2x2[i])
                        if i >= 2:
                            s.wait_ge(pe.h, snapP2T[i - 2])
                        s.activation(out=ln2lo[:, p, :], in_=ln2f[:], func=AF.Square,
                                     accum_out=dcl[:, 2:3])
                        ac.inc(s.activation(out=dcl[:, 3:4], in_=dcl[:, 2:3], func=AF.Sqrt,
                                            scale=miscc[:, 2:3], bias=miscc[:, 1:2]))
                    blk.scalar(p2_ac)
                    snapP2ac = ac.n

                    def p2_dv2(v, i=i, p=p, snapP2ac=snapP2ac):
                        v.wait_ge(ac.h, snapP2ac)
                        if i >= 2:
                            v.wait_ge(pe.h, snapP2T[i - 2])
                        v.reciprocal(out=dcl[:, 4:5], in_=dcl[:, 3:4])
                        v.drain()
                        v.tensor_scalar(out=ln2f[:], in0=ln2f[:], scalar1=dcl[:, 4:5], scalar2=None,
                                        op0=ALU.mult)
                        v.tensor_copy(out=ln2h[:, p, :], in_=ln2f[:])
                        dv.inc(v.tensor_tensor(out=ln2lo[:, p, :], in0=ln2f[:], in1=ln2h[:, p, :],
                                               op=ALU.subtract))
                    blk.vector(p2_dv2)
                    snapP2b[i] = dv.n

                    def p2_st(s, i=i, p=p):
                        s.wait_ge(dv.h, snapP2b[i])
                        qs.inc(s.dma_start(out=x2_d[i * 128:(i + 1) * 128, 1024:2048],
                                           in_=x2t[:, p, 1024:2048]), 16)
                        qs.inc(s.dma_start(out=ln2_d[i * 128:(i + 1) * 128, :], in_=ln2h[:, p, :]), 16)
                    blk.sync(p2_st)
                    snapP2st[i] = qs.n

                    for grp in range(4):
                        G = i * 4 + grp
                        bk = G % 2
                        dw_g2 = snapP2gc.get(G - 2)

                        def p2_pet(t, i=i, p=p, grp=grp, bk=bk, dw_g2=dw_g2):
                            if grp == 0:
                                t.wait_ge(dv.h, snapP2b[i])
                            if dw_g2 is not None:
                                t.wait_ge(dw.h, dw_g2)
                            for kk in range(8):
                                gg = grp * 8 + kk
                                kt, half = gg // 2, gg % 2
                                src = ln2h if half == 0 else ln2lo
                                sl = dptb[:, bk, kk * 128:kk * 128 + 128]
                                pe.inc(t.transpose(sl, src[:, p, kt * 128:(kt + 1) * 128], identb[:]))
                        blk.tensor(p2_pet)
                        snapP2gt[G] = pe.n

                        def p2_dv3(v, i=i, p=p, grp=grp, bk=bk, G=G):
                            if grp == 0 and i >= 1:
                                v.wait_ge(pe.h, snapP2r[i - 1])
                            v.wait_ge(pe.h, snapP2gt[G])
                            for kk in range(8):
                                gg = grp * 8 + kk
                                kt, half = gg // 2, gg % 2
                                dw.inc(v.tensor_copy(out=lnT[:, half, kt, :],
                                                     in_=dptb[:, bk, kk * 128:kk * 128 + 128]))
                        blk.vector(p2_dv3)
                        snapP2gc[G] = dw.n
                    snapP2T[i] = pe.n
                    snapP2c = dw.n

                    def p2_per(t, i=i, snapP2c=snapP2c):
                        t.wait_ge(dw.h, snapP2c)
                        if i >= 1:
                            t.wait_ge(dv.h, snapP2lg[i - 1])
                        first = True
                        for kt in range(NKT):
                            t.matmul(dplg[:], wg_hi[:, kt, :], lnT[:, 0, kt, :], start=first, stop=False)
                            first = False
                            t.matmul(dplg[:], wg_lo[:, kt, :], lnT[:, 0, kt, :], start=False, stop=False)
                            li = t.matmul(dplg[:], wg_hi[:, kt, :], lnT[:, 1, kt, :], start=False,
                                          stop=(kt == NKT - 1))
                        pe.inc(li)
                    blk.tensor(p2_per)
                    snapP2r[i] = pe.n

                    def p2_dv4(v, i=i):
                        v.wait_ge(pe.h, snapP2r[i])
                        if i >= 1:
                            v.wait_ge(pe.h, snapP2t2[i - 1])
                        dv.inc(v.tensor_copy(out=lgT[:], in_=dplg[:]))
                    blk.vector(p2_dv4)
                    snapP2lg[i] = dv.n

                    def p2_pe2(t, i=i):
                        t.wait_ge(dv.h, snapP2lg[i])
                        if i >= 1:
                            t.wait_ge(dv.h, snapP2l5[i - 1])
                        pe.inc(t.transpose(dplt[:], lgT[:], identf[0:32, 0:32]))
                    blk.tensor(p2_pe2)
                    snapP2t2[i] = pe.n

                    def p2_dv5(v, i=i):
                        v.wait_ge(pe.h, snapP2t2[i])
                        dv.inc(v.tensor_copy(out=logits_tm[:, i, :], in_=dplt[:]))
                    blk.vector(p2_dv5)
                    snapP2l5[i] = dv.n
                qs_d1_end = qs.n
                snapD1_dv = dv.n

            phase_barrier()
        if upto >= 6:
            with _multi(
                nc.sbuf_tensor("probs", [128, NE], F32),
                nc.sbuf_tensor("vals8", [128, TOPK], F32),
                nc.sbuf_tensor("idxu", [128, TOPK], U32),
                nc.sbuf_tensor("r2c", [128, 16], F32),
                nc.sbuf_tensor("ohp", [128, 4, NE], F32),
                nc.sbuf_tensor("ohj", [128, NE], F32),
                nc.sbuf_tensor("ohjT", [32, 128], F32),
                nc.sbuf_tensor("scj", [128, 4], F32),
                nc.sbuf_tensor("scr32", [128, NE], F32),
                nc.sbuf_tensor("bstc", [128, 8], F32),
                nc.sbuf_tensor("carry", [1, NE], F32),
                nc.sbuf_tensor("ohsi", [128, NE], F32),
                nc.sbuf_tensor("poscol", [128, 8], F32),
                nc.psum_tensor("rpt", [32, 128], F32),
                nc.psum_tensor("rps1", [128, NE], F32),
                nc.psum_tensor("rppre", [128, NE], F32),
                nc.psum_tensor("rpcar", [1, NE], F32),
            ) as (probs, vals8, idxu, r2c, ohp, ohj, ohjT, scj, scr32, bstc, carry, ohsi, poscol, rpt, rps1, rppre, rpcar):
                snap_j_pe = [None]
                snap_j_dv = [None]
                for i in range(NTT):
                    def d2_dv1(v, i=i):
                        if i == 0:
                            v.wait_ge(dv.h, snapD1_dv)
                        v.tensor_reduce(out=r2c[:, 0:1], in_=logits_tm[:, i, :], axis=AX.X, op=ALU.max)
                        v.drain()
                        dv.inc(v.tensor_scalar(out=probs[:], in0=logits_tm[:, i, :],
                                               scalar1=r2c[:, 0:1], scalar2=None, op0=ALU.subtract))
                    blk.vector(d2_dv1)
                    s1 = dv.n

                    def d2_ac(s, i=i, s1=s1):
                        s.wait_ge(dv.h, s1)
                        ac.inc(s.activation(out=probs[:], in_=probs[:], func=AF.Exp,
                                            accum_out=r2c[:, 1:2]))
                    blk.scalar(d2_ac)
                    s2 = ac.n

                    def d2_dv2(v, i=i, s2=s2):
                        v.wait_ge(ac.h, s2)
                        v.reciprocal(out=r2c[:, 2:3], in_=r2c[:, 1:2])
                        v.drain()
                        v.tensor_scalar(out=probs[:], in0=probs[:], scalar1=r2c[:, 2:3], scalar2=None,
                                        op0=ALU.mult)
                        v.drain()
                        v.max(out=vals8[:], in_=probs[:])
                        v.drain()
                        v.max_index(out=idxu[:], in_max=vals8[:], in_values=probs[:])
                        v.drain()
                        v.tensor_copy(out=idx8_all[:, i, :], in_=idxu[:])
                        v.tensor_reduce(out=r2c[:, 3:4], in_=vals8[:], axis=AX.X, op=ALU.add)
                        v.drain()
                        v.tensor_scalar(out=r2c[:, 4:5], in0=r2c[:, 3:4], scalar1=1e-12, scalar2=None,
                                        op0=ALU.max)
                        v.drain()
                        v.reciprocal(out=r2c[:, 5:6], in_=r2c[:, 4:5])
                        v.drain()
                        v.tensor_scalar(out=w8_all[:, i, :], in0=vals8[:], scalar1=r2c[:, 5:6],
                                        scalar2=None, op0=ALU.mult)
                        for k in range(4):
                            li = v.tensor_scalar(out=ohp[:, k, :], in0=iota32[:],
                                                 scalar1=idx8_all[:, i, k:k + 1], scalar2=None,
                                                 op0=ALU.is_equal)
                        dv.inc(li)
                    blk.vector(d2_dv2)
                    s3 = dv.n

                    for j in range(4):
                        def d2_dvj(v, i=i, j=j, s3=s3):
                            if j == 0:
                                v.wait_ge(dv.h, s3)
                            if snap_j_pe[0] is not None:
                                v.wait_ge(pe.h, snap_j_pe[0])
                            dw.inc(v.tensor_scalar(out=ohj[:], in0=iota32[:],
                                                   scalar1=idx8_all[:, i, SSEL + j:SSEL + j + 1],
                                                   scalar2=None, op0=ALU.is_equal))
                        blk.vector(d2_dvj)
                        s4 = dw.n

                        def d2_pej(t, i=i, j=j, s4=s4):
                            t.wait_ge(dw.h, s4)
                            if snap_j_dv[0] is not None:
                                t.wait_ge(dv.h, snap_j_dv[0])
                            pe.inc(t.transpose(rpt[:], ohj[:], identf[:]))
                        blk.tensor(d2_pej)
                        s5 = pe.n

                        def d2_dvj3(v, i=i, j=j, s5=s5):
                            v.wait_ge(pe.h, s5)
                            dw.inc(v.tensor_copy(out=ohjT[:], in_=rpt[:]))
                        blk.vector(d2_dvj3)
                        s6 = dw.n

                        def d2_pej3(t, i=i, j=j, s6=s6):
                            t.wait_ge(dw.h, s6)
                            pe.inc(t.matmul(rps1[:], ohjT[:], simd[:], start=True, stop=True))
                        blk.tensor(d2_pej3)
                        snap_j_pe[0] = pe.n
                        s7 = pe.n

                        def d2_dvj4(v, i=i, j=j, s7=s7):
                            v.wait_ge(pe.h, s7)
                            for k in range(4):
                                v.tensor_tensor(out=scr32[:], in0=rps1[:], in1=ohp[:, k, :],
                                                op=ALU.mult)
                                v.drain()
                                v.tensor_reduce(out=scj[:, k:k + 1], in_=scr32[:], axis=AX.X,
                                                op=ALU.add)
                            v.drain()
                            v.tensor_tensor(out=bstc[:, 0:1], in0=scj[:, 0:1], in1=scj[:, 1:2],
                                            op=ALU.max)
                            v.tensor_tensor(out=bstc[:, 1:2], in0=scj[:, 2:3], in1=scj[:, 3:4],
                                            op=ALU.max)
                            v.drain()
                            v.tensor_tensor(out=bstc[:, 0:1], in0=bstc[:, 0:1], in1=bstc[:, 1:2],
                                            op=ALU.max)
                            v.drain()
                            v.tensor_scalar(out=bstc[:, 2:3], in0=scj[:, 0:1], scalar1=bstc[:, 0:1],
                                            scalar2=None, op0=ALU.is_ge)
                            v.drain()
                            v.tensor_scalar(out=bstc[:, 4:5], in0=bstc[:, 2:3], scalar1=-1.0,
                                            scalar2=1.0, op0=ALU.mult, op1=ALU.add)
                            v.tensor_tensor(out=bstc[:, 6:7], in0=idx8_all[:, i, 0:1], in1=bstc[:, 2:3],
                                            op=ALU.mult)
                            for k in range(1, 4):
                                v.drain()
                                v.tensor_scalar(out=bstc[:, 2:3], in0=scj[:, k:k + 1],
                                                scalar1=bstc[:, 0:1], scalar2=None, op0=ALU.is_ge)
                                v.drain()
                                v.tensor_tensor(out=bstc[:, 3:4], in0=bstc[:, 2:3], in1=bstc[:, 4:5],
                                                op=ALU.mult)
                                v.drain()
                                v.scalar_tensor_tensor(out=bstc[:, 6:7], in0=idx8_all[:, i, k:k + 1],
                                                       scalar=bstc[:, 3:4], in1=bstc[:, 6:7],
                                                       op0=ALU.mult, op1=ALU.add)
                                v.tensor_scalar(out=bstc[:, 5:6], in0=bstc[:, 3:4], scalar1=-1.0,
                                                scalar2=1.0, op0=ALU.mult, op1=ALU.add)
                                v.drain()
                                v.tensor_tensor(out=bstc[:, 4:5], in0=bstc[:, 4:5], in1=bstc[:, 5:6],
                                                op=ALU.mult)
                            v.drain()
                            v.tensor_scalar(out=bstc[:, 7:8], in0=bstc[:, 0:1], scalar1=THR,
                                            scalar2=None, op0=ALU.is_lt)
                            v.drain()
                            v.tensor_tensor(out=scr32[:, 0:1], in0=idx8_all[:, i, SSEL + j:SSEL + j + 1],
                                            in1=bstc[:, 7:8], op=ALU.mult)
                            v.tensor_scalar(out=bstc[:, 7:8], in0=bstc[:, 7:8], scalar1=-1.0,
                                            scalar2=1.0, op0=ALU.mult, op1=ALU.add)
                            v.drain()
                            v.tensor_tensor(out=scr32[:, 1:2], in0=bstc[:, 6:7], in1=bstc[:, 7:8],
                                            op=ALU.mult)
                            v.drain()
                            dv.inc(v.tensor_tensor(out=idx8_all[:, i, SSEL + j:SSEL + j + 1],
                                                   in0=scr32[:, 0:1], in1=scr32[:, 1:2], op=ALU.add))
                        blk.vector(d2_dvj4)
                        snap_j_dv[0] = dv.n
                snapD2route = dv.n

                def d2_ci(v):
                    v.wait_ge(dv.h, snapD2route)
                    dv.inc(v.memset(carry[:], 0.0))
                blk.vector(d2_ci)
                snapPcar = dv.n

                first_carry = True
                snapPpe = None
                for s_ in range(TOPK):
                    for i in range(NTT):
                        def p_dv1(v, s_=s_, i=i, snapPpe=snapPpe):
                            if snapPpe is not None:
                                v.wait_ge(pe.h, snapPpe)
                            dw.inc(v.tensor_scalar(out=ohsi[:], in0=iota32[:],
                                                   scalar1=idx8_all[:, i, s_:s_ + 1], scalar2=None,
                                                   op0=ALU.is_equal))
                        blk.vector(p_dv1)
                        sP1 = dw.n

                        def p_pe(t, s_=s_, i=i, sP1=sP1, fc=first_carry, snapPcar=snapPcar):
                            t.wait_ge(dw.h, sP1)
                            t.wait_ge(dv.h, snapPcar)
                            t.matmul(rppre[:], ustrict[:], ohsi[:], start=True, stop=False,
                                     skip_group_check=True)
                            t.matmul(rppre[:], ones_r1[:], carry[:], start=False, stop=True,
                                     skip_group_check=True)
                            pe.inc(t.matmul(rpcar[:], ones_c[:], ohsi[:], start=fc, stop=True,
                                            skip_group_check=True))
                        blk.tensor(p_pe)
                        first_carry = False
                        snapPpe = pe.n
                        sP2 = pe.n

                        def p_dv2(v, s_=s_, i=i, sP2=sP2):
                            v.wait_ge(pe.h, sP2)
                            v.tensor_tensor(out=scr32[:], in0=rppre[:], in1=ohsi[:], op=ALU.mult)
                            v.tensor_copy(out=carry[:], in_=rpcar[:])
                            v.tensor_reduce(out=poscol[:, 0:1], in_=scr32[:], axis=AX.X, op=ALU.add)
                            v.drain()
                            v.tensor_scalar(out=poscol[:, 1:2], in0=idx8_all[:, i, s_:s_ + 1],
                                            scalar1=float(CAP), scalar2=poscol[:, 0:1],
                                            op0=ALU.mult, op1=ALU.add)
                            v.tensor_scalar(out=poscol[:, 2:3], in0=poscol[:, 0:1], scalar1=float(CAP),
                                            scalar2=None, op0=ALU.is_ge)
                            v.drain()
                            v.tensor_scalar(out=poscol[:, 3:4], in0=poscol[:, 2:3], scalar1=-1.0,
                                            scalar2=1.0, op0=ALU.mult, op1=ALU.add)
                            v.tensor_tensor(out=poscol[:, 5:6], in0=trash[:], in1=poscol[:, 2:3],
                                            op=ALU.mult)
                            v.drain()
                            v.tensor_tensor(out=poscol[:, 4:5], in0=poscol[:, 1:2], in1=poscol[:, 3:4],
                                            op=ALU.mult)
                            v.drain()
                            v.tensor_tensor(out=poscol[:, 6:7], in0=poscol[:, 4:5], in1=poscol[:, 5:6],
                                            op=ALU.add)
                            v.drain()
                            dv.inc(v.tensor_copy(out=posu_all[:, s_ * NTT + i:s_ * NTT + i + 1],
                                                 in_=poscol[:, 6:7]))
                        blk.vector(p_dv2)
                        snapPcar = dv.n
                snapD2pos = dv.n

            phase_barrier()
        if upto >= 7:
            with nc.sbuf_tensor("lnrow", [128, 2, DM], BF16) as lnrow:
                qa_d3 = qa.n
                snapD3sc = [None] * NTT
                for i in range(NTT):
                    def d3_sy(s, i=i):
                        if i == 0:
                            s.wait_ge(dv.h, snapD2pos)
                            s.wait_ge(qs.h, qs_d1_end)
                        if i >= 2:
                            s.wait_ge(qi.h, snapD3sc[i - 2])
                        qa.inc(s.dma_start(out=lnrow[:, i % 2, :],
                                           in_=ln2_d[i * 128:(i + 1) * 128, :]), 16)
                    blk.sync(d3_sy)

                    def d3_gp(g, i=i):
                        g.wait_ge(qa.h, qa_d3 + 16 * (i + 1))
                        if i == 0:
                            g.wait_ge(dv.h, snapD2pos)
                        for s_ in range(TOPK):
                            qi.inc(g.indirect_dma_start(
                                out=xdisp_d[:],
                                out_offset=IOA(ap=posu_all[:, s_ * NTT + i:s_ * NTT + i + 1], axis=0),
                                in_=lnrow[:, i % 2, :], in_offset=None), 16)
                    blk.gpsimd(d3_gp)
                    snapD3sc[i] = qi.n
                qi_d3_end = qi.n

            phase_barrier()
        if upto >= 8:
            with _multi(
                nc.sbuf_tensor("xg", [128, 8, DM], BF16),
                nc.sbuf_tensor("xgT", [128, NKT, CAP], BF16),
                nc.sbuf_tensor("wgu", [128, 2, NKT, 128], BF16),
                nc.sbuf_tensor("wdt", [128, 6, DM], BF16),
                nc.sbuf_tensor("sg6", [128, 6, CAP], F32),
                nc.sbuf_tensor("htil", [128, 6, CAP], BF16),
                nc.sbuf_tensor("ytile", [128, 2, DM], BF16),
                nc.psum_tensor("gps", [128, 2, 512], F32),
                nc.psum_tensor("dnps", [128, 4, 512], F32),
                nc.psum_tensor("gptb", [128, 2, 1024], BF16),
            ) as (xg, xgT, wgu, wdt, sg6, htil, ytile, gps, dnps, gptb):
                def d4_z(v):
                    dv.inc(v.memset(ytile[:, 0, :], 0.0))
                blk.vector(d4_z)
                sZ = dv.n

                def d4_zs(s, sZ=sZ):
                    s.wait_ge(dv.h, sZ)
                    qs.inc(s.dma_start(out=ydisp_d[NDISP:NDISP + 128, :], in_=ytile[:, 0, :]), 16)
                blk.sync(d4_zs)
                snapD4zs = qs.n

                qa_d4 = qa.n
                qwsnapG = {}
                qwsnapD = {}
                snapD4xT = [None] * NE
                snapD4gu = {}
                snapD4guE = [None] * NE
                snapD4dnE = [None] * NE
                snapD4uend = [None] * NE
                snapD4sg = {}
                snapD4xc = [None] * NE
                snapD4y = {}
                snapD4st = {}
                snapD4gt = {}
                snapD4gc = {}
                evD4_before = {}

                def d4_gp0(g):
                    g.wait_ge(qi.h, qi_d3_end)
                    for mt in range(2):
                        qw.inc(g.dma_start(out=wgu[:, mt % 2, :, :],
                                           in_=w_gu_t[0, mt].rearrange("(kt kp) m -> kp kt m",
                                                                       kp=128)), 16)
                        qwsnapG[(0, mt)] = qw.n
                    qw.inc(g.dma_start(out=wdt[:],
                                       in_=w_d_t[0].rearrange("(kt kp) d -> kp kt d", kp=128)), 16)
                    qwsnapD[0] = qw.n
                blk.gpsimd(d4_gp0)

                for e in range(NE):
                    ep = e % 2

                    def d4_syl(s, e=e):
                        if e == 0:
                            s.wait_ge(qi.h, qi_d3_end)
                        else:
                            s.wait_ge(pe.h, snapD4xT[e - 1])
                        iv = xdisp_d[e * CAP:(e + 1) * CAP, :].rearrange(
                            "(rt p) d -> p rt d", p=128)
                        qa.inc(s.dma_start(out=xg[:], in_=iv), 16)
                    blk.sync(d4_syl)

                    for rt in range(8):
                        for grp in range(2):
                            G = e * 16 + rt * 2 + grp
                            bk = G % 2
                            dw_g2 = snapD4gc.get(G - 2)

                            def d4_pex(t, e=e, ep=ep, rt=rt, grp=grp, bk=bk, dw_g2=dw_g2):
                                if rt == 0 and grp == 0:
                                    t.wait_ge(qa.h, qa_d4 + 16 * (e + 1))
                                if dw_g2 is not None:
                                    t.wait_ge(dw.h, dw_g2)
                                for kk in range(8):
                                    kt = grp * 8 + kk
                                    sl = gptb[:, bk, kk * 128:kk * 128 + 128]
                                    pe.inc(t.transpose(sl, xg[:, rt, kt * 128:(kt + 1) * 128],
                                                       identb[:]))
                            blk.tensor(d4_pex)
                            snapD4gt[G] = pe.n

                            def d4_dvx(v, e=e, ep=ep, rt=rt, grp=grp, bk=bk, G=G):
                                if rt == 0 and grp == 0 and e >= 1:
                                    v.wait_ge(pe.h, snapD4guE[e - 1])
                                v.wait_ge(pe.h, snapD4gt[G])
                                for kk in range(8):
                                    kt = grp * 8 + kk
                                    dw.inc(v.tensor_copy(
                                        out=xgT[:, kt, rt * 128:(rt + 1) * 128],
                                        in_=gptb[:, bk, kk * 128:kk * 128 + 128]))
                            blk.vector(d4_dvx)
                            snapD4gc[G] = dw.n
                    snapD4xT[e] = pe.n
                    snapD4xc[e] = dw.n

                    evD4_before[e] = ev.n
                    for mt in range(12):
                        def d4_pg(t, e=e, ep=ep, mt=mt):
                            t.wait_ge(qw.h, qwsnapG[(e, mt)])
                            if mt == 0:
                                t.wait_ge(dw.h, snapD4xc[e])
                            if mt >= 1:
                                t.wait_ge(ev.h, evD4_before[e] + 2 * mt)
                            for nr in range(2):
                                for kt in range(NKT):
                                    li = t.matmul(gps[:, nr, :], wgu[:, (e * 12 + mt) % 2, kt, :],
                                                  xgT[:, kt, nr * 512:(nr + 1) * 512],
                                                  start=(kt == 0), stop=(kt == NKT - 1))
                            pe.inc(li)
                        blk.tensor(d4_pg)
                        snapD4gu[(e, mt)] = pe.n

                        def d4_gpw(g, e=e, mt=mt):
                            g.wait_ge(pe.h, snapD4gu[(e, mt)])
                            if mt + 2 < 12:
                                e2, m2 = e, mt + 2
                            else:
                                e2, m2 = e + 1, mt + 2 - 12
                            if e2 < NE:
                                qw.inc(g.dma_start(out=wgu[:, (e2 * 12 + m2) % 2, :, :],
                                                   in_=w_gu_t[e2, m2].rearrange(
                                                       "(kt kp) m -> kp kt m", kp=128)), 16)
                                qwsnapG[(e2, m2)] = qw.n
                        blk.gpsimd(d4_gpw)

                        if mt < 6:
                            def d4_ev1(s, e=e, mt=mt):
                                s.wait_ge(pe.h, snapD4gu[(e, mt)])
                                if e >= 1 and mt == 0:
                                    s.wait_ge(ev.h, evD4_before[e - 1] + 24)
                                for nr in range(2):
                                    i2 = s.activation(out=sg6[:, mt, nr * 512:(nr + 1) * 512],
                                                      in_=gps[:, nr, :], func=AF.Silu)
                                    ev.inc(i2)
                            blk.scalar(d4_ev1)
                            snapD4sg[(e, mt)] = ev.n
                        else:
                            def d4_ev2(v, e=e, mt=mt):
                                v.wait_ge(pe.h, snapD4gu[(e, mt)])
                                v.wait_ge(ev.h, snapD4sg[(e, mt - 6)])
                                if e >= 1 and mt == 6:
                                    v.wait_ge(pe.h, snapD4dnE[e - 1])
                                for nr in range(2):
                                    i2 = v.tensor_tensor(out=htil[:, mt - 6, nr * 512:(nr + 1) * 512],
                                                         in0=gps[:, nr, :],
                                                         in1=sg6[:, mt - 6, nr * 512:(nr + 1) * 512],
                                                         op=ALU.mult)
                                    ev.inc(i2)
                            blk.vector(d4_ev2)
                    snapD4guE[e] = pe.n
                    snapD4uend[e] = ev.n

                    for rt in range(8):
                        def d4_pd(t, e=e, ep=ep, rt=rt):
                            if rt == 0:
                                t.wait_ge(ev.h, snapD4uend[e])
                                t.wait_ge(qw.h, qwsnapD[e])
                                if e == 0:
                                    t.wait_ge(qs.h, snapD4zs)
                            if rt >= 1:
                                t.wait_ge(dv.h, snapD4y[(e, rt - 1)])
                            for nd in range(4):
                                for kt in range(6):
                                    li = t.matmul(dnps[:, nd, :], htil[:, kt, rt * 128:(rt + 1) * 128],
                                                  wdt[:, kt, nd * 512:(nd + 1) * 512],
                                                  start=(kt == 0), stop=(kt == 5))
                            pe.inc(li)
                        blk.tensor(d4_pd)
                        snapD4dn_rt = pe.n

                        def d4_dy(v, e=e, rt=rt, snapD4dn_rt=snapD4dn_rt):
                            v.wait_ge(pe.h, snapD4dn_rt)
                            idx2 = e * 8 + rt
                            if idx2 >= 2:
                                pidx = idx2 - 2
                                v.wait_ge(qs.h, snapD4st[(pidx // 8, pidx % 8)])
                            for nd in range(4):
                                ci = v.tensor_copy(out=ytile[:, rt % 2, nd * 512:(nd + 1) * 512],
                                                   in_=dnps[:, nd, :])
                            dv.inc(ci)
                        blk.vector(d4_dy)
                        snapD4y[(e, rt)] = dv.n

                        def d4_sy(s, e=e, rt=rt):
                            s.wait_ge(dv.h, snapD4y[(e, rt)])
                            r0 = e * CAP + rt * 128
                            qs.inc(s.dma_start(out=ydisp_d[r0:r0 + 128, :], in_=ytile[:, rt % 2, :]), 16)
                        blk.sync(d4_sy)
                        snapD4st[(e, rt)] = qs.n
                    snapD4dnE[e] = pe.n

                    if e + 1 < NE:
                        def d4_gpd(g, e=e):
                            g.wait_ge(pe.h, snapD4dnE[e])
                            qw.inc(g.dma_start(out=wdt[:],
                                               in_=w_d_t[e + 1].rearrange(
                                                   "(kt kp) d -> kp kt d", kp=128)), 16)
                            qwsnapD[e + 1] = qw.n
                        blk.gpsimd(d4_gpd)
                qs_d4_end = qs.n

            phase_barrier()
        if upto >= 9:
            with _multi(
                nc.sbuf_tensor("x2acc", [128, 2, DM], F32),
                nc.sbuf_tensor("ygt", [128, 4, DM], BF16),
            ) as (x2acc, ygt):
                qa_d5 = qa.n
                qi_d5 = qi.n
                snapD5stt = {}
                snapD5out = {}
                for i in range(NTT):
                    def d5_sy(s, i=i):
                        if i == 0:
                            s.wait_ge(qs.h, qs_d4_end)
                        if i >= 2:
                            s.wait_ge(qs.h, snapD5out[i - 2])
                        qa.inc(s.dma_start(out=x2acc[:, i % 2, :],
                                           in_=x2_d[i * 128:(i + 1) * 128, :]), 16)
                    blk.sync(d5_sy)

                    for s_ in range(TOPK):
                        it = i * TOPK + s_

                        def d5_gp(g, i=i, s_=s_, it=it):
                            if it == 0:
                                g.wait_ge(qs.h, qs_d4_end)
                            if it >= 4:
                                g.wait_ge(dv.h, snapD5stt[it - 4])
                            qi.inc(g.indirect_dma_start(
                                out=ygt[:, it % 4, :], out_offset=None,
                                in_=ydisp_d[:],
                                in_offset=IOA(ap=posu_all[:, s_ * NTT + i:s_ * NTT + i + 1],
                                              axis=0)), 16)
                        blk.gpsimd(d5_gp)

                        def d5_dv(v, i=i, s_=s_, it=it):
                            if s_ == 0:
                                v.wait_ge(qa.h, qa_d5 + 16 * (i + 1))
                            v.wait_ge(qi.h, qi_d5 + 16 * (it + 1))
                            dv.inc(v.scalar_tensor_tensor(out=x2acc[:, i % 2, :],
                                                          in0=ygt[:, it % 4, :],
                                                          scalar=w8_all[:, i, s_:s_ + 1],
                                                          in1=x2acc[:, i % 2, :],
                                                          op0=ALU.mult, op1=ALU.add))
                            snapD5stt[it] = dv.n
                        blk.vector(d5_dv)
                    sD5 = dv.n

                    def d5_out(s, i=i, sD5=sD5):
                        s.wait_ge(dv.h, sD5)
                        qs.inc(s.dma_start(out=out_d[i * 128:(i + 1) * 128, :],
                                           in_=x2acc[:, i % 2, :]), 16)
                    blk.sync(d5_out)
                    snapD5out[i] = qs.n

                def d5_fin(s):
                    s.wait_ge(qs.h, qs.n)
                blk.sync(d5_fin)


        if 1 <= upto < 2:
            ln1_cm.__exit__(None, None, None)
        if upto < 3:
            for cm in reversed(mres_cm):
                cm.__exit__(None, None, None)

        def _tail_fin(s):
            for s2 in (qs, qa, qw, qi, qc):
                s.wait_ge(s2.h, s2.n)
        blk.sync(_tail_fin)

        if dbg:
            scr = {"x_tm_d": x_tm_d, "sgate_d": sgate_d, "hgn_d": hgn_d,
                   "ln2_d": ln2_d, "x2_d": x2_d, "xdisp_d": xdisp_d, "ydisp_d": ydisp_d}

            def dbg_sync(s):
                s.wait_ge(qs.h, qs.n)
                s.wait_ge(dv.h, dv.n)
                s.wait_ge(ac.h, ac.n)
                s.wait_ge(gp.h, gp.n)
                s.wait_ge(qi.h, qi.n)
                for name, shape, dt in dbg:
                    if name == "ln1xT":
                        continue
                    if name == "posu_all":
                        qs.inc(s.dma_start(out=dbg_outs[name][:], in_=posu_all[:]), 16)
                        continue
                    if name == "w8_all":
                        qs.inc(s.dma_start(out=dbg_outs[name][:],
                                           in_=w8_all.rearrange("p a b -> p (a b)")), 16)
                        continue
                    if name == "idx8_all":
                        qs.inc(s.dma_start(out=dbg_outs[name][:],
                                           in_=idx8_all.rearrange("p a b -> p (a b)")), 16)
                        continue
                    src = scr[name]
                    sl = tuple(slice(0, z) for z in shape)
                    qs.inc(s.dma_start(out=dbg_outs[name][:], in_=src.__getitem__(sl)), 16)
                s.wait_ge(qs.h, qs.n)
            blk.sync(dbg_sync)

    return nc


def _consts():
    c = {}
    c["ident_bf"] = np.eye(128).astype(ml_dtypes.bfloat16)
    c["ident_f"] = np.eye(128, dtype=np.float32)
    c["iota32"] = np.broadcast_to(np.arange(NE, dtype=np.float32), (128, NE)).copy()
    c["trash_c"] = (NDISP + np.arange(128, dtype=np.float32)).reshape(128, 1)
    m = np.empty((2, 128, CH), np.float32)
    for st in range(2):
        s_glob = st * 128 + np.arange(128)[:, None]
        t_glob = np.arange(CH)[None, :]
        m[st] = np.where(t_glob >= s_glob, 0.0, MASKNEG)
    c["mask_st"] = m
    c["ustrict"] = np.triu(np.ones((128, 128), np.float32), 1)
    c["ones_r1"] = np.ones((1, 128), np.float32)
    c["ones_c"] = np.ones((128, 1), np.float32)
    bc = np.zeros((64, HEADS, 128), np.float32)
    for h in range(HEADS):
        bc[h, h, :] = 1.0
    c["bcsel"] = bc.astype(ml_dtypes.bfloat16)
    c["noti"] = (1.0 - np.eye(NE)).astype(np.float32)
    c["i32d"] = np.eye(NE, dtype=np.float32)
    mc = np.zeros((128, 8), np.float32)
    mc[:, 0] = 1.0
    mc[:, 1] = EPS
    mc[:, 2] = 1.0 / DM
    mc[:, 3] = 1.0 / DIN
    mc[:, 4] = -1.0
    c["misc_c"] = mc
    return c


def _prep_weights(inp):
    d = {}
    wt = np.ascontiguousarray(inp["L0_attn_in_proj_w"].T)
    pad = np.zeros((DM, NMT * 128 - PROJ), np.float32)
    wt = np.concatenate([wt, pad], axis=1)
    d["w_in_t"] = np.ascontiguousarray(wt.reshape(DM, NMT, 128).transpose(1, 0, 2))
    d["conv_w_t"] = np.ascontiguousarray(inp["L0_attn_conv_w"].reshape(34, 128, KCONV).transpose(1, 0, 2))
    d["conv_b_t"] = np.ascontiguousarray(inp["L0_attn_conv_b"].reshape(34, 128).T)
    d["dt_bias_c"] = np.ascontiguousarray(inp["L0_attn_dt_bias"].reshape(64, 1))
    d["a_log_c"] = np.ascontiguousarray(inp["L0_attn_A_log"].reshape(64, 1))
    d["d_row"] = np.ascontiguousarray(inp["L0_attn_D"].reshape(1, 64))
    d["norm_w_t"] = np.ascontiguousarray(inp["L0_attn_norm_w"].reshape(32, 128).T)
    d["w_out_t"] = np.ascontiguousarray(inp["L0_attn_out_proj_w"].T.reshape(32, 128, DM))
    d["w_g_t"] = np.ascontiguousarray(inp["L0_ffn_gate_weight"].T.reshape(NKT, 128, NE))
    wgu = inp["L0_ffn_gate_up_proj"]
    d["w_gu_t"] = np.ascontiguousarray(
        wgu.transpose(0, 2, 1).reshape(NE, DM, 12, 128).transpose(0, 2, 1, 3))
    d["w_d_t"] = np.ascontiguousarray(inp["L0_ffn_down_proj"].transpose(0, 2, 1))
    d["sim"] = np.ascontiguousarray(inp["L0_ffn_similarity"])
    return d


def kernel(**inputs):
    x = inputs["L0_attn_hidden_states"]
    shared = _prep_weights(inputs)
    shared.update(_consts())
    in_maps = []
    for c in range(8):
        m = dict(shared)
        m["x"] = np.ascontiguousarray(x[c])
        in_maps.append(m)
    nc = build_nc()
    res = run_bass_kernel_spmd(nc, in_maps, core_ids=list(range(8)))
    return np.stack([r["out"] for r in res.results]).astype(np.float32)




# version 03
# speedup vs baseline: 1.0542x; 1.0542x over previous
import numpy as np
import ml_dtypes

import concourse.bass as bass
import concourse.mybir as mybir
from concourse.bass_utils import run_bass_kernel_spmd

F32 = mybir.dt.float32
BF16 = mybir.dt.bfloat16
U32 = mybir.dt.uint32
AF = mybir.ActivationFunctionType
ALU = mybir.AluOpType
AX = mybir.AxisListType
IOA = bass.IndirectOffsetOnAxis

L = 2048
DM = 2048
DIN = 4096
HEADS = 64
HD = 64
STATE = 128
KCONV = 4
CD = DIN + 2 * STATE
PROJ = DIN + CD + HEADS
NMT = 67
CH = 256
NCH = L // CH
EPS = 1e-5
NE = 32
TOPK = 8
SSEL = 4
THR = 0.5
DFF = 768
CAP = 896
NDISP = NE * CAP
NTT = L // 128
NKT = DM // 128
MASKNEG = -1e30



from contextlib import contextmanager, ExitStack


@contextmanager
def _multi(*cms):
    with ExitStack() as _st:
        yield [_st.enter_context(cm) for cm in cms]


class Sem:
    __slots__ = ("h", "n")

    def __init__(self, stack, nc, name):
        self.h = stack.enter_context(nc.semaphore(name))
        self.n = 0

    def inc(self, instr, k=1):
        instr.then_inc(self.h, k)
        self.n += k
        return instr


def build_nc(dbg=None, upto=99, serial_b=False):
    nc = bass.Bass("TRN2", target_bir_lowering=False)

    def dram(name, shape, dt, kind):
        return nc.dram_tensor(name, shape, dt, kind=kind).ap()

    x_in = dram("x", [L, DM], F32, "ExternalInput")
    w_in_t = dram("w_in_t", [NMT, DM, 128], F32, "ExternalInput")
    conv_w_t = dram("conv_w_t", [128, 34, KCONV], F32, "ExternalInput")
    conv_b_t = dram("conv_b_t", [128, 34], F32, "ExternalInput")
    dt_bias_c = dram("dt_bias_c", [64, 1], F32, "ExternalInput")
    a_log_c = dram("a_log_c", [64, 1], F32, "ExternalInput")
    d_row_i = dram("d_row", [1, 64], F32, "ExternalInput")
    norm_w_t = dram("norm_w_t", [128, 32], F32, "ExternalInput")
    w_out_t = dram("w_out_t", [32, 128, DM], F32, "ExternalInput")
    w_g_t = dram("w_g_t", [NKT, 128, NE], F32, "ExternalInput")
    w_gu_t = dram("w_gu_t", [NE, 12, DM, 128], F32, "ExternalInput")
    w_d_t = dram("w_d_t", [NE, DFF, DM], F32, "ExternalInput")
    sim_i = dram("sim", [NE, NE], F32, "ExternalInput")
    ident_bf_i = dram("ident_bf", [128, 128], BF16, "ExternalInput")
    ident_f_i = dram("ident_f", [128, 128], F32, "ExternalInput")
    iota32_i = dram("iota32", [128, NE], F32, "ExternalInput")
    trash_i = dram("trash_c", [128, 1], F32, "ExternalInput")
    mask_i = dram("mask_st", [2, 128, CH], F32, "ExternalInput")
    ustrict_i = dram("ustrict", [128, 128], F32, "ExternalInput")
    ones_r1_i = dram("ones_r1", [1, 128], F32, "ExternalInput")
    ones_c_i = dram("ones_c", [128, 1], F32, "ExternalInput")
    bcsel_i = dram("bcsel", [64, HEADS, 128], BF16, "ExternalInput")
    noti_i = dram("noti", [NE, NE], F32, "ExternalInput")
    i32_i = dram("i32d", [NE, NE], F32, "ExternalInput")
    misc_i = dram("misc_c", [128, 8], F32, "ExternalInput")

    x_tm_d = dram("x_tm_d", [L, DIN], BF16, "Internal")
    sgate_d = dram("sgate_d", [L, DIN], BF16, "Internal")
    hgn_d = dram("hgn_d", [32, 128, L], BF16, "Internal")
    ln2_d = dram("ln2_d", [L, DM], BF16, "Internal")
    x2_d = dram("x2_d", [L, DM], F32, "Internal")
    xdisp_d = dram("xdisp_d", [NDISP + 128, DM], BF16, "Internal")
    ydisp_d = dram("ydisp_d", [NDISP + 128, DM], BF16, "Internal")

    out_d = dram("out", [L, DM], F32, "ExternalOutput")
    dbg_outs = {}
    if dbg:
        for name, shape, dt in dbg:
            dbg_outs[name] = dram("dbg_" + name, shape, dt, "ExternalOutput")

    es = ExitStack()
    with es:
        def sb(name, shape, dt):
            return es.enter_context(nc.sbuf_tensor(name, shape, dt))

        identb = sb("identb", [128, 128], BF16)
        identf = sb("identf", [128, 128], F32)
        iota32 = sb("iota32s", [128, NE], F32)
        trash = sb("trashs", [128, 1], F32)
        maskst = sb("masksts", [128, 2, CH], F32)
        ustrict = sb("ustricts", [128, 128], F32)
        ones_r1 = sb("ones_r1s", [1, 128], F32)
        ones_c = sb("ones_cs", [128, 1], F32)
        convw = sb("convws", [128, 34, KCONV], F32)
        convb = sb("convbs", [128, 34], F32)
        dtbias = sb("dtbiass", [64, 1], F32)
        alogc = sb("alogcs", [64, 1], F32)
        acol = sb("acols", [64, 1], F32)
        drow = sb("drows", [1, 64], F32)
        dbc = sb("dbcs", [128, 64], F32)
        normw = sb("normws", [128, 32], F32)
        simd = sb("simds", [NE, NE], F32)
        simraw = sb("simraws", [NE, NE], F32)
        noti = sb("notis", [NE, NE], F32)
        i32d = sb("i32ds", [NE, NE], F32)
        miscc = sb("misccs", [128, 8], F32)
        wg_hi = sb("wg_hi", [128, NKT, NE], BF16)
        wg_lo = sb("wg_lo", [128, NKT, NE], BF16)
        wg_f = sb("wg_f", [128, NKT, NE], F32)
        bcsel = sb("bcsels", [64, HEADS, 128], BF16)

        logits_tm = sb("logits_tm", [128, NTT, NE], F32)
        w8_all = sb("w8_all", [128, NTT, TOPK], F32)
        idx8_all = sb("idx8_all", [128, NTT, TOPK], F32)
        posu_all = sb("posu_all", [128, TOPK * NTT], U32)

        names = ["qc", "qa", "qw", "qs", "qi", "pe", "dv", "dw", "ev", "ac", "gp"]
        S = {k: Sem(es, nc, "sem_" + k) for k in names}
        qc, qa, qw, qs, qi, pe, dv, dw, ev, ac, gp = (S[k] for k in names)

        blk = es.enter_context(nc.Block())


        def phase_barrier():
            bar = [(dv, dv.n), (dw, dw.n), (ev, ev.n), (ac, ac.n), (gp, gp.n),
                   (pe, pe.n), (qs, qs.n), (qi, qi.n), (qa, qa.n), (qw, qw.n),
                   (qc, qc.n)]

            def mk(eng):
                for s2, v2 in bar:
                    eng.wait_ge(s2.h, v2)
            blk.tensor(mk)
            blk.vector(mk)
            blk.scalar(mk)
            blk.gpsimd(mk)
            blk.sync(mk)

        def p0_sync(s):
            for dst, src in [
                (identb, ident_bf_i), (identf, ident_f_i), (iota32, iota32_i),
                (trash, trash_i), (ustrict, ustrict_i),
                (ones_r1, ones_r1_i), (ones_c, ones_c_i), (bcsel, bcsel_i),
                (dtbias, dt_bias_c), (alogc, a_log_c), (drow, d_row_i),
                (simraw, sim_i), (noti, noti_i), (i32d, i32_i), (miscc, misc_i),
            ]:
                qc.inc(s.dma_start(out=dst[:], in_=src[:]), 16)
            qc.inc(s.dma_start(out=maskst[:], in_=mask_i.rearrange("a p c -> p a c")), 16)
            qc.inc(s.dma_start(out=convw[:], in_=conv_w_t[:]), 16)
            qc.inc(s.dma_start(out=convb[:], in_=conv_b_t[:]), 16)
            qc.inc(s.dma_start(out=normw[:], in_=norm_w_t[:]), 16)
            qc.inc(s.dma_start(out=wg_f[:], in_=w_g_t.rearrange("a p c -> p a c")), 16)
        blk.sync(p0_sync)
        qc_all = qc.n

        p0ps_cm = nc.psum_tensor("p0ps", [128, 64], F32)
        p0ps = p0ps_cm.__enter__()

        def p0_pe(t):
            t.wait_ge(qc.h, qc_all)
            pe.inc(t.matmul(p0ps[:], ones_r1[:], drow[:], start=True, stop=True))
        blk.tensor(p0_pe)
        snap_p0pe = pe.n

        def p0_dv(v):
            v.wait_ge(qc.h, qc_all)
            v.wait_ge(pe.h, snap_p0pe)
            v.tensor_copy(out=dbc[:], in_=p0ps[:])
            v.tensor_tensor(out=simd[:], in0=simraw[:], in1=noti[:], op=ALU.mult)
            v.tensor_tensor(out=simd[:], in0=simd[:], in1=i32d[:], op=ALU.add)
            v.tensor_copy(out=wg_hi[:], in_=wg_f[:])
            dv.inc(v.tensor_tensor(out=wg_lo[:], in0=wg_f[:], in1=wg_hi[:], op=ALU.subtract))
        blk.vector(p0_dv)
        snap_p0dv = dv.n

        def p0_ac(s):
            s.wait_ge(qc.h, qc_all)
            s.activation(out=acol[:], in_=alogc[:], func=AF.Exp)
            ac.inc(s.mul(acol[:], acol[:], -1.0))
        blk.scalar(p0_ac)
        snap_p0ac = ac.n
        p0ps_cm.__exit__(None, None, None)

        mres_cm = [
            nc.sbuf_tensor("b_fm", [128, L], BF16),
            nc.sbuf_tensor("c_fm", [128, L], BF16),
            nc.sbuf_tensor("b_tm", [128, NTT, 128], BF16),
            nc.sbuf_tensor("dt_fm", [64, L], F32),
            nc.sbuf_tensor("da_fm", [64, L], F32),
            nc.sbuf_tensor("dt_tm", [128, NTT, 64], F32),
            nc.sbuf_tensor("s_all", [128, HEADS * HD], F32),
            nc.sbuf_tensor("s_bf", [128, HEADS * HD], BF16),
        ]
        b_fm, c_fm, b_tm, dt_fm, da_fm, dt_tm, s_all, s_bf = [cm.__enter__() for cm in mres_cm]

        if upto >= 1:
            ln1_cm = nc.sbuf_tensor("ln1xT", [128, NKT, L], BF16)
            ln1xT = ln1_cm.__enter__()
            with _multi(
                nc.sbuf_tensor("axt", [128, 2, DM], F32),
                nc.sbuf_tensor("axc", [128, 2, DM], F32),
                nc.sbuf_tensor("acl", [128, 8], F32),
                nc.sbuf_tensor("alnb", [128, 2, DM], BF16),
                nc.psum_tensor("aptb", [128, 2, 1024], BF16),
            ) as (axt, axc, acl, alnb, aptb):
                qa_a0 = qa.n
                snapA1 = [None] * NTT
                snapAT = [None] * NTT
                snapApt = {}
                snapApc = {}
                snapApc = {}
                for i in range(NTT):
                    p = i % 2

                    def a_sy(s, i=i, p=p):
                        if i >= 2:
                            s.wait_ge(dv.h, snapA1[i - 2])
                        qa.inc(s.dma_start(out=axt[:, p, :], in_=x_in[i * 128:(i + 1) * 128, :]), 16)
                    blk.sync(a_sy)

                    def a_dv1(v, i=i, p=p):
                        v.wait_ge(qa.h, qa_a0 + 16 * (i + 1))
                        v.tensor_reduce(out=acl[:, 0:1], in_=axt[:, p, :], axis=AX.X, op=ALU.add)
                        v.drain()
                        v.tensor_scalar(out=acl[:, 1:2], in0=acl[:, 0:1], scalar1=1.0 / DM,
                                        scalar2=None, op0=ALU.mult)
                        v.drain()
                        dv.inc(v.tensor_scalar(out=axc[:, p, :], in0=axt[:, p, :], scalar1=acl[:, 1:2],
                                               scalar2=None, op0=ALU.subtract))
                    blk.vector(a_dv1)
                    snapA1[i] = dv.n

                    def a_ac(s, i=i, p=p):
                        s.wait_ge(dv.h, snapA1[i])
                        if i >= 2:
                            s.wait_ge(pe.h, snapAT[i - 2])
                        s.activation(out=alnb[:, p, :], in_=axc[:, p, :], func=AF.Square,
                                     accum_out=acl[:, 4:5])
                        ac.inc(s.activation(out=acl[:, 5:6], in_=acl[:, 4:5], func=AF.Sqrt,
                                            scale=miscc[:, 2:3], bias=miscc[:, 1:2]))
                    blk.scalar(a_ac)
                    snapAac = ac.n

                    def a_dv2(v, i=i, p=p, snapAac=snapAac):
                        v.wait_ge(ac.h, snapAac)
                        if i >= 2:
                            v.wait_ge(pe.h, snapAT[i - 2])
                        v.reciprocal(out=acl[:, 6:7], in_=acl[:, 5:6])
                        v.drain()
                        dv.inc(v.tensor_scalar(out=alnb[:, p, :], in0=axc[:, p, :], scalar1=acl[:, 6:7],
                                               scalar2=None, op0=ALU.mult))
                    blk.vector(a_dv2)
                    snapA2 = dv.n

                    for grp in range(2):
                        G = i * 2 + grp
                        bk = G % 2
                        dw_g2 = snapApc.get(G - 2)

                        def a_pe(t, i=i, p=p, grp=grp, bk=bk, snapA2=snapA2, dw_g2=dw_g2):
                            if grp == 0:
                                t.wait_ge(dv.h, snapA2)
                            if dw_g2 is not None:
                                t.wait_ge(dw.h, dw_g2)
                            for kk in range(8):
                                kt = grp * 8 + kk
                                sl = aptb[:, bk, kk * 128:kk * 128 + 128]
                                pe.inc(t.transpose(sl, alnb[:, p, kt * 128:(kt + 1) * 128], identb[:]))
                        blk.tensor(a_pe)
                        snapApt[G] = pe.n

                        def a_dv3(v, i=i, p=p, grp=grp, bk=bk, G=G):
                            v.wait_ge(pe.h, snapApt[G])
                            for kk in range(8):
                                kt = grp * 8 + kk
                                dw.inc(v.tensor_copy(out=ln1xT[:, kt, i * 128:(i + 1) * 128],
                                                     in_=aptb[:, bk, kk * 128:kk * 128 + 128]))
                        blk.vector(a_dv3)
                        snapApc[G] = dw.n
                    snapAT[i] = pe.n
                snapA_dv = dv.n
                snapA_dw = dw.n
                if dbg and any(n == "ln1xT" for n, _, _ in dbg):
                    def a_dump(s):
                        s.wait_ge(dw.h, dw.n)
                        s.wait_ge(dv.h, dv.n)
                        s.wait_ge(pe.h, pe.n)
                        qs.inc(s.dma_start(out=dbg_outs["ln1xT"][:],
                                           in_=ln1xT.rearrange("p a b -> p (a b)")), 16)
                        s.wait_ge(qs.h, qs.n)
                    blk.sync(a_dump)

            phase_barrier()
        if upto >= 2:
            with _multi(
                nc.sbuf_tensor("wbuf", [128, 2, NKT, 128], BF16),
                nc.sbuf_tensor("gcx", [128, DM], BF16),
                nc.sbuf_tensor("stage", [128, 2, NKT, 128], BF16),
                nc.sbuf_tensor("convin", [128, 3 + L], F32),
                nc.sbuf_tensor("cacc", [128, L], F32),
                nc.psum_tensor("bps", [128, 4, 512], F32),
                nc.psum_tensor("bptb", [128, 2, 1024], BF16),
                nc.psum_tensor("bptf", [128, 2, 512], F32),
            ) as (wbuf, gcx, stage, convin, cacc, bps, bptb, bptf):
                dtr = convin[0:64, 3:3 + L]
                dtt1 = cacc[0:64, 0:L]

                def b_init(v):
                    dv.inc(v.memset(convin[:, 0:3], 0.0))
                blk.vector(b_init)
                snapBz = dv.n

                qw_b0 = qw.n
                qwsnapB = {}
                bgrp_ctr = [0]
                snapBgt = {}
                snapBgc = {}
                ev_b0 = ev.n
                evB_before = {}
                snapBmm = [None] * NMT
                snapBsilu = {}
                snapBT = {}
                snapBC = {}
                last_stage_dma = [None, None]
                last_gcx_T = [None]

                def sec_of(m):
                    return "gate" if m < 32 else ("xbc" if m < 66 else "dt")

                for m0 in range(2):
                    def b_gp0(g, m0=m0):
                        if m0 == 0:
                            g.wait_ge(qc.h, qc_all)
                        qw.inc(g.dma_start(out=wbuf[:, m0 % 2, :, :],
                                           in_=w_in_t[m0].rearrange("(kt kp) m -> kp kt m", kp=128)), 16)
                        qwsnapB[m0] = qw.n
                    blk.gpsimd(b_gp0)

                for m in range(NMT):
                    if serial_b:
                        phase_barrier()
                    sec = sec_of(m)
                    evB_before[m] = ev.n

                    def b_pe(t, m=m):
                        t.wait_ge(qw.h, qwsnapB[m])
                        if m == 0:
                            t.wait_ge(dv.h, snapA_dv)
                            t.wait_ge(dw.h, snapA_dw)
                        if m >= 1:
                            t.wait_ge(ev.h, evB_before[m - 1] + 4)
                        for kt in range(NKT):
                            for nt in range(4):
                                li = t.matmul(bps[:, nt, :], wbuf[:, m % 2, kt, :],
                                              ln1xT[:, kt, nt * 512:(nt + 1) * 512],
                                              start=(kt == 0), stop=(kt == NKT - 1),
                                              skip_group_check=True)
                        pe.inc(li)
                    blk.tensor(b_pe)
                    snapBmm[m] = pe.n

                    if m + 2 < NMT:
                        def b_gp(g, m=m):
                            g.wait_ge(pe.h, snapBmm[m])
                            qw.inc(g.dma_start(out=wbuf[:, (m + 2) % 2, :, :],
                                               in_=w_in_t[m + 2].rearrange("(kt kp) m -> kp kt m",
                                                                           kp=128)), 16)
                            qwsnapB[m + 2] = qw.n
                        blk.gpsimd(b_gp)

                    if sec == "gate":
                        def b_ac(s, m=m):
                            s.wait_ge(pe.h, snapBmm[m])
                            if last_gcx_T[0] is not None:
                                s.wait_ge(pe.h, last_gcx_T[0])
                            for nt in range(4):
                                i2 = s.activation(out=gcx[:, nt * 512:(nt + 1) * 512],
                                                  in_=bps[:, nt, :], func=AF.Silu)
                                ev.inc(i2)
                        blk.scalar(b_ac)
                        snapBsilu_ev = ev.n

                        for grp in range(2):
                            G = bgrp_ctr[0]
                            bgrp_ctr[0] += 1
                            bk = G % 2
                            dw_g2 = snapBgc.get(G - 2)

                            def b_pet(t, m=m, grp=grp, bk=bk, dw_g2=dw_g2, snapBsilu_ev=snapBsilu_ev):
                                if grp == 0:
                                    t.wait_ge(ev.h, snapBsilu_ev)
                                    if last_stage_dma[m % 2] is not None:
                                        t.wait_ge(qs.h, last_stage_dma[m % 2])
                                if dw_g2 is not None:
                                    t.wait_ge(dw.h, dw_g2)
                                for kk in range(8):
                                    tt = grp * 8 + kk
                                    sl = bptb[:, bk, kk * 128:kk * 128 + 128]
                                    pe.inc(t.transpose(sl, gcx[:, tt * 128:(tt + 1) * 128], identb[:]))
                            blk.tensor(b_pet)
                            snapBgt[G] = pe.n

                            def b_dvc(v, m=m, grp=grp, bk=bk, G=G):
                                v.wait_ge(pe.h, snapBgt[G])
                                for kk in range(8):
                                    tt = grp * 8 + kk
                                    dw.inc(v.tensor_copy(out=stage[:, m % 2, tt, :],
                                                         in_=bptb[:, bk, kk * 128:kk * 128 + 128]))
                            blk.vector(b_dvc)
                            snapBgc[G] = dw.n
                        last_gcx_T[0] = pe.n
                        snapBcp = dw.n

                        def b_sy(s, m=m, snapBcp=snapBcp):
                            s.wait_ge(dw.h, snapBcp)
                            od = sgate_d.rearrange("(tt p) f -> p tt f", p=128)
                            qs.inc(s.dma_start(out=od[:, :, m * 128:(m + 1) * 128],
                                               in_=stage[:, m % 2, :, :]), 16)
                        blk.sync(b_sy)
                        last_stage_dma[m % 2] = qs.n

                    elif sec == "xbc":
                        ct = m - 32

                        def b_dv(v, m=m, ct=ct):
                            v.wait_ge(pe.h, snapBmm[m])
                            if m == 32:
                                v.wait_ge(dv.h, snapBz)
                            for nt in range(4):
                                ev.inc(v.tensor_copy(out=convin[:, 3 + nt * 512:3 + (nt + 1) * 512],
                                                     in_=bps[:, nt, :]))
                            if ct > 0:
                                v.wait_ge(ac.h, snapBsilu[m - 1])
                            v.tensor_scalar(out=cacc[:], in0=convin[:, 0:L],
                                            scalar1=convw[:, ct, 0:1], scalar2=convb[:, ct:ct + 1],
                                            op0=ALU.mult, op1=ALU.add)
                            for kk in range(1, KCONV):
                                li = v.scalar_tensor_tensor(out=cacc[:], in0=convin[:, kk:kk + L],
                                                            scalar=convw[:, ct, kk:kk + 1], in1=cacc[:],
                                                            op0=ALU.mult, op1=ALU.add)
                            dv.inc(li)
                        blk.vector(b_dv)
                        snapBcv = dv.n

                        def b_ac(s, m=m, ct=ct, snapBcv=snapBcv):
                            s.wait_ge(dv.h, snapBcv)
                            if ct < 32 and last_gcx_T[0] is not None:
                                s.wait_ge(pe.h, last_gcx_T[0])
                            dst = b_fm if ct == 32 else (c_fm if ct == 33 else gcx)
                            ac.inc(s.activation(out=dst[:], in_=cacc[:], func=AF.Silu))
                        blk.scalar(b_ac)
                        snapBsilu[m] = ac.n

                        if ct <= 32:
                            for grp in range(2):
                                G = bgrp_ctr[0]
                                bgrp_ctr[0] += 1
                                bk = G % 2
                                dw_g2 = snapBgc.get(G - 2)

                                def b_pet(t, m=m, ct=ct, grp=grp, bk=bk, dw_g2=dw_g2):
                                    if grp == 0:
                                        t.wait_ge(ac.h, snapBsilu[m])
                                        if ct < 32 and last_stage_dma[m % 2] is not None:
                                            t.wait_ge(qs.h, last_stage_dma[m % 2])
                                    if dw_g2 is not None:
                                        t.wait_ge(dw.h, dw_g2)
                                    src = b_fm if ct == 32 else gcx
                                    for kk in range(8):
                                        tt = grp * 8 + kk
                                        sl = bptb[:, bk, kk * 128:kk * 128 + 128]
                                        pe.inc(t.transpose(sl, src[:, tt * 128:(tt + 1) * 128],
                                                           identb[:]))
                                blk.tensor(b_pet)
                                snapBgt[G] = pe.n

                                def b_dvc(v, m=m, ct=ct, grp=grp, bk=bk, G=G):
                                    v.wait_ge(pe.h, snapBgt[G])
                                    for kk in range(8):
                                        tt = grp * 8 + kk
                                        dst = b_tm[:, tt, :] if ct == 32 else stage[:, m % 2, tt, :]
                                        dw.inc(v.tensor_copy(
                                            out=dst, in_=bptb[:, bk, kk * 128:kk * 128 + 128]))
                                blk.vector(b_dvc)
                                snapBgc[G] = dw.n
                            if ct < 32:
                                last_gcx_T[0] = pe.n
                            snapBcp = dw.n

                            if ct < 32:
                                def b_sy(s, m=m, ct=ct, snapBcp=snapBcp):
                                    s.wait_ge(dw.h, snapBcp)
                                    od = x_tm_d.rearrange("(tt p) f -> p tt f", p=128)
                                    qs.inc(s.dma_start(out=od[:, :, ct * 128:(ct + 1) * 128],
                                                       in_=stage[:, m % 2, :, :]), 16)
                                blk.sync(b_sy)
                                last_stage_dma[m % 2] = qs.n
                    else:
                        def b_dv(v, m=m):
                            v.wait_ge(pe.h, snapBmm[m])
                            v.wait_ge(ac.h, snapBsilu[65])
                            for nt in range(4):
                                ev.inc(v.tensor_copy(out=dtr[:, nt * 512:(nt + 1) * 512],
                                                     in_=bps[0:64, nt, :]))
                            dv.inc(v.tensor_scalar(out=dtr[:], in0=dtr[:], scalar1=dtbias[:],
                                                   scalar2=None, op0=ALU.add))
                        blk.vector(b_dv)
                        snapBdt1 = dv.n

                        def b_ac(s, m=m, snapBdt1=snapBdt1):
                            s.wait_ge(dv.h, snapBdt1)
                            s.activation(out=dtt1[:], in_=dtr[:], func=AF.Abs)
                            s.activation(out=dtt1[:], in_=dtt1[:], func=AF.Exp, scale=miscc[0:64, 4:5])
                            ac.inc(s.activation(out=dtt1[:], in_=dtt1[:], func=AF.Ln,
                                                bias=miscc[0:64, 0:1]))
                        blk.scalar(b_ac)
                        snapBdt2 = ac.n

                        def b_dv2(v, m=m, snapBdt2=snapBdt2):
                            v.wait_ge(ac.h, snapBdt2)
                            v.wait_ge(ac.h, snap_p0ac)
                            v.tensor_scalar(out=dt_fm[:], in0=dtr[:], scalar1=0.0, scalar2=None,
                                            op0=ALU.max)
                            v.tensor_tensor(out=dt_fm[:], in0=dt_fm[:], in1=dtt1[:], op=ALU.add)
                            dv.inc(v.tensor_scalar(out=da_fm[:], in0=dt_fm[:], scalar1=acol[:],
                                                   scalar2=None, op0=ALU.mult))
                        blk.vector(b_dv2)
                        snapBdt3 = dv.n

                        for grp in range(2):
                            bk = grp % 2

                            def b_pe2(t, m=m, grp=grp, bk=bk, snapBdt3=snapBdt3):
                                if grp == 0:
                                    t.wait_ge(dv.h, snapBdt3)
                                for kk in range(8):
                                    tt = grp * 8 + kk
                                    pe.inc(t.transpose(bptf[:, bk, kk * 64:kk * 64 + 64],
                                                       dt_fm[:, tt * 128:(tt + 1) * 128],
                                                       identf[0:64, 0:64]))
                            blk.tensor(b_pe2)
                            sBt = pe.n

                            def b_dv3(v, m=m, grp=grp, bk=bk, sBt=sBt):
                                v.wait_ge(pe.h, sBt)
                                for kk in range(8):
                                    tt = grp * 8 + kk
                                    dw.inc(v.tensor_copy(out=dt_tm[:, tt, :],
                                                         in_=bptf[:, bk, kk * 64:kk * 64 + 64]))
                            blk.vector(b_dv3)

                snapB_dv = dv.n
                snapB_dw = dw.n
                snapB_ac = ac.n
                qs_b_end = qs.n
            ln1_cm.__exit__(None, None, None)

            phase_barrier()
        if upto >= 3:
            with _multi(
                nc.sbuf_tensor("xch", [128, 2, 2, DIN], BF16),
                nc.sbuf_tensor("sgch", [128, 2, DIN], BF16),
                nc.sbuf_tensor("dacum", [64, CH], F32),
                nc.sbuf_tensor("dachi", [64, CH], BF16),
                nc.sbuf_tensor("daclo", [64, CH], BF16),
                nc.sbuf_tensor("dacum_tm", [128, 2, 64], F32),
                nc.sbuf_tensor("expda_tm", [128, 2, 64], F32),
                nc.sbuf_tensor("zeros64", [64, CH], F32),
                nc.sbuf_tensor("segT", [128, 2, 2, CH], F32),
                nc.sbuf_tensor("mts", [128, 2, 2, CH], BF16),
                nc.sbuf_tensor("xdt", [128, 2, 2, HD], BF16),
                nc.sbuf_tensor("xdec", [128, 2, 2, HD], BF16),
                nc.sbuf_tensor("ydc", [128, 2, 2, HD], F32),
                nc.sbuf_tensor("xdd", [128, 2, 2, HD], F32),
                nc.sbuf_tensor("y1c", [128, 2, 2, HD], F32),
                nc.sbuf_tensor("ych", [128, 2, DIN], BF16),
                nc.sbuf_tensor("ecol", [128, 64], F32),
                nc.sbuf_tensor("erow", [1, 64], F32),
                nc.sbuf_tensor("sqs", [128, DIN], BF16),
                nc.sbuf_tensor("rcol", [128, 8], F32),
                nc.sbuf_tensor("hstage", [128, 32, 128], BF16),
                nc.psum_tensor("cpsS", [128, 2, CH], F32),
                nc.psum_tensor("cpA", [128, 2, 512], F32),
                nc.psum_tensor("cpY", [128, 2, 512], F32),
                nc.psum_tensor("cpM", [128, 512], F32),
                nc.psum_tensor("cptb", [128, 2, 1024], BF16),
            ) as (xch, sgch, dacum, dachi, daclo, dacum_tm, expda_tm, zeros64, segT, mts, xdt, xdec, ydc, xdd, y1c, ych, ecol, erow, sqs, rcol, hstage, cpsS, cpA, cpY, cpM, cptb):
                def c_init(v):
                    v.wait_ge(qs.h, qs_b_end)
                    v.memset(s_all[:], 0.0)
                    v.memset(s_bf[:], 0.0)
                    dv.inc(v.memset(zeros64[:], 0.0))
                blk.vector(c_init)
                snapCini = dv.n

                qa_c0 = qa.n
                snapCgpend = [None] * NCH
                snapChgmul = [None] * NCH
                snapCsbf = [None] * NCH
                snapCh2last = [None] * NCH
                snapCecl = [None] * NCH
                snapCerow = [None] * NCH
                snapChgT = [None] * NCH
                snapCcprT = [None] * NCH
                last_hstage_dma = [None]
                cgrp_ctr = [0]
                snapCgt = {}
                snapCgc = {}

                xv = x_tm_d.rearrange("(c tt p) f -> p c tt f", p=128, tt=2)
                gv = sgate_d.rearrange("(c tt p) f -> p c tt f", p=128, tt=2)
                for c in range(NCH):
                    cp = c % 2

                    def c_sy(s, c=c, cp=cp):
                        if c == 0:
                            s.wait_ge(dv.h, snapCini)
                        if c >= 2:
                            s.wait_ge(gp.h, snapCgpend[c - 2])
                            s.wait_ge(dv.h, snapChgmul[c - 2])
                        qa.inc(s.dma_start(out=xch[:, cp, :, :], in_=xv[:, c, :, :]), 16)
                        if c >= 1:
                            s.wait_ge(dv.h, snapChgmul[c - 1])
                            s.wait_ge(pe.h, snapChgT[c - 1])
                        qa.inc(s.dma_start(out=sgch[:], in_=gv[:, c, :, :]), 16)
                    blk.sync(c_sy)

                    def c_scan(v, c=c):
                        v.wait_ge(qa.h, qa_c0 + 32 * (c + 1))
                        if c >= 1:
                            v.wait_ge(pe.h, snapCcprT[c - 1])
                        v.tensor_tensor_scan(out=dacum[:], data0=da_fm[:, c * CH:(c + 1) * CH],
                                             data1=zeros64[:], initial=0.0, op0=ALU.add, op1=ALU.add)
                        v.tensor_copy(out=dachi[:], in_=dacum[:])
                        dv.inc(v.tensor_tensor(out=daclo[:], in0=dacum[:], in1=dachi[:],
                                               op=ALU.subtract))
                    blk.vector(c_scan)
                    snapCscan = dv.n

                    def c_pe0(t, c=c, snapCscan=snapCscan):
                        t.wait_ge(dv.h, snapCscan)
                        if c == 0:
                            t.wait_ge(dv.h, snapB_dv)
                            t.wait_ge(dw.h, snapB_dw)
                            t.wait_ge(ac.h, snapB_ac)
                        if c >= 1:
                            t.wait_ge(ac.h, snapCerow[c - 1])
                        pe.inc(t.transpose(cpM[0:1, 128:192], dacum[:, CH - 1:CH], identf[0:64, 0:64]))
                    blk.tensor(c_pe0)
                    snapCcpr = pe.n

                    def c_ac0(s, c=c, snapCcpr=snapCcpr):
                        s.wait_ge(pe.h, snapCcpr)
                        ac.inc(s.activation(out=erow[:], in_=cpM[0:1, 128:192], func=AF.Exp))
                    blk.scalar(c_ac0)
                    snapCerow[c] = ac.n

                    def c_pe0b(t, c=c):
                        t.wait_ge(ac.h, snapCerow[c])
                        if c >= 1:
                            t.wait_ge(dv.h, snapCecl[c - 1])
                        pe.inc(t.matmul(cpM[:, 0:64], ones_r1[:], erow[:], start=True, stop=True))
                    blk.tensor(c_pe0b)
                    snapCpec = pe.n

                    def c_dve(v, c=c, snapCpec=snapCpec):
                        v.wait_ge(pe.h, snapCpec)
                        dv.inc(v.tensor_copy(out=ecol[:], in_=cpM[:, 0:64]))
                    blk.vector(c_dve)
                    snapCecl_pre = dv.n

                    for st in range(2):
                        dv_now = dv.n

                        def c_pt(t, c=c, st=st, snapCscan=snapCscan, dv_now=dv_now):
                            if st == 0:
                                t.wait_ge(dv.h, snapCscan)
                            t.wait_ge(dv.h, dv_now)
                            pe.inc(t.transpose(cpM[:, 64:128], dacum[:, st * 128:(st + 1) * 128],
                                               identf[0:64, 0:64]))
                        blk.tensor(c_pt)
                        sPT = pe.n

                        def c_ptc(v, c=c, st=st, sPT=sPT):
                            v.wait_ge(pe.h, sPT)
                            dv.inc(v.tensor_copy(out=dacum_tm[:, st, :], in_=cpM[:, 64:128]))
                        blk.vector(c_ptc)
                    snapCdat = dv.n

                    def c_act(s, c=c, snapCdat=snapCdat):
                        s.wait_ge(dv.h, snapCdat)
                        ac.inc(s.activation(out=expda_tm.rearrange("p a b -> p (a b)"),
                                            in_=dacum_tm.rearrange("p a b -> p (a b)"), func=AF.Exp))
                    blk.scalar(c_act)
                    snapCexp = ac.n

                    def c_psc(t, c=c, snapCdat=snapCdat):
                        t.wait_ge(dv.h, snapCdat)
                        if c >= 1:
                            t.wait_ge(dw.h, snapCh2last[c - 1])
                        for st in range(2):
                            li = t.matmul(cpsS[:, st, :],
                                          b_fm[:, c * CH + st * 128:c * CH + st * 128 + 128],
                                          c_fm[:, c * CH:(c + 1) * CH], start=True, stop=True)
                        pe.inc(li)
                    blk.tensor(c_psc)
                    snapCsc = pe.n

                    snapCh1 = {}
                    snapCh2 = {}
                    snapCh3 = {}
                    snapCpR = {}
                    snapCpY = {}
                    snapCgph = {}
                    snapCeseg = {}
                    for h in range(HEADS):
                        hp = h % 2

                        def c_pe1(t, c=c, h=h, hp=hp):
                            if h == 0:
                                t.wait_ge(ac.h, snapCexp)
                                t.wait_ge(pe.h, snapCsc)
                                if c >= 1:
                                    t.wait_ge(dv.h, snapCsbf[c - 1])
                            if h >= 2:
                                t.wait_ge(dv.h, snapCh1[h - 2])
                            t.matmul(cpA[:, hp, 0:CH], bcsel[:, h, :], dachi[:], start=True, stop=False)
                            pe.inc(t.matmul(cpA[:, hp, 0:CH], bcsel[:, h, :], daclo[:],
                                            start=False, stop=True))
                        blk.tensor(c_pe1)
                        snapCpR[h] = pe.n

                        def c_dv1(v, c=c, h=h, hp=hp):
                            v.wait_ge(pe.h, snapCpR[h])
                            if h >= 2:
                                v.wait_ge(ac.h, snapCeseg[h - 2])
                                v.wait_ge(pe.h, snapCpY[h - 2])
                            if h == 0:
                                v.drain()
                            for st in range(2):
                                v.scalar_tensor_tensor(out=segT[:, hp, st, :], in0=maskst[:, st, :],
                                                       scalar=dacum_tm[:, st, h:h + 1],
                                                       in1=cpA[:, hp, 0:CH],
                                                       op0=ALU.subtract, op1=ALU.add)
                            for st in range(2):
                                li = v.tensor_scalar(out=xdt[:, hp, st, :],
                                                     in0=xch[:, cp, st, h * HD:(h + 1) * HD],
                                                     scalar1=dt_tm[:, 2 * c + st, h:h + 1],
                                                     scalar2=None, op0=ALU.mult)
                            dv.inc(li)
                        blk.vector(c_dv1)
                        snapCh1[h] = dv.n

                        def c_ac1(s, c=c, h=h, hp=hp):
                            s.wait_ge(dv.h, snapCh1[h])
                            if h >= 2:
                                s.wait_ge(dw.h, snapCh2[h - 2])
                                s.wait_ge(gp.h, snapCgph[h - 2])
                            for st in range(2):
                                s.activation(out=xdd[:, hp, st, :],
                                             in_=xch[:, cp, st, h * HD:(h + 1) * HD],
                                             func=AF.Copy, scale=dbc[:, h:h + 1], bias=0.0)
                            for st in range(2):
                                li = s.activation(out=segT[:, hp, st, :], in_=segT[:, hp, st, :],
                                                  func=AF.Exp)
                            ac.inc(li)
                        blk.scalar(c_ac1)
                        snapCeseg[h] = ac.n

                        def c_dv2(v, c=c, h=h, hp=hp):
                            v.wait_ge(ac.h, snapCeseg[h])
                            if h >= 2:
                                v.wait_ge(pe.h, snapCpY[h - 2])
                            for st in range(2):
                                v.tensor_tensor(out=mts[:, hp, st, :], in0=segT[:, hp, st, :],
                                                in1=cpsS[:, st, :], op=ALU.mult)
                            for st in range(2):
                                li = v.tensor_scalar(out=xdec[:, hp, st, :], in0=xdt[:, hp, st, :],
                                                     scalar1=segT[:, hp, st, CH - 1:CH], scalar2=None,
                                                     op0=ALU.mult)
                            dw.inc(li)
                        blk.vector(c_dv2)
                        snapCh2[h] = dw.n

                        def c_pe2(t, c=c, h=h, hp=hp):
                            t.wait_ge(dw.h, snapCh2[h])
                            if h >= 2:
                                t.wait_ge(dv.h, snapCh3[h - 2])
                            t.matmul(cpY[:, hp, 0:HD], mts[:, hp, 0, 0:128], xdt[:, hp, 0, :],
                                     start=True, stop=True)
                            t.matmul(cpY[:, hp, HD:2 * HD], mts[:, hp, 0, 128:CH], xdt[:, hp, 0, :],
                                     start=True, stop=False)
                            t.matmul(cpY[:, hp, HD:2 * HD], mts[:, hp, 1, 128:CH], xdt[:, hp, 1, :],
                                     start=False, stop=True)
                            t.matmul(cpY[:, hp, 2 * HD:3 * HD], c_fm[:, c * CH:c * CH + 128],
                                     s_bf[:, h * HD:(h + 1) * HD], start=True, stop=True)
                            t.matmul(cpY[:, hp, 3 * HD:4 * HD], c_fm[:, c * CH + 128:(c + 1) * CH],
                                     s_bf[:, h * HD:(h + 1) * HD], start=True, stop=True)
                            t.matmul(cpA[:, hp, CH:CH + HD], b_tm[:, 2 * c, :], xdec[:, hp, 0, :],
                                     start=True, stop=False)
                            pe.inc(t.matmul(cpA[:, hp, CH:CH + HD], b_tm[:, 2 * c + 1, :], xdec[:, hp, 1, :],
                                            start=False, stop=True))
                        blk.tensor(c_pe2)
                        snapCpY[h] = pe.n

                        def c_dv3(v, c=c, h=h, hp=hp):
                            v.wait_ge(pe.h, snapCpY[h])
                            if h >= 2:
                                v.wait_ge(gp.h, snapCgph[h - 2])
                            for st in range(2):
                                v.tensor_copy(out=ydc[:, hp, st, :], in_=cpY[:, hp, st * HD:(st + 1) * HD])
                            for st in range(2):
                                v.scalar_tensor_tensor(out=y1c[:, hp, st, :], in0=cpY[:, hp, (2 + st) * HD:(3 + st) * HD],
                                                       scalar=expda_tm[:, st, h:h + 1],
                                                       in1=ydc[:, hp, st, :],
                                                       op0=ALU.mult, op1=ALU.add)
                            v.scalar_tensor_tensor(out=s_all[:, h * HD:(h + 1) * HD],
                                                   in0=s_all[:, h * HD:(h + 1) * HD],
                                                   scalar=ecol[:, h:h + 1], in1=cpA[:, hp, CH:CH + HD],
                                                   op0=ALU.mult, op1=ALU.add)
                            dv.inc(v.tensor_copy(out=s_bf[:, h * HD:(h + 1) * HD],
                                                 in_=s_all[:, h * HD:(h + 1) * HD]))
                        blk.vector(c_dv3)
                        snapCh3[h] = dv.n

                        def c_gp(g, c=c, h=h, hp=hp):
                            g.wait_ge(dv.h, snapCh3[h])
                            g.wait_ge(ac.h, snapCeseg[h])
                            if h == 0 and c >= 1:
                                g.wait_ge(dv.h, snapChgmul[c - 1])
                            for st in range(2):
                                li = g.tensor_tensor(out=ych[:, st, h * HD:(h + 1) * HD],
                                                     in0=xdd[:, hp, st, :],
                                                     in1=y1c[:, hp, st, :], op=ALU.add)
                            gp.inc(li)
                        blk.gpsimd(c_gp)
                        snapCgph[h] = gp.n
                    snapCh2last[c] = snapCh2[HEADS - 1]
                    snapCgpend[c] = gp.n
                    snapCsbf[c] = snapCh3[HEADS - 1]
                    snapCecl[c] = snapCecl_pre
                    snapCcprT[c] = snapCpR[HEADS - 1]

                    def c_hg(v, c=c, cp=cp):
                        v.wait_ge(gp.h, snapCgpend[c])
                        for st in range(2):
                            li = v.tensor_tensor(out=ych[:, st, :], in0=ych[:, st, :],
                                                 in1=sgch[:, st, :], op=ALU.mult)
                        dv.inc(li)
                    blk.vector(c_hg)
                    snapChgmul[c] = dv.n

                    def c_hga(s, c=c):
                        s.wait_ge(dv.h, snapChgmul[c])
                        for st in range(2):
                            s.activation(out=sqs[:], in_=ych[:, st, :], func=AF.Square,
                                         accum_out=rcol[:, st:st + 1])
                        for st in range(2):
                            li = s.activation(out=rcol[:, 2 + st:3 + st], in_=rcol[:, st:st + 1],
                                              func=AF.Sqrt, scale=miscc[:, 3:4], bias=miscc[:, 1:2])
                        ac.inc(li)
                    blk.scalar(c_hga)
                    snapChga = ac.n

                    def c_hg2(v, c=c, snapChga=snapChga):
                        v.wait_ge(ac.h, snapChga)
                        for st in range(2):
                            v.reciprocal(out=rcol[:, 4 + st:5 + st], in_=rcol[:, 2 + st:3 + st])
                        v.drain()
                        for st in range(2):
                            li = v.tensor_scalar(out=sgch[:, st, :], in0=ych[:, st, :],
                                                 scalar1=rcol[:, 4 + st:5 + st], scalar2=None,
                                                 op0=ALU.mult)
                        dv.inc(li)
                    blk.vector(c_hg2)
                    snapChgn = dv.n

                    snapCTg = {}
                    for st in range(2):
                        for grp in range(4):
                            G = cgrp_ctr[0]
                            cgrp_ctr[0] += 1
                            bk = G % 2
                            dw_g2 = snapCgc.get(G - 2)

                            def c_pet2(t, c=c, st=st, grp=grp, bk=bk, dw_g2=dw_g2, snapChgn=snapChgn):
                                if grp == 0:
                                    if st == 0:
                                        t.wait_ge(dv.h, snapChgn)
                                    if last_hstage_dma[0] is not None:
                                        t.wait_ge(qs.h, last_hstage_dma[0])
                                if dw_g2 is not None:
                                    t.wait_ge(dw.h, dw_g2)
                                for kk in range(8):
                                    ft = grp * 8 + kk
                                    sl = cptb[:, bk, kk * 128:kk * 128 + 128]
                                    pe.inc(t.transpose(sl, sgch[:, st, ft * 128:(ft + 1) * 128],
                                                       identb[:]))
                            blk.tensor(c_pet2)
                            snapCgt[G] = pe.n

                            def c_dvc2(v, c=c, st=st, grp=grp, bk=bk, G=G):
                                v.wait_ge(pe.h, snapCgt[G])
                                for kk in range(8):
                                    ft = grp * 8 + kk
                                    dw.inc(v.tensor_scalar(
                                        out=hstage[:, ft, :],
                                        in0=cptb[:, bk, kk * 128:kk * 128 + 128],
                                        scalar1=normw[:, ft:ft + 1], scalar2=None, op0=ALU.mult))
                            blk.vector(c_dvc2)
                            snapCgc[G] = dw.n
                        if st == 1:
                            snapChgT[c] = pe.n
                        snapChst = dw.n

                        def c_sy2(s, c=c, st=st, snapChst=snapChst):
                            s.wait_ge(dw.h, snapChst)
                            t0 = c * CH + st * 128
                            qs.inc(s.dma_start(out=hgn_d[:, :, t0:t0 + 128].rearrange("ft p t -> p ft t"),
                                               in_=hstage[:]), 16)
                        blk.sync(c_sy2)
                        last_hstage_dma[0] = qs.n
                qs_c_end = qs.n
            for cm in reversed(mres_cm):
                cm.__exit__(None, None, None)

            phase_barrier()
        if upto >= 4:
            hv = hgn_d.rearrange("ft p t -> p ft t")
            with _multi(
                nc.sbuf_tensor("wo1", [128, 32, 1024], BF16),
                nc.sbuf_tensor("hgt1", [128, 2, 32, 128], BF16),
                nc.sbuf_tensor("xr1", [128, 2, 1024], F32),
                nc.sbuf_tensor("o1t", [128, 2, 1024], F32),
                nc.psum_tensor("p1ps", [128, 2, 2, 512], F32),
            ) as (wo1, hgt1, xr1, o1t, p1ps):
                def p1_w(g):
                    g.wait_ge(qs.h, qs_c_end)
                    for gq in range(4):
                        qw.inc(g.dma_start(
                            out=wo1[:, gq * 8:(gq + 1) * 8, :],
                            in_=w_out_t[gq * 8:(gq + 1) * 8, :, 0:1024].rearrange("ft p d -> p ft d")), 16)
                blk.gpsimd(p1_w)
                qw_p1 = qw.n

                qa_p1 = qa.n
                snapP1mm = [None] * NTT
                snapP1ev = [None] * NTT
                snapP1st = [None] * NTT
                for i in range(NTT):
                    p = i % 2

                    def p1_sy(s, i=i, p=p):
                        if i == 0:
                            s.wait_ge(qs.h, qs_c_end)
                        if i >= 2:
                            s.wait_ge(pe.h, snapP1mm[i - 2])
                            s.wait_ge(dv.h, snapP1ev[i - 2])
                            s.wait_ge(qs.h, snapP1st[i - 2])
                        qa.inc(s.dma_start(out=hgt1[:, p, :, :], in_=hv[:, :, i * 128:(i + 1) * 128]), 16)
                        qa.inc(s.dma_start(out=xr1[:, p, :],
                                           in_=x_in[i * 128:(i + 1) * 128, 0:1024]), 16)
                    blk.sync(p1_sy)

                    def p1_pe(t, i=i, p=p):
                        t.wait_ge(qa.h, qa_p1 + 32 * i + 16)
                        if i == 0:
                            t.wait_ge(qw.h, qw_p1)
                        if i >= 2:
                            t.wait_ge(dv.h, snapP1ev[i - 2])
                        for ft in range(32):
                            for nd in range(2):
                                li = t.matmul(p1ps[:, p, nd, :], hgt1[:, p, ft, :],
                                              wo1[:, ft, nd * 512:(nd + 1) * 512],
                                              start=(ft == 0), stop=(ft == 31),
                                              skip_group_check=True)
                        pe.inc(li)
                    blk.tensor(p1_pe)
                    snapP1mm[i] = pe.n

                    def p1_dv(v, i=i, p=p):
                        v.wait_ge(pe.h, snapP1mm[i])
                        v.wait_ge(qa.h, qa_p1 + 32 * (i + 1))
                        for nd in range(2):
                            li = v.tensor_tensor(out=o1t[:, p, nd * 512:(nd + 1) * 512],
                                                 in0=p1ps[:, p, nd, :],
                                                 in1=xr1[:, p, nd * 512:(nd + 1) * 512], op=ALU.add)
                        dv.inc(li)
                    blk.vector(p1_dv)
                    snapP1ev[i] = dv.n

                    def p1_st(s, i=i, p=p):
                        s.wait_ge(dv.h, snapP1ev[i])
                        qs.inc(s.dma_start(out=x2_d[i * 128:(i + 1) * 128, 0:1024], in_=o1t[:, p, :]), 16)
                    blk.sync(p1_st)
                    snapP1st[i] = qs.n
                qs_p1_end = qs.n

            phase_barrier()
        if upto >= 5:
            with _multi(
                nc.sbuf_tensor("wo2", [128, 32, 1024], BF16),
                nc.sbuf_tensor("hgt2", [128, 2, 32, 128], BF16),
                nc.sbuf_tensor("xr2", [128, 2, 1024], F32),
                nc.sbuf_tensor("x2t", [128, 2, DM], F32),
                nc.sbuf_tensor("ln2f", [128, DM], F32),
                nc.sbuf_tensor("ln2h", [128, 2, DM], BF16),
                nc.sbuf_tensor("ln2lo", [128, 2, DM], BF16),
                nc.sbuf_tensor("lnT", [128, 2, NKT, 128], BF16),
                nc.sbuf_tensor("dcl", [128, 8], F32),
                nc.sbuf_tensor("lgT", [32, 128], F32),
                nc.psum_tensor("p2ps", [128, 2, 2, 512], F32),
                nc.psum_tensor("dptb", [128, 2, 1024], BF16),
                nc.psum_tensor("dplg", [32, 128], F32),
                nc.psum_tensor("dplt", [128, 32], F32),
            ) as (wo2, hgt2, xr2, x2t, ln2f, ln2h, ln2lo, lnT, dcl, lgT, p2ps, dptb, dplg, dplt):
                def p2_w(g):
                    g.wait_ge(qs.h, qs_p1_end)
                    for gq in range(4):
                        qw.inc(g.dma_start(
                            out=wo2[:, gq * 8:(gq + 1) * 8, :],
                            in_=w_out_t[gq * 8:(gq + 1) * 8, :, 1024:2048].rearrange(
                                "ft p d -> p ft d")), 16)
                blk.gpsimd(p2_w)
                qw_p2 = qw.n

                qa_p2 = qa.n
                snapP2mm = [None] * NTT
                snapP2x2 = [None] * NTT
                snapP2b = [None] * NTT
                snapP2st = [None] * NTT
                snapP2T = [None] * NTT
                snapP2r = [None] * NTT
                snapP2lg = [None] * NTT
                snapP2t2 = [None] * NTT
                snapP2l5 = [None] * NTT
                snapP2gt = {}
                snapP2gc = {}
                for i in range(NTT):
                    p = i % 2

                    def p2_sy(s, i=i, p=p):
                        if i == 0:
                            s.wait_ge(qs.h, qs_p1_end)
                        if i >= 2:
                            s.wait_ge(pe.h, snapP2mm[i - 2])
                            s.wait_ge(dv.h, snapP2x2[i - 2])
                            s.wait_ge(qs.h, snapP2st[i - 2])
                        qa.inc(s.dma_start(out=hgt2[:, p, :, :], in_=hv[:, :, i * 128:(i + 1) * 128]), 16)
                        qa.inc(s.dma_start(out=xr2[:, p, :],
                                           in_=x_in[i * 128:(i + 1) * 128, 1024:2048]), 16)
                        qa.inc(s.dma_start(out=x2t[:, p, 0:1024],
                                           in_=x2_d[i * 128:(i + 1) * 128, 0:1024]), 16)
                    blk.sync(p2_sy)

                    def p2_pe(t, i=i, p=p):
                        t.wait_ge(qa.h, qa_p2 + 48 * i + 16)
                        if i == 0:
                            t.wait_ge(qw.h, qw_p2)
                        if i >= 2:
                            t.wait_ge(dv.h, snapP2x2[i - 2])
                        for ft in range(32):
                            for nd in range(2):
                                li = t.matmul(p2ps[:, p, nd, :], hgt2[:, p, ft, :],
                                              wo2[:, ft, nd * 512:(nd + 1) * 512],
                                              start=(ft == 0), stop=(ft == 31),
                                              skip_group_check=True)
                        pe.inc(li)
                    blk.tensor(p2_pe)
                    snapP2mm[i] = pe.n

                    def p2_dv1(v, i=i, p=p):
                        v.wait_ge(pe.h, snapP2mm[i])
                        v.wait_ge(qa.h, qa_p2 + 48 * (i + 1))
                        for nd in range(2):
                            v.tensor_tensor(out=x2t[:, p, 1024 + nd * 512:1024 + (nd + 1) * 512],
                                            in0=p2ps[:, p, nd, :],
                                            in1=xr2[:, p, nd * 512:(nd + 1) * 512], op=ALU.add)
                        v.tensor_reduce(out=dcl[:, 0:1], in_=x2t[:, p, :], axis=AX.X, op=ALU.add)
                        v.drain()
                        v.tensor_scalar(out=dcl[:, 1:2], in0=dcl[:, 0:1], scalar1=1.0 / DM,
                                        scalar2=None, op0=ALU.mult)
                        v.drain()
                        dv.inc(v.tensor_scalar(out=ln2f[:], in0=x2t[:, p, :], scalar1=dcl[:, 1:2],
                                               scalar2=None, op0=ALU.subtract))
                    blk.vector(p2_dv1)
                    snapP2x2[i] = dv.n

                    def p2_ac(s, i=i, p=p):
                        s.wait_ge(dv.h, snapP2x2[i])
                        if i >= 2:
                            s.wait_ge(pe.h, snapP2T[i - 2])
                        s.activation(out=ln2lo[:, p, :], in_=ln2f[:], func=AF.Square,
                                     accum_out=dcl[:, 2:3])
                        ac.inc(s.activation(out=dcl[:, 3:4], in_=dcl[:, 2:3], func=AF.Sqrt,
                                            scale=miscc[:, 2:3], bias=miscc[:, 1:2]))
                    blk.scalar(p2_ac)
                    snapP2ac = ac.n

                    def p2_dv2(v, i=i, p=p, snapP2ac=snapP2ac):
                        v.wait_ge(ac.h, snapP2ac)
                        if i >= 2:
                            v.wait_ge(pe.h, snapP2T[i - 2])
                        v.reciprocal(out=dcl[:, 4:5], in_=dcl[:, 3:4])
                        v.drain()
                        v.tensor_scalar(out=ln2f[:], in0=ln2f[:], scalar1=dcl[:, 4:5], scalar2=None,
                                        op0=ALU.mult)
                        v.tensor_copy(out=ln2h[:, p, :], in_=ln2f[:])
                        dv.inc(v.tensor_tensor(out=ln2lo[:, p, :], in0=ln2f[:], in1=ln2h[:, p, :],
                                               op=ALU.subtract))
                    blk.vector(p2_dv2)
                    snapP2b[i] = dv.n

                    def p2_st(s, i=i, p=p):
                        s.wait_ge(dv.h, snapP2b[i])
                        qs.inc(s.dma_start(out=x2_d[i * 128:(i + 1) * 128, 1024:2048],
                                           in_=x2t[:, p, 1024:2048]), 16)
                        qs.inc(s.dma_start(out=ln2_d[i * 128:(i + 1) * 128, :], in_=ln2h[:, p, :]), 16)
                    blk.sync(p2_st)
                    snapP2st[i] = qs.n

                    for grp in range(4):
                        G = i * 4 + grp
                        bk = G % 2
                        dw_g2 = snapP2gc.get(G - 2)

                        def p2_pet(t, i=i, p=p, grp=grp, bk=bk, dw_g2=dw_g2):
                            if grp == 0:
                                t.wait_ge(dv.h, snapP2b[i])
                            if dw_g2 is not None:
                                t.wait_ge(dw.h, dw_g2)
                            for kk in range(8):
                                gg = grp * 8 + kk
                                kt, half = gg // 2, gg % 2
                                src = ln2h if half == 0 else ln2lo
                                sl = dptb[:, bk, kk * 128:kk * 128 + 128]
                                pe.inc(t.transpose(sl, src[:, p, kt * 128:(kt + 1) * 128], identb[:]))
                        blk.tensor(p2_pet)
                        snapP2gt[G] = pe.n

                        def p2_dv3(v, i=i, p=p, grp=grp, bk=bk, G=G):
                            if grp == 0 and i >= 1:
                                v.wait_ge(pe.h, snapP2r[i - 1])
                            v.wait_ge(pe.h, snapP2gt[G])
                            for kk in range(8):
                                gg = grp * 8 + kk
                                kt, half = gg // 2, gg % 2
                                dw.inc(v.tensor_copy(out=lnT[:, half, kt, :],
                                                     in_=dptb[:, bk, kk * 128:kk * 128 + 128]))
                        blk.vector(p2_dv3)
                        snapP2gc[G] = dw.n
                    snapP2T[i] = pe.n
                    snapP2c = dw.n

                    def p2_per(t, i=i, snapP2c=snapP2c):
                        t.wait_ge(dw.h, snapP2c)
                        if i >= 1:
                            t.wait_ge(dv.h, snapP2lg[i - 1])
                        first = True
                        for kt in range(NKT):
                            t.matmul(dplg[:], wg_hi[:, kt, :], lnT[:, 0, kt, :], start=first, stop=False)
                            first = False
                            t.matmul(dplg[:], wg_lo[:, kt, :], lnT[:, 0, kt, :], start=False, stop=False)
                            li = t.matmul(dplg[:], wg_hi[:, kt, :], lnT[:, 1, kt, :], start=False,
                                          stop=(kt == NKT - 1))
                        pe.inc(li)
                    blk.tensor(p2_per)
                    snapP2r[i] = pe.n

                    def p2_dv4(v, i=i):
                        v.wait_ge(pe.h, snapP2r[i])
                        if i >= 1:
                            v.wait_ge(pe.h, snapP2t2[i - 1])
                        dv.inc(v.tensor_copy(out=lgT[:], in_=dplg[:]))
                    blk.vector(p2_dv4)
                    snapP2lg[i] = dv.n

                    def p2_pe2(t, i=i):
                        t.wait_ge(dv.h, snapP2lg[i])
                        if i >= 1:
                            t.wait_ge(dv.h, snapP2l5[i - 1])
                        pe.inc(t.transpose(dplt[:], lgT[:], identf[0:32, 0:32]))
                    blk.tensor(p2_pe2)
                    snapP2t2[i] = pe.n

                    def p2_dv5(v, i=i):
                        v.wait_ge(pe.h, snapP2t2[i])
                        dv.inc(v.tensor_copy(out=logits_tm[:, i, :], in_=dplt[:]))
                    blk.vector(p2_dv5)
                    snapP2l5[i] = dv.n
                qs_d1_end = qs.n
                snapD1_dv = dv.n

            phase_barrier()
        if upto >= 6:
            with _multi(
                nc.sbuf_tensor("probs", [128, NE], F32),
                nc.sbuf_tensor("vals8", [128, TOPK], F32),
                nc.sbuf_tensor("idxu", [128, TOPK], U32),
                nc.sbuf_tensor("r2c", [128, 16], F32),
                nc.sbuf_tensor("ohp", [128, 4, NE], F32),
                nc.sbuf_tensor("ohj", [128, NE], F32),
                nc.sbuf_tensor("ohjT", [32, 128], F32),
                nc.sbuf_tensor("scj", [128, 4], F32),
                nc.sbuf_tensor("scr32", [128, NE], F32),
                nc.sbuf_tensor("bstc", [128, 8], F32),
                nc.sbuf_tensor("carry", [1, NE], F32),
                nc.sbuf_tensor("ohsi", [128, NE], F32),
                nc.sbuf_tensor("poscol", [128, 8], F32),
                nc.psum_tensor("rpt", [32, 128], F32),
                nc.psum_tensor("rps1", [128, NE], F32),
                nc.psum_tensor("rppre", [128, NE], F32),
                nc.psum_tensor("rpcar", [1, NE], F32),
            ) as (probs, vals8, idxu, r2c, ohp, ohj, ohjT, scj, scr32, bstc, carry, ohsi, poscol, rpt, rps1, rppre, rpcar):
                snap_j_pe = [None]
                snap_j_dv = [None]
                for i in range(NTT):
                    def d2_dv1(v, i=i):
                        if i == 0:
                            v.wait_ge(dv.h, snapD1_dv)
                        v.tensor_reduce(out=r2c[:, 0:1], in_=logits_tm[:, i, :], axis=AX.X, op=ALU.max)
                        v.drain()
                        dv.inc(v.tensor_scalar(out=probs[:], in0=logits_tm[:, i, :],
                                               scalar1=r2c[:, 0:1], scalar2=None, op0=ALU.subtract))
                    blk.vector(d2_dv1)
                    s1 = dv.n

                    def d2_ac(s, i=i, s1=s1):
                        s.wait_ge(dv.h, s1)
                        ac.inc(s.activation(out=probs[:], in_=probs[:], func=AF.Exp,
                                            accum_out=r2c[:, 1:2]))
                    blk.scalar(d2_ac)
                    s2 = ac.n

                    def d2_dv2(v, i=i, s2=s2):
                        v.wait_ge(ac.h, s2)
                        v.reciprocal(out=r2c[:, 2:3], in_=r2c[:, 1:2])
                        v.drain()
                        v.tensor_scalar(out=probs[:], in0=probs[:], scalar1=r2c[:, 2:3], scalar2=None,
                                        op0=ALU.mult)
                        v.drain()
                        v.max(out=vals8[:], in_=probs[:])
                        v.drain()
                        v.max_index(out=idxu[:], in_max=vals8[:], in_values=probs[:])
                        v.drain()
                        v.tensor_copy(out=idx8_all[:, i, :], in_=idxu[:])
                        v.tensor_reduce(out=r2c[:, 3:4], in_=vals8[:], axis=AX.X, op=ALU.add)
                        v.drain()
                        v.tensor_scalar(out=r2c[:, 4:5], in0=r2c[:, 3:4], scalar1=1e-12, scalar2=None,
                                        op0=ALU.max)
                        v.drain()
                        v.reciprocal(out=r2c[:, 5:6], in_=r2c[:, 4:5])
                        v.drain()
                        v.tensor_scalar(out=w8_all[:, i, :], in0=vals8[:], scalar1=r2c[:, 5:6],
                                        scalar2=None, op0=ALU.mult)
                        for k in range(4):
                            li = v.tensor_scalar(out=ohp[:, k, :], in0=iota32[:],
                                                 scalar1=idx8_all[:, i, k:k + 1], scalar2=None,
                                                 op0=ALU.is_equal)
                        dv.inc(li)
                    blk.vector(d2_dv2)
                    s3 = dv.n

                    for j in range(4):
                        def d2_dvj(v, i=i, j=j, s3=s3):
                            if j == 0:
                                v.wait_ge(dv.h, s3)
                            if snap_j_pe[0] is not None:
                                v.wait_ge(pe.h, snap_j_pe[0])
                            dw.inc(v.tensor_scalar(out=ohj[:], in0=iota32[:],
                                                   scalar1=idx8_all[:, i, SSEL + j:SSEL + j + 1],
                                                   scalar2=None, op0=ALU.is_equal))
                        blk.vector(d2_dvj)
                        s4 = dw.n

                        def d2_pej(t, i=i, j=j, s4=s4):
                            t.wait_ge(dw.h, s4)
                            if snap_j_dv[0] is not None:
                                t.wait_ge(dv.h, snap_j_dv[0])
                            pe.inc(t.transpose(rpt[:], ohj[:], identf[:]))
                        blk.tensor(d2_pej)
                        s5 = pe.n

                        def d2_dvj3(v, i=i, j=j, s5=s5):
                            v.wait_ge(pe.h, s5)
                            dw.inc(v.tensor_copy(out=ohjT[:], in_=rpt[:]))
                        blk.vector(d2_dvj3)
                        s6 = dw.n

                        def d2_pej3(t, i=i, j=j, s6=s6):
                            t.wait_ge(dw.h, s6)
                            pe.inc(t.matmul(rps1[:], ohjT[:], simd[:], start=True, stop=True))
                        blk.tensor(d2_pej3)
                        snap_j_pe[0] = pe.n
                        s7 = pe.n

                        def d2_dvj4(v, i=i, j=j, s7=s7):
                            v.wait_ge(pe.h, s7)
                            for k in range(4):
                                v.tensor_tensor(out=scr32[:], in0=rps1[:], in1=ohp[:, k, :],
                                                op=ALU.mult)
                                v.drain()
                                v.tensor_reduce(out=scj[:, k:k + 1], in_=scr32[:], axis=AX.X,
                                                op=ALU.add)
                            v.drain()
                            v.tensor_tensor(out=bstc[:, 0:1], in0=scj[:, 0:1], in1=scj[:, 1:2],
                                            op=ALU.max)
                            v.tensor_tensor(out=bstc[:, 1:2], in0=scj[:, 2:3], in1=scj[:, 3:4],
                                            op=ALU.max)
                            v.drain()
                            v.tensor_tensor(out=bstc[:, 0:1], in0=bstc[:, 0:1], in1=bstc[:, 1:2],
                                            op=ALU.max)
                            v.drain()
                            v.tensor_scalar(out=bstc[:, 2:3], in0=scj[:, 0:1], scalar1=bstc[:, 0:1],
                                            scalar2=None, op0=ALU.is_ge)
                            v.drain()
                            v.tensor_scalar(out=bstc[:, 4:5], in0=bstc[:, 2:3], scalar1=-1.0,
                                            scalar2=1.0, op0=ALU.mult, op1=ALU.add)
                            v.tensor_tensor(out=bstc[:, 6:7], in0=idx8_all[:, i, 0:1], in1=bstc[:, 2:3],
                                            op=ALU.mult)
                            for k in range(1, 4):
                                v.drain()
                                v.tensor_scalar(out=bstc[:, 2:3], in0=scj[:, k:k + 1],
                                                scalar1=bstc[:, 0:1], scalar2=None, op0=ALU.is_ge)
                                v.drain()
                                v.tensor_tensor(out=bstc[:, 3:4], in0=bstc[:, 2:3], in1=bstc[:, 4:5],
                                                op=ALU.mult)
                                v.drain()
                                v.scalar_tensor_tensor(out=bstc[:, 6:7], in0=idx8_all[:, i, k:k + 1],
                                                       scalar=bstc[:, 3:4], in1=bstc[:, 6:7],
                                                       op0=ALU.mult, op1=ALU.add)
                                v.tensor_scalar(out=bstc[:, 5:6], in0=bstc[:, 3:4], scalar1=-1.0,
                                                scalar2=1.0, op0=ALU.mult, op1=ALU.add)
                                v.drain()
                                v.tensor_tensor(out=bstc[:, 4:5], in0=bstc[:, 4:5], in1=bstc[:, 5:6],
                                                op=ALU.mult)
                            v.drain()
                            v.tensor_scalar(out=bstc[:, 7:8], in0=bstc[:, 0:1], scalar1=THR,
                                            scalar2=None, op0=ALU.is_lt)
                            v.drain()
                            v.tensor_tensor(out=scr32[:, 0:1], in0=idx8_all[:, i, SSEL + j:SSEL + j + 1],
                                            in1=bstc[:, 7:8], op=ALU.mult)
                            v.tensor_scalar(out=bstc[:, 7:8], in0=bstc[:, 7:8], scalar1=-1.0,
                                            scalar2=1.0, op0=ALU.mult, op1=ALU.add)
                            v.drain()
                            v.tensor_tensor(out=scr32[:, 1:2], in0=bstc[:, 6:7], in1=bstc[:, 7:8],
                                            op=ALU.mult)
                            v.drain()
                            dv.inc(v.tensor_tensor(out=idx8_all[:, i, SSEL + j:SSEL + j + 1],
                                                   in0=scr32[:, 0:1], in1=scr32[:, 1:2], op=ALU.add))
                        blk.vector(d2_dvj4)
                        snap_j_dv[0] = dv.n
                snapD2route = dv.n

                def d2_ci(v):
                    v.wait_ge(dv.h, snapD2route)
                    dv.inc(v.memset(carry[:], 0.0))
                blk.vector(d2_ci)
                snapPcar = dv.n

                first_carry = True
                snapPpe = None
                for s_ in range(TOPK):
                    for i in range(NTT):
                        def p_dv1(v, s_=s_, i=i, snapPpe=snapPpe):
                            if snapPpe is not None:
                                v.wait_ge(pe.h, snapPpe)
                            dw.inc(v.tensor_scalar(out=ohsi[:], in0=iota32[:],
                                                   scalar1=idx8_all[:, i, s_:s_ + 1], scalar2=None,
                                                   op0=ALU.is_equal))
                        blk.vector(p_dv1)
                        sP1 = dw.n

                        def p_pe(t, s_=s_, i=i, sP1=sP1, fc=first_carry, snapPcar=snapPcar):
                            t.wait_ge(dw.h, sP1)
                            t.wait_ge(dv.h, snapPcar)
                            t.matmul(rppre[:], ustrict[:], ohsi[:], start=True, stop=False,
                                     skip_group_check=True)
                            t.matmul(rppre[:], ones_r1[:], carry[:], start=False, stop=True,
                                     skip_group_check=True)
                            pe.inc(t.matmul(rpcar[:], ones_c[:], ohsi[:], start=fc, stop=True,
                                            skip_group_check=True))
                        blk.tensor(p_pe)
                        first_carry = False
                        snapPpe = pe.n
                        sP2 = pe.n

                        def p_dv2(v, s_=s_, i=i, sP2=sP2):
                            v.wait_ge(pe.h, sP2)
                            v.tensor_tensor(out=scr32[:], in0=rppre[:], in1=ohsi[:], op=ALU.mult)
                            v.tensor_copy(out=carry[:], in_=rpcar[:])
                            v.tensor_reduce(out=poscol[:, 0:1], in_=scr32[:], axis=AX.X, op=ALU.add)
                            v.drain()
                            v.tensor_scalar(out=poscol[:, 1:2], in0=idx8_all[:, i, s_:s_ + 1],
                                            scalar1=float(CAP), scalar2=poscol[:, 0:1],
                                            op0=ALU.mult, op1=ALU.add)
                            v.tensor_scalar(out=poscol[:, 2:3], in0=poscol[:, 0:1], scalar1=float(CAP),
                                            scalar2=None, op0=ALU.is_ge)
                            v.drain()
                            v.tensor_scalar(out=poscol[:, 3:4], in0=poscol[:, 2:3], scalar1=-1.0,
                                            scalar2=1.0, op0=ALU.mult, op1=ALU.add)
                            v.tensor_tensor(out=poscol[:, 5:6], in0=trash[:], in1=poscol[:, 2:3],
                                            op=ALU.mult)
                            v.drain()
                            v.tensor_tensor(out=poscol[:, 4:5], in0=poscol[:, 1:2], in1=poscol[:, 3:4],
                                            op=ALU.mult)
                            v.drain()
                            v.tensor_tensor(out=poscol[:, 6:7], in0=poscol[:, 4:5], in1=poscol[:, 5:6],
                                            op=ALU.add)
                            v.drain()
                            dv.inc(v.tensor_copy(out=posu_all[:, s_ * NTT + i:s_ * NTT + i + 1],
                                                 in_=poscol[:, 6:7]))
                        blk.vector(p_dv2)
                        snapPcar = dv.n
                snapD2pos = dv.n

            phase_barrier()
        if upto >= 7:
            with nc.sbuf_tensor("lnrow", [128, 2, DM], BF16) as lnrow:
                qa_d3 = qa.n
                snapD3sc = [None] * NTT
                for i in range(NTT):
                    def d3_sy(s, i=i):
                        if i == 0:
                            s.wait_ge(dv.h, snapD2pos)
                            s.wait_ge(qs.h, qs_d1_end)
                        if i >= 2:
                            s.wait_ge(qi.h, snapD3sc[i - 2])
                        qa.inc(s.dma_start(out=lnrow[:, i % 2, :],
                                           in_=ln2_d[i * 128:(i + 1) * 128, :]), 16)
                    blk.sync(d3_sy)

                    def d3_gp(g, i=i):
                        g.wait_ge(qa.h, qa_d3 + 16 * (i + 1))
                        if i == 0:
                            g.wait_ge(dv.h, snapD2pos)
                        for s_ in range(TOPK):
                            qi.inc(g.indirect_dma_start(
                                out=xdisp_d[:],
                                out_offset=IOA(ap=posu_all[:, s_ * NTT + i:s_ * NTT + i + 1], axis=0),
                                in_=lnrow[:, i % 2, :], in_offset=None), 16)
                    blk.gpsimd(d3_gp)
                    snapD3sc[i] = qi.n
                qi_d3_end = qi.n

            phase_barrier()
        if upto >= 8:
            with _multi(
                nc.sbuf_tensor("xg", [128, CAP // 128, DM], BF16),
                nc.sbuf_tensor("xgT", [128, NKT, CAP], BF16),
                nc.sbuf_tensor("wgu", [128, 2, NKT, 128], BF16),
                nc.sbuf_tensor("wdt", [128, 6, DM], BF16),
                nc.sbuf_tensor("sg6", [128, 6, CAP], F32),
                nc.sbuf_tensor("htil", [128, 6, CAP], BF16),
                nc.sbuf_tensor("ytile", [128, 2, DM], BF16),
                nc.psum_tensor("gps", [128, 2, 512], F32),
                nc.psum_tensor("dnps", [128, 4, 512], F32),
                nc.psum_tensor("gptb", [128, 2, 1024], BF16),
            ) as (xg, xgT, wgu, wdt, sg6, htil, ytile, gps, dnps, gptb):
                def d4_z(v):
                    dv.inc(v.memset(ytile[:, 0, :], 0.0))
                blk.vector(d4_z)
                sZ = dv.n

                def d4_zs(s, sZ=sZ):
                    s.wait_ge(dv.h, sZ)
                    qs.inc(s.dma_start(out=ydisp_d[NDISP:NDISP + 128, :], in_=ytile[:, 0, :]), 16)
                blk.sync(d4_zs)
                snapD4zs = qs.n

                qa_d4 = qa.n
                qwsnapG = {}
                qwsnapD = {}
                snapD4xT = [None] * NE
                snapD4gu = {}
                snapD4guE = [None] * NE
                snapD4dnE = [None] * NE
                snapD4uend = [None] * NE
                snapD4sg = {}
                snapD4xc = [None] * NE
                snapD4y = {}
                snapD4st = {}
                snapD4gt = {}
                snapD4gc = {}
                evD4_before = {}

                def d4_gp0(g):
                    g.wait_ge(qi.h, qi_d3_end)
                    for mt in range(2):
                        qw.inc(g.dma_start(out=wgu[:, mt % 2, :, :],
                                           in_=w_gu_t[0, mt].rearrange("(kt kp) m -> kp kt m",
                                                                       kp=128)), 16)
                        qwsnapG[(0, mt)] = qw.n
                    qw.inc(g.dma_start(out=wdt[:],
                                       in_=w_d_t[0].rearrange("(kt kp) d -> kp kt d", kp=128)), 16)
                    qwsnapD[0] = qw.n
                blk.gpsimd(d4_gp0)

                for e in range(NE):
                    ep = e % 2

                    def d4_syl(s, e=e):
                        if e == 0:
                            s.wait_ge(qi.h, qi_d3_end)
                        else:
                            s.wait_ge(pe.h, snapD4xT[e - 1])
                        iv = xdisp_d[e * CAP:(e + 1) * CAP, :].rearrange(
                            "(rt p) d -> p rt d", p=128)
                        qa.inc(s.dma_start(out=xg[:], in_=iv), 16)
                    blk.sync(d4_syl)

                    for rt in range(CAP // 128):
                        for grp in range(2):
                            G = e * 16 + rt * 2 + grp
                            bk = G % 2
                            dw_g2 = snapD4gc.get(G - 2)

                            def d4_pex(t, e=e, ep=ep, rt=rt, grp=grp, bk=bk, dw_g2=dw_g2):
                                if rt == 0 and grp == 0:
                                    t.wait_ge(qa.h, qa_d4 + 16 * (e + 1))
                                if dw_g2 is not None:
                                    t.wait_ge(dw.h, dw_g2)
                                for kk in range(8):
                                    kt = grp * 8 + kk
                                    sl = gptb[:, bk, kk * 128:kk * 128 + 128]
                                    pe.inc(t.transpose(sl, xg[:, rt, kt * 128:(kt + 1) * 128],
                                                       identb[:]))
                            blk.tensor(d4_pex)
                            snapD4gt[G] = pe.n

                            def d4_dvx(v, e=e, ep=ep, rt=rt, grp=grp, bk=bk, G=G):
                                if rt == 0 and grp == 0 and e >= 1:
                                    v.wait_ge(pe.h, snapD4guE[e - 1])
                                v.wait_ge(pe.h, snapD4gt[G])
                                for kk in range(8):
                                    kt = grp * 8 + kk
                                    dw.inc(v.tensor_copy(
                                        out=xgT[:, kt, rt * 128:(rt + 1) * 128],
                                        in_=gptb[:, bk, kk * 128:kk * 128 + 128]))
                            blk.vector(d4_dvx)
                            snapD4gc[G] = dw.n
                    snapD4xT[e] = pe.n
                    snapD4xc[e] = dw.n

                    evD4_before[e] = ev.n
                    for mt in range(12):
                        def d4_pg(t, e=e, ep=ep, mt=mt):
                            t.wait_ge(qw.h, qwsnapG[(e, mt)])
                            if mt == 0:
                                t.wait_ge(dw.h, snapD4xc[e])
                            if mt >= 1:
                                t.wait_ge(ev.h, evD4_before[e] + 2 * mt)
                            for kt in range(NKT):
                                for nr in range(2):
                                    n0, n1 = (0, 512) if nr == 0 else (512, CAP)
                                    li = t.matmul(gps[:, nr, 0:n1 - n0], wgu[:, (e * 12 + mt) % 2, kt, :],
                                                  xgT[:, kt, n0:n1],
                                                  start=(kt == 0), stop=(kt == NKT - 1),
                                                  skip_group_check=True)
                            pe.inc(li)
                        blk.tensor(d4_pg)
                        snapD4gu[(e, mt)] = pe.n

                        def d4_gpw(g, e=e, mt=mt):
                            g.wait_ge(pe.h, snapD4gu[(e, mt)])
                            if mt + 2 < 12:
                                e2, m2 = e, mt + 2
                            else:
                                e2, m2 = e + 1, mt + 2 - 12
                            if e2 < NE:
                                qw.inc(g.dma_start(out=wgu[:, (e2 * 12 + m2) % 2, :, :],
                                                   in_=w_gu_t[e2, m2].rearrange(
                                                       "(kt kp) m -> kp kt m", kp=128)), 16)
                                qwsnapG[(e2, m2)] = qw.n
                        blk.gpsimd(d4_gpw)

                        if mt < 6:
                            def d4_ev1(s, e=e, mt=mt):
                                s.wait_ge(pe.h, snapD4gu[(e, mt)])
                                if e >= 1 and mt == 0:
                                    s.wait_ge(ev.h, evD4_before[e - 1] + 24)
                                for nr in range(2):
                                    n0, n1 = (0, 512) if nr == 0 else (512, CAP)
                                    i2 = s.activation(out=sg6[:, mt, n0:n1],
                                                      in_=gps[:, nr, 0:n1 - n0], func=AF.Silu)
                                    ev.inc(i2)
                            blk.scalar(d4_ev1)
                            snapD4sg[(e, mt)] = ev.n
                        else:
                            def d4_ev2(v, e=e, mt=mt):
                                v.wait_ge(pe.h, snapD4gu[(e, mt)])
                                v.wait_ge(ev.h, snapD4sg[(e, mt - 6)])
                                if e >= 1 and mt == 6:
                                    v.wait_ge(pe.h, snapD4dnE[e - 1])
                                for nr in range(2):
                                    n0, n1 = (0, 512) if nr == 0 else (512, CAP)
                                    i2 = v.tensor_tensor(out=htil[:, mt - 6, n0:n1],
                                                         in0=gps[:, nr, 0:n1 - n0],
                                                         in1=sg6[:, mt - 6, n0:n1],
                                                         op=ALU.mult)
                                    ev.inc(i2)
                            blk.vector(d4_ev2)
                    snapD4guE[e] = pe.n
                    snapD4uend[e] = ev.n

                    for rt in range(CAP // 128):
                        def d4_pd(t, e=e, ep=ep, rt=rt):
                            if rt == 0:
                                t.wait_ge(ev.h, snapD4uend[e])
                                t.wait_ge(qw.h, qwsnapD[e])
                                if e == 0:
                                    t.wait_ge(qs.h, snapD4zs)
                            if rt >= 1:
                                t.wait_ge(dv.h, snapD4y[(e, rt - 1)])
                            for kt in range(6):
                                for nd in range(4):
                                    li = t.matmul(dnps[:, nd, :], htil[:, kt, rt * 128:(rt + 1) * 128],
                                                  wdt[:, kt, nd * 512:(nd + 1) * 512],
                                                  start=(kt == 0), stop=(kt == 5),
                                                  skip_group_check=True)
                            pe.inc(li)
                        blk.tensor(d4_pd)
                        snapD4dn_rt = pe.n

                        def d4_dy(v, e=e, rt=rt, snapD4dn_rt=snapD4dn_rt):
                            v.wait_ge(pe.h, snapD4dn_rt)
                            rpe = CAP // 128
                            idx2 = e * rpe + rt
                            if idx2 >= 2:
                                pidx = idx2 - 2
                                v.wait_ge(qs.h, snapD4st[(pidx // rpe, pidx % rpe)])
                            for nd in range(4):
                                ci = v.tensor_copy(out=ytile[:, rt % 2, nd * 512:(nd + 1) * 512],
                                                   in_=dnps[:, nd, :])
                            dv.inc(ci)
                        blk.vector(d4_dy)
                        snapD4y[(e, rt)] = dv.n

                        def d4_sy(s, e=e, rt=rt):
                            s.wait_ge(dv.h, snapD4y[(e, rt)])
                            r0 = e * CAP + rt * 128
                            qs.inc(s.dma_start(out=ydisp_d[r0:r0 + 128, :], in_=ytile[:, rt % 2, :]), 16)
                        blk.sync(d4_sy)
                        snapD4st[(e, rt)] = qs.n
                    snapD4dnE[e] = pe.n

                    if e + 1 < NE:
                        def d4_gpd(g, e=e):
                            g.wait_ge(pe.h, snapD4dnE[e])
                            qw.inc(g.dma_start(out=wdt[:],
                                               in_=w_d_t[e + 1].rearrange(
                                                   "(kt kp) d -> kp kt d", kp=128)), 16)
                            qwsnapD[e + 1] = qw.n
                        blk.gpsimd(d4_gpd)
                qs_d4_end = qs.n

            phase_barrier()
        if upto >= 9:
            with _multi(
                nc.sbuf_tensor("x2acc", [128, 2, DM], F32),
                nc.sbuf_tensor("ygt", [128, 4, DM], BF16),
            ) as (x2acc, ygt):
                qa_d5 = qa.n
                qi_d5 = qi.n
                snapD5stt = {}
                snapD5out = {}
                for i in range(NTT):
                    def d5_sy(s, i=i):
                        if i == 0:
                            s.wait_ge(qs.h, qs_d4_end)
                        if i >= 2:
                            s.wait_ge(qs.h, snapD5out[i - 2])
                        qa.inc(s.dma_start(out=x2acc[:, i % 2, :],
                                           in_=x2_d[i * 128:(i + 1) * 128, :]), 16)
                    blk.sync(d5_sy)

                    for s_ in range(TOPK):
                        it = i * TOPK + s_

                        def d5_gp(g, i=i, s_=s_, it=it):
                            if it == 0:
                                g.wait_ge(qs.h, qs_d4_end)
                            if it >= 4:
                                g.wait_ge(dv.h, snapD5stt[it - 4])
                            qi.inc(g.indirect_dma_start(
                                out=ygt[:, it % 4, :], out_offset=None,
                                in_=ydisp_d[:],
                                in_offset=IOA(ap=posu_all[:, s_ * NTT + i:s_ * NTT + i + 1],
                                              axis=0)), 16)
                        blk.gpsimd(d5_gp)

                        def d5_dv(v, i=i, s_=s_, it=it):
                            if s_ == 0:
                                v.wait_ge(qa.h, qa_d5 + 16 * (i + 1))
                            v.wait_ge(qi.h, qi_d5 + 16 * (it + 1))
                            dv.inc(v.scalar_tensor_tensor(out=x2acc[:, i % 2, :],
                                                          in0=ygt[:, it % 4, :],
                                                          scalar=w8_all[:, i, s_:s_ + 1],
                                                          in1=x2acc[:, i % 2, :],
                                                          op0=ALU.mult, op1=ALU.add))
                            snapD5stt[it] = dv.n
                        blk.vector(d5_dv)
                    sD5 = dv.n

                    def d5_out(s, i=i, sD5=sD5):
                        s.wait_ge(dv.h, sD5)
                        qs.inc(s.dma_start(out=out_d[i * 128:(i + 1) * 128, :],
                                           in_=x2acc[:, i % 2, :]), 16)
                    blk.sync(d5_out)
                    snapD5out[i] = qs.n

                def d5_fin(s):
                    s.wait_ge(qs.h, qs.n)
                blk.sync(d5_fin)


        if 1 <= upto < 2:
            ln1_cm.__exit__(None, None, None)
        if upto < 3:
            for cm in reversed(mres_cm):
                cm.__exit__(None, None, None)

        def _tail_fin(s):
            for s2 in (qs, qa, qw, qi, qc):
                s.wait_ge(s2.h, s2.n)
        blk.sync(_tail_fin)

        if dbg:
            scr = {"x_tm_d": x_tm_d, "sgate_d": sgate_d, "hgn_d": hgn_d,
                   "ln2_d": ln2_d, "x2_d": x2_d, "xdisp_d": xdisp_d, "ydisp_d": ydisp_d}

            def dbg_sync(s):
                s.wait_ge(qs.h, qs.n)
                s.wait_ge(dv.h, dv.n)
                s.wait_ge(ac.h, ac.n)
                s.wait_ge(gp.h, gp.n)
                s.wait_ge(qi.h, qi.n)
                for name, shape, dt in dbg:
                    if name == "ln1xT":
                        continue
                    if name == "posu_all":
                        qs.inc(s.dma_start(out=dbg_outs[name][:], in_=posu_all[:]), 16)
                        continue
                    if name == "w8_all":
                        qs.inc(s.dma_start(out=dbg_outs[name][:],
                                           in_=w8_all.rearrange("p a b -> p (a b)")), 16)
                        continue
                    if name == "idx8_all":
                        qs.inc(s.dma_start(out=dbg_outs[name][:],
                                           in_=idx8_all.rearrange("p a b -> p (a b)")), 16)
                        continue
                    src = scr[name]
                    sl = tuple(slice(0, z) for z in shape)
                    qs.inc(s.dma_start(out=dbg_outs[name][:], in_=src.__getitem__(sl)), 16)
                s.wait_ge(qs.h, qs.n)
            blk.sync(dbg_sync)

    return nc


def _consts():
    c = {}
    c["ident_bf"] = np.eye(128).astype(ml_dtypes.bfloat16)
    c["ident_f"] = np.eye(128, dtype=np.float32)
    c["iota32"] = np.broadcast_to(np.arange(NE, dtype=np.float32), (128, NE)).copy()
    c["trash_c"] = (NDISP + np.arange(128, dtype=np.float32)).reshape(128, 1)
    m = np.empty((2, 128, CH), np.float32)
    for st in range(2):
        s_glob = st * 128 + np.arange(128)[:, None]
        t_glob = np.arange(CH)[None, :]
        m[st] = np.where(t_glob >= s_glob, 0.0, MASKNEG)
    c["mask_st"] = m
    c["ustrict"] = np.triu(np.ones((128, 128), np.float32), 1)
    c["ones_r1"] = np.ones((1, 128), np.float32)
    c["ones_c"] = np.ones((128, 1), np.float32)
    bc = np.zeros((64, HEADS, 128), np.float32)
    for h in range(HEADS):
        bc[h, h, :] = 1.0
    c["bcsel"] = bc.astype(ml_dtypes.bfloat16)
    c["noti"] = (1.0 - np.eye(NE)).astype(np.float32)
    c["i32d"] = np.eye(NE, dtype=np.float32)
    mc = np.zeros((128, 8), np.float32)
    mc[:, 0] = 1.0
    mc[:, 1] = EPS
    mc[:, 2] = 1.0 / DM
    mc[:, 3] = 1.0 / DIN
    mc[:, 4] = -1.0
    c["misc_c"] = mc
    return c


def _prep_weights(inp):
    d = {}
    wt = np.ascontiguousarray(inp["L0_attn_in_proj_w"].T)
    pad = np.zeros((DM, NMT * 128 - PROJ), np.float32)
    wt = np.concatenate([wt, pad], axis=1)
    d["w_in_t"] = np.ascontiguousarray(wt.reshape(DM, NMT, 128).transpose(1, 0, 2))
    d["conv_w_t"] = np.ascontiguousarray(inp["L0_attn_conv_w"].reshape(34, 128, KCONV).transpose(1, 0, 2))
    d["conv_b_t"] = np.ascontiguousarray(inp["L0_attn_conv_b"].reshape(34, 128).T)
    d["dt_bias_c"] = np.ascontiguousarray(inp["L0_attn_dt_bias"].reshape(64, 1))
    d["a_log_c"] = np.ascontiguousarray(inp["L0_attn_A_log"].reshape(64, 1))
    d["d_row"] = np.ascontiguousarray(inp["L0_attn_D"].reshape(1, 64))
    d["norm_w_t"] = np.ascontiguousarray(inp["L0_attn_norm_w"].reshape(32, 128).T)
    d["w_out_t"] = np.ascontiguousarray(inp["L0_attn_out_proj_w"].T.reshape(32, 128, DM))
    d["w_g_t"] = np.ascontiguousarray(inp["L0_ffn_gate_weight"].T.reshape(NKT, 128, NE))
    wgu = inp["L0_ffn_gate_up_proj"]
    d["w_gu_t"] = np.ascontiguousarray(
        wgu.transpose(0, 2, 1).reshape(NE, DM, 12, 128).transpose(0, 2, 1, 3))
    d["w_d_t"] = np.ascontiguousarray(inp["L0_ffn_down_proj"].transpose(0, 2, 1))
    d["sim"] = np.ascontiguousarray(inp["L0_ffn_similarity"])
    return d


def kernel(**inputs):
    x = inputs["L0_attn_hidden_states"]
    shared = _prep_weights(inputs)
    shared.update(_consts())
    in_maps = []
    for c in range(8):
        m = dict(shared)
        m["x"] = np.ascontiguousarray(x[c])
        in_maps.append(m)
    nc = build_nc()
    res = run_bass_kernel_spmd(nc, in_maps, core_ids=list(range(8)))
    return np.stack([r["out"] for r in res.results]).astype(np.float32)




# version 04
# speedup vs baseline: 1.2108x; 1.1485x over previous
import numpy as np
import ml_dtypes

import concourse.bass as bass
import concourse.mybir as mybir
from concourse.bass_utils import run_bass_kernel_spmd

F32 = mybir.dt.float32
BF16 = mybir.dt.bfloat16
U32 = mybir.dt.uint32
AF = mybir.ActivationFunctionType
ALU = mybir.AluOpType
AX = mybir.AxisListType
IOA = bass.IndirectOffsetOnAxis

L = 2048
DM = 2048
DIN = 4096
HEADS = 64
HD = 64
STATE = 128
KCONV = 4
CD = DIN + 2 * STATE
PROJ = DIN + CD + HEADS
NMT = 67
CH = 256
NCH = L // CH
EPS = 1e-5
NE = 32
TOPK = 8
SSEL = 4
THR = 0.5
DFF = 768
CAP = 896
NDISP = NE * CAP
NTT = L // 128
NKT = DM // 128
MASKNEG = -1e30



from contextlib import contextmanager, ExitStack


@contextmanager
def _multi(*cms):
    with ExitStack() as _st:
        yield [_st.enter_context(cm) for cm in cms]


class Sem:
    __slots__ = ("h", "n")

    def __init__(self, stack, nc, name):
        self.h = stack.enter_context(nc.semaphore(name))
        self.n = 0

    def inc(self, instr, k=1):
        instr.then_inc(self.h, k)
        self.n += k
        return instr


def build_nc(dbg=None, upto=99, serial_b=False):
    nc = bass.Bass("TRN2", target_bir_lowering=False)

    def dram(name, shape, dt, kind):
        return nc.dram_tensor(name, shape, dt, kind=kind).ap()

    x_in = dram("x", [L, DM], F32, "ExternalInput")
    w_in_t = dram("w_in_t", [NMT, DM, 128], F32, "ExternalInput")
    conv_w_t = dram("conv_w_t", [128, 34, KCONV], F32, "ExternalInput")
    conv_b_t = dram("conv_b_t", [128, 34], F32, "ExternalInput")
    dt_bias_c = dram("dt_bias_c", [64, 1], F32, "ExternalInput")
    a_log_c = dram("a_log_c", [64, 1], F32, "ExternalInput")
    d_row_i = dram("d_row", [1, 64], F32, "ExternalInput")
    norm_w_t = dram("norm_w_t", [128, 32], F32, "ExternalInput")
    w_out_t = dram("w_out_t", [32, 128, DM], F32, "ExternalInput")
    w_g_t = dram("w_g_t", [NKT, 128, NE], F32, "ExternalInput")
    w_gu_t = dram("w_gu_t", [NE, 12, DM, 128], F32, "ExternalInput")
    w_d_t = dram("w_d_t", [NE, DFF, DM], F32, "ExternalInput")
    sim_i = dram("sim", [NE, NE], F32, "ExternalInput")
    ident_bf_i = dram("ident_bf", [128, 128], BF16, "ExternalInput")
    ident_f_i = dram("ident_f", [128, 128], F32, "ExternalInput")
    iota32_i = dram("iota32", [128, NE], F32, "ExternalInput")
    trash_i = dram("trash_c", [128, 1], F32, "ExternalInput")
    mask_i = dram("mask_st", [2, 128, CH], F32, "ExternalInput")
    ustrict_i = dram("ustrict", [128, 128], F32, "ExternalInput")
    ones_r1_i = dram("ones_r1", [1, 128], F32, "ExternalInput")
    ones_c_i = dram("ones_c", [128, 1], F32, "ExternalInput")
    bcsel_i = dram("bcsel", [64, HEADS, 128], BF16, "ExternalInput")
    noti_i = dram("noti", [NE, NE], F32, "ExternalInput")
    i32_i = dram("i32d", [NE, NE], F32, "ExternalInput")
    misc_i = dram("misc_c", [128, 8], F32, "ExternalInput")

    x_tm_d = dram("x_tm_d", [L, DIN], BF16, "Internal")
    sgate_d = dram("sgate_d", [L, DIN], BF16, "Internal")
    hgn_d = dram("hgn_d", [32, 128, L], BF16, "Internal")
    ln2_d = dram("ln2_d", [L, DM], BF16, "Internal")
    x2_d = dram("x2_d", [L, DM], F32, "Internal")
    xdisp_d = dram("xdisp_d", [NDISP + 128, DM], BF16, "Internal")
    ydisp_d = dram("ydisp_d", [NDISP + 128, DM], BF16, "Internal")

    out_d = dram("out", [L, DM], F32, "ExternalOutput")
    dbg_outs = {}
    if dbg:
        for name, shape, dt in dbg:
            dbg_outs[name] = dram("dbg_" + name, shape, dt, "ExternalOutput")

    es = ExitStack()
    with es:
        def sb(name, shape, dt):
            return es.enter_context(nc.sbuf_tensor(name, shape, dt))

        identb = sb("identb", [128, 128], BF16)
        identf = sb("identf", [128, 128], F32)
        iota32 = sb("iota32s", [128, NE], F32)
        trash = sb("trashs", [128, 1], F32)
        maskst = sb("masksts", [128, 2, CH], F32)
        ustrict = sb("ustricts", [128, 128], F32)
        ones_r1 = sb("ones_r1s", [1, 128], F32)
        ones_c = sb("ones_cs", [128, 1], F32)
        convw = sb("convws", [128, 34, KCONV], F32)
        convb = sb("convbs", [128, 34], F32)
        dtbias = sb("dtbiass", [64, 1], F32)
        alogc = sb("alogcs", [64, 1], F32)
        acol = sb("acols", [64, 1], F32)
        drow = sb("drows", [1, 64], F32)
        dbc = sb("dbcs", [128, 64], F32)
        normw = sb("normws", [128, 32], F32)
        simd = sb("simds", [NE, NE], F32)
        simraw = sb("simraws", [NE, NE], F32)
        noti = sb("notis", [NE, NE], F32)
        i32d = sb("i32ds", [NE, NE], F32)
        miscc = sb("misccs", [128, 8], F32)
        wg_hi = sb("wg_hi", [128, NKT, NE], BF16)
        wg_lo = sb("wg_lo", [128, NKT, NE], BF16)
        wg_f = sb("wg_f", [128, NKT, NE], F32)
        bcsel = sb("bcsels", [64, HEADS, 128], BF16)

        logits_tm = sb("logits_tm", [128, NTT, NE], F32)
        w8_all = sb("w8_all", [128, NTT, TOPK], F32)
        idx8_all = sb("idx8_all", [128, NTT, TOPK], F32)
        posu_all = sb("posu_all", [128, TOPK * NTT], U32)

        names = ["qc", "qa", "qw", "qs", "qi", "pe", "dv", "dw", "ev", "ac", "gp"]
        S = {k: Sem(es, nc, "sem_" + k) for k in names}
        qc, qa, qw, qs, qi, pe, dv, dw, ev, ac, gp = (S[k] for k in names)

        blk = es.enter_context(nc.Block())


        def phase_barrier():
            bar = [(dv, dv.n), (dw, dw.n), (ev, ev.n), (ac, ac.n), (gp, gp.n),
                   (pe, pe.n), (qs, qs.n), (qi, qi.n), (qa, qa.n), (qw, qw.n),
                   (qc, qc.n)]

            def mk(eng):
                for s2, v2 in bar:
                    eng.wait_ge(s2.h, v2)
            blk.tensor(mk)
            blk.vector(mk)
            blk.scalar(mk)
            blk.gpsimd(mk)
            blk.sync(mk)

        def p0_sync(s):
            for dst, src in [
                (identb, ident_bf_i), (identf, ident_f_i), (iota32, iota32_i),
                (trash, trash_i), (ustrict, ustrict_i),
                (ones_r1, ones_r1_i), (ones_c, ones_c_i), (bcsel, bcsel_i),
                (dtbias, dt_bias_c), (alogc, a_log_c), (drow, d_row_i),
                (simraw, sim_i), (noti, noti_i), (i32d, i32_i), (miscc, misc_i),
            ]:
                qc.inc(s.dma_start(out=dst[:], in_=src[:]), 16)
            qc.inc(s.dma_start(out=maskst[:], in_=mask_i.rearrange("a p c -> p a c")), 16)
            qc.inc(s.dma_start(out=convw[:], in_=conv_w_t[:]), 16)
            qc.inc(s.dma_start(out=convb[:], in_=conv_b_t[:]), 16)
            qc.inc(s.dma_start(out=normw[:], in_=norm_w_t[:]), 16)
            qc.inc(s.dma_start(out=wg_f[:], in_=w_g_t.rearrange("a p c -> p a c")), 16)
        blk.sync(p0_sync)
        qc_all = qc.n

        p0ps_cm = nc.psum_tensor("p0ps", [128, 64], F32)
        p0ps = p0ps_cm.__enter__()

        def p0_pe(t):
            t.wait_ge(qc.h, qc_all)
            pe.inc(t.matmul(p0ps[:], ones_r1[:], drow[:], start=True, stop=True))
        blk.tensor(p0_pe)
        snap_p0pe = pe.n

        def p0_dv(v):
            v.wait_ge(qc.h, qc_all)
            v.wait_ge(pe.h, snap_p0pe)
            v.tensor_copy(out=dbc[:], in_=p0ps[:])
            v.tensor_tensor(out=simd[:], in0=simraw[:], in1=noti[:], op=ALU.mult)
            v.tensor_tensor(out=simd[:], in0=simd[:], in1=i32d[:], op=ALU.add)
            v.tensor_copy(out=wg_hi[:], in_=wg_f[:])
            dv.inc(v.tensor_tensor(out=wg_lo[:], in0=wg_f[:], in1=wg_hi[:], op=ALU.subtract))
        blk.vector(p0_dv)
        snap_p0dv = dv.n

        def p0_ac(s):
            s.wait_ge(qc.h, qc_all)
            s.activation(out=acol[:], in_=alogc[:], func=AF.Exp)
            ac.inc(s.mul(acol[:], acol[:], -1.0))
        blk.scalar(p0_ac)
        snap_p0ac = ac.n
        p0ps_cm.__exit__(None, None, None)

        mres_cm = [
            nc.sbuf_tensor("b_fm", [128, L], BF16),
            nc.sbuf_tensor("c_fm", [128, L], BF16),
            nc.sbuf_tensor("b_tm", [128, NTT, 128], BF16),
            nc.sbuf_tensor("dt_fm", [64, L], F32),
            nc.sbuf_tensor("da_fm", [64, L], F32),
            nc.sbuf_tensor("dt_tm", [128, NTT, 64], F32),
            nc.sbuf_tensor("s_all", [128, HEADS * HD], F32),
            nc.sbuf_tensor("s_bf", [128, HEADS * HD], BF16),
        ]
        b_fm, c_fm, b_tm, dt_fm, da_fm, dt_tm, s_all, s_bf = [cm.__enter__() for cm in mres_cm]

        if upto >= 1:
            ln1_cm = nc.sbuf_tensor("ln1xT", [128, NKT, L], BF16)
            ln1xT = ln1_cm.__enter__()
            with _multi(
                nc.sbuf_tensor("axt", [128, 2, DM], F32),
                nc.sbuf_tensor("axc", [128, 2, DM], F32),
                nc.sbuf_tensor("acl", [128, 8], F32),
                nc.sbuf_tensor("alnb", [128, 2, DM], BF16),
                nc.psum_tensor("aptb", [128, 2, 1024], BF16),
            ) as (axt, axc, acl, alnb, aptb):
                qa_a0 = qa.n
                snapA1 = [None] * NTT
                snapAT = [None] * NTT
                snapApt = {}
                snapApc = {}
                snapApc = {}
                for i in range(NTT):
                    p = i % 2

                    def a_sy(s, i=i, p=p):
                        if i >= 2:
                            s.wait_ge(dv.h, snapA1[i - 2])
                        qa.inc(s.dma_start(out=axt[:, p, :], in_=x_in[i * 128:(i + 1) * 128, :]), 16)
                    blk.sync(a_sy)

                    def a_dv1(v, i=i, p=p):
                        v.wait_ge(qa.h, qa_a0 + 16 * (i + 1))
                        v.tensor_reduce(out=acl[:, 0:1], in_=axt[:, p, :], axis=AX.X, op=ALU.add)
                        v.drain()
                        v.tensor_scalar(out=acl[:, 1:2], in0=acl[:, 0:1], scalar1=1.0 / DM,
                                        scalar2=None, op0=ALU.mult)
                        v.drain()
                        dv.inc(v.tensor_scalar(out=axc[:, p, :], in0=axt[:, p, :], scalar1=acl[:, 1:2],
                                               scalar2=None, op0=ALU.subtract))
                    blk.vector(a_dv1)
                    snapA1[i] = dv.n

                    def a_ac(s, i=i, p=p):
                        s.wait_ge(dv.h, snapA1[i])
                        if i >= 2:
                            s.wait_ge(pe.h, snapAT[i - 2])
                        s.activation(out=alnb[:, p, :], in_=axc[:, p, :], func=AF.Square,
                                     accum_out=acl[:, 4:5])
                        ac.inc(s.activation(out=acl[:, 5:6], in_=acl[:, 4:5], func=AF.Sqrt,
                                            scale=miscc[:, 2:3], bias=miscc[:, 1:2]))
                    blk.scalar(a_ac)
                    snapAac = ac.n

                    def a_dv2(v, i=i, p=p, snapAac=snapAac):
                        v.wait_ge(ac.h, snapAac)
                        if i >= 2:
                            v.wait_ge(pe.h, snapAT[i - 2])
                        v.reciprocal(out=acl[:, 6:7], in_=acl[:, 5:6])
                        v.drain()
                        dv.inc(v.tensor_scalar(out=alnb[:, p, :], in0=axc[:, p, :], scalar1=acl[:, 6:7],
                                               scalar2=None, op0=ALU.mult))
                    blk.vector(a_dv2)
                    snapA2 = dv.n

                    for grp in range(2):
                        G = i * 2 + grp
                        bk = G % 2
                        dw_g2 = snapApc.get(G - 2)

                        def a_pe(t, i=i, p=p, grp=grp, bk=bk, snapA2=snapA2, dw_g2=dw_g2):
                            if grp == 0:
                                t.wait_ge(dv.h, snapA2)
                            if dw_g2 is not None:
                                t.wait_ge(dw.h, dw_g2)
                            for kk in range(8):
                                kt = grp * 8 + kk
                                sl = aptb[:, bk, kk * 128:kk * 128 + 128]
                                pe.inc(t.transpose(sl, alnb[:, p, kt * 128:(kt + 1) * 128], identb[:]))
                        blk.tensor(a_pe)
                        snapApt[G] = pe.n

                        def a_dv3(v, i=i, p=p, grp=grp, bk=bk, G=G):
                            v.wait_ge(pe.h, snapApt[G])
                            for kk in range(8):
                                kt = grp * 8 + kk
                                dw.inc(v.tensor_copy(out=ln1xT[:, kt, i * 128:(i + 1) * 128],
                                                     in_=aptb[:, bk, kk * 128:kk * 128 + 128]))
                        blk.vector(a_dv3)
                        snapApc[G] = dw.n
                    snapAT[i] = pe.n
                snapA_dv = dv.n
                snapA_dw = dw.n
                if dbg and any(n == "ln1xT" for n, _, _ in dbg):
                    def a_dump(s):
                        s.wait_ge(dw.h, dw.n)
                        s.wait_ge(dv.h, dv.n)
                        s.wait_ge(pe.h, pe.n)
                        qs.inc(s.dma_start(out=dbg_outs["ln1xT"][:],
                                           in_=ln1xT.rearrange("p a b -> p (a b)")), 16)
                        s.wait_ge(qs.h, qs.n)
                    blk.sync(a_dump)

            phase_barrier()
        if upto >= 2:
            with _multi(
                nc.sbuf_tensor("wbuf", [128, 2, NKT, 128], BF16),
                nc.sbuf_tensor("gcx", [128, DM], BF16),
                nc.sbuf_tensor("stage", [128, 2, NKT, 128], BF16),
                nc.sbuf_tensor("convin", [128, 3 + L], F32),
                nc.sbuf_tensor("cacc", [128, L], F32),
                nc.psum_tensor("bps", [128, 4, 512], F32),
                nc.psum_tensor("bptb", [128, 2, 1024], BF16),
                nc.psum_tensor("bptf", [128, 2, 512], F32),
            ) as (wbuf, gcx, stage, convin, cacc, bps, bptb, bptf):
                dtr = convin[0:64, 3:3 + L]
                dtt1 = cacc[0:64, 0:L]

                def b_init(v):
                    dv.inc(v.memset(convin[:, 0:3], 0.0))
                blk.vector(b_init)
                snapBz = dv.n

                qw_b0 = qw.n
                qwsnapB = {}
                bgrp_ctr = [0]
                snapBgt = {}
                snapBgc = {}
                ev_b0 = ev.n
                evB_before = {}
                snapBmm = [None] * NMT
                snapBsilu = {}
                snapBT = {}
                snapBC = {}
                last_stage_dma = [None, None]
                last_gcx_T = [None]

                def sec_of(m):
                    return "gate" if m < 32 else ("xbc" if m < 66 else "dt")

                for m0 in range(2):
                    def b_gp0(g, m0=m0):
                        if m0 == 0:
                            g.wait_ge(qc.h, qc_all)
                        qw.inc(g.dma_start(out=wbuf[:, m0 % 2, :, :],
                                           in_=w_in_t[m0].rearrange("(kt kp) m -> kp kt m", kp=128)), 16)
                        qwsnapB[m0] = qw.n
                    blk.gpsimd(b_gp0)

                for m in range(NMT):
                    if serial_b:
                        phase_barrier()
                    sec = sec_of(m)
                    evB_before[m] = ev.n

                    def b_pe(t, m=m):
                        t.wait_ge(qw.h, qwsnapB[m])
                        if m == 0:
                            t.wait_ge(dv.h, snapA_dv)
                            t.wait_ge(dw.h, snapA_dw)
                        if m >= 1:
                            t.wait_ge(ev.h, evB_before[m - 1] + 4)
                        for kt in range(NKT):
                            for nt in range(4):
                                li = t.matmul(bps[:, nt, :], wbuf[:, m % 2, kt, :],
                                              ln1xT[:, kt, nt * 512:(nt + 1) * 512],
                                              start=(kt == 0), stop=(kt == NKT - 1),
                                              skip_group_check=True)
                        pe.inc(li)
                    blk.tensor(b_pe)
                    snapBmm[m] = pe.n

                    if m + 2 < NMT:
                        def b_gp(g, m=m):
                            g.wait_ge(pe.h, snapBmm[m])
                            qw.inc(g.dma_start(out=wbuf[:, (m + 2) % 2, :, :],
                                               in_=w_in_t[m + 2].rearrange("(kt kp) m -> kp kt m",
                                                                           kp=128)), 16)
                            qwsnapB[m + 2] = qw.n
                        blk.gpsimd(b_gp)

                    if sec == "gate":
                        def b_ac(s, m=m):
                            s.wait_ge(pe.h, snapBmm[m])
                            if last_gcx_T[0] is not None:
                                s.wait_ge(pe.h, last_gcx_T[0])
                            for nt in range(4):
                                i2 = s.activation(out=gcx[:, nt * 512:(nt + 1) * 512],
                                                  in_=bps[:, nt, :], func=AF.Silu)
                                ev.inc(i2)
                        blk.scalar(b_ac)
                        snapBsilu_ev = ev.n

                        for grp in range(2):
                            G = bgrp_ctr[0]
                            bgrp_ctr[0] += 1
                            bk = G % 2
                            dw_g2 = snapBgc.get(G - 2)

                            def b_pet(t, m=m, grp=grp, bk=bk, dw_g2=dw_g2, snapBsilu_ev=snapBsilu_ev):
                                if grp == 0:
                                    t.wait_ge(ev.h, snapBsilu_ev)
                                    if last_stage_dma[m % 2] is not None:
                                        t.wait_ge(qs.h, last_stage_dma[m % 2])
                                if dw_g2 is not None:
                                    t.wait_ge(dw.h, dw_g2)
                                for kk in range(8):
                                    tt = grp * 8 + kk
                                    sl = bptb[:, bk, kk * 128:kk * 128 + 128]
                                    pe.inc(t.transpose(sl, gcx[:, tt * 128:(tt + 1) * 128], identb[:]))
                            blk.tensor(b_pet)
                            snapBgt[G] = pe.n

                            def b_dvc(v, m=m, grp=grp, bk=bk, G=G):
                                v.wait_ge(pe.h, snapBgt[G])
                                for kk in range(8):
                                    tt = grp * 8 + kk
                                    dw.inc(v.tensor_copy(out=stage[:, m % 2, tt, :],
                                                         in_=bptb[:, bk, kk * 128:kk * 128 + 128]))
                            blk.vector(b_dvc)
                            snapBgc[G] = dw.n
                        last_gcx_T[0] = pe.n
                        snapBcp = dw.n

                        def b_sy(s, m=m, snapBcp=snapBcp):
                            s.wait_ge(dw.h, snapBcp)
                            od = sgate_d.rearrange("(tt p) f -> p tt f", p=128)
                            qs.inc(s.dma_start(out=od[:, :, m * 128:(m + 1) * 128],
                                               in_=stage[:, m % 2, :, :]), 16)
                        blk.sync(b_sy)
                        last_stage_dma[m % 2] = qs.n

                    elif sec == "xbc":
                        ct = m - 32

                        def b_dv(v, m=m, ct=ct):
                            v.wait_ge(pe.h, snapBmm[m])
                            if m == 32:
                                v.wait_ge(dv.h, snapBz)
                            for nt in range(4):
                                ev.inc(v.tensor_copy(out=convin[:, 3 + nt * 512:3 + (nt + 1) * 512],
                                                     in_=bps[:, nt, :]))
                            if ct > 0:
                                v.wait_ge(ac.h, snapBsilu[m - 1])
                            v.tensor_scalar(out=cacc[:], in0=convin[:, 0:L],
                                            scalar1=convw[:, ct, 0:1], scalar2=convb[:, ct:ct + 1],
                                            op0=ALU.mult, op1=ALU.add)
                            for kk in range(1, KCONV):
                                li = v.scalar_tensor_tensor(out=cacc[:], in0=convin[:, kk:kk + L],
                                                            scalar=convw[:, ct, kk:kk + 1], in1=cacc[:],
                                                            op0=ALU.mult, op1=ALU.add)
                            dv.inc(li)
                        blk.vector(b_dv)
                        snapBcv = dv.n

                        def b_ac(s, m=m, ct=ct, snapBcv=snapBcv):
                            s.wait_ge(dv.h, snapBcv)
                            if ct < 32 and last_gcx_T[0] is not None:
                                s.wait_ge(pe.h, last_gcx_T[0])
                            dst = b_fm if ct == 32 else (c_fm if ct == 33 else gcx)
                            ac.inc(s.activation(out=dst[:], in_=cacc[:], func=AF.Silu))
                        blk.scalar(b_ac)
                        snapBsilu[m] = ac.n

                        if ct <= 32:
                            for grp in range(2):
                                G = bgrp_ctr[0]
                                bgrp_ctr[0] += 1
                                bk = G % 2
                                dw_g2 = snapBgc.get(G - 2)

                                def b_pet(t, m=m, ct=ct, grp=grp, bk=bk, dw_g2=dw_g2):
                                    if grp == 0:
                                        t.wait_ge(ac.h, snapBsilu[m])
                                        if ct < 32 and last_stage_dma[m % 2] is not None:
                                            t.wait_ge(qs.h, last_stage_dma[m % 2])
                                    if dw_g2 is not None:
                                        t.wait_ge(dw.h, dw_g2)
                                    src = b_fm if ct == 32 else gcx
                                    for kk in range(8):
                                        tt = grp * 8 + kk
                                        sl = bptb[:, bk, kk * 128:kk * 128 + 128]
                                        pe.inc(t.transpose(sl, src[:, tt * 128:(tt + 1) * 128],
                                                           identb[:]))
                                blk.tensor(b_pet)
                                snapBgt[G] = pe.n

                                def b_dvc(v, m=m, ct=ct, grp=grp, bk=bk, G=G):
                                    v.wait_ge(pe.h, snapBgt[G])
                                    for kk in range(8):
                                        tt = grp * 8 + kk
                                        dst = b_tm[:, tt, :] if ct == 32 else stage[:, m % 2, tt, :]
                                        dw.inc(v.tensor_copy(
                                            out=dst, in_=bptb[:, bk, kk * 128:kk * 128 + 128]))
                                blk.vector(b_dvc)
                                snapBgc[G] = dw.n
                            if ct < 32:
                                last_gcx_T[0] = pe.n
                            snapBcp = dw.n

                            if ct < 32:
                                def b_sy(s, m=m, ct=ct, snapBcp=snapBcp):
                                    s.wait_ge(dw.h, snapBcp)
                                    od = x_tm_d.rearrange("(tt p) f -> p tt f", p=128)
                                    qs.inc(s.dma_start(out=od[:, :, ct * 128:(ct + 1) * 128],
                                                       in_=stage[:, m % 2, :, :]), 16)
                                blk.sync(b_sy)
                                last_stage_dma[m % 2] = qs.n
                    else:
                        def b_dv(v, m=m):
                            v.wait_ge(pe.h, snapBmm[m])
                            v.wait_ge(ac.h, snapBsilu[65])
                            for nt in range(4):
                                ev.inc(v.tensor_copy(out=dtr[:, nt * 512:(nt + 1) * 512],
                                                     in_=bps[0:64, nt, :]))
                            dv.inc(v.tensor_scalar(out=dtr[:], in0=dtr[:], scalar1=dtbias[:],
                                                   scalar2=None, op0=ALU.add))
                        blk.vector(b_dv)
                        snapBdt1 = dv.n

                        def b_ac(s, m=m, snapBdt1=snapBdt1):
                            s.wait_ge(dv.h, snapBdt1)
                            s.activation(out=dtt1[:], in_=dtr[:], func=AF.Abs)
                            s.activation(out=dtt1[:], in_=dtt1[:], func=AF.Exp, scale=miscc[0:64, 4:5])
                            ac.inc(s.activation(out=dtt1[:], in_=dtt1[:], func=AF.Ln,
                                                bias=miscc[0:64, 0:1]))
                        blk.scalar(b_ac)
                        snapBdt2 = ac.n

                        def b_dv2(v, m=m, snapBdt2=snapBdt2):
                            v.wait_ge(ac.h, snapBdt2)
                            v.wait_ge(ac.h, snap_p0ac)
                            v.tensor_scalar(out=dt_fm[:], in0=dtr[:], scalar1=0.0, scalar2=None,
                                            op0=ALU.max)
                            v.tensor_tensor(out=dt_fm[:], in0=dt_fm[:], in1=dtt1[:], op=ALU.add)
                            dv.inc(v.tensor_scalar(out=da_fm[:], in0=dt_fm[:], scalar1=acol[:],
                                                   scalar2=None, op0=ALU.mult))
                        blk.vector(b_dv2)
                        snapBdt3 = dv.n

                        for grp in range(2):
                            bk = grp % 2

                            def b_pe2(t, m=m, grp=grp, bk=bk, snapBdt3=snapBdt3):
                                if grp == 0:
                                    t.wait_ge(dv.h, snapBdt3)
                                for kk in range(8):
                                    tt = grp * 8 + kk
                                    pe.inc(t.transpose(bptf[:, bk, kk * 64:kk * 64 + 64],
                                                       dt_fm[:, tt * 128:(tt + 1) * 128],
                                                       identf[0:64, 0:64]))
                            blk.tensor(b_pe2)
                            sBt = pe.n

                            def b_dv3(v, m=m, grp=grp, bk=bk, sBt=sBt):
                                v.wait_ge(pe.h, sBt)
                                for kk in range(8):
                                    tt = grp * 8 + kk
                                    dw.inc(v.tensor_copy(out=dt_tm[:, tt, :],
                                                         in_=bptf[:, bk, kk * 64:kk * 64 + 64]))
                            blk.vector(b_dv3)

                snapB_dv = dv.n
                snapB_dw = dw.n
                snapB_ac = ac.n
                qs_b_end = qs.n
            ln1_cm.__exit__(None, None, None)

            phase_barrier()
        if upto >= 3:
            with _multi(
                nc.sbuf_tensor("xch", [128, 2, 2, DIN], BF16),
                nc.sbuf_tensor("sgch", [128, 2, DIN], BF16),
                nc.sbuf_tensor("dacum", [64, CH], F32),
                nc.sbuf_tensor("dachi", [64, CH], BF16),
                nc.sbuf_tensor("daclo", [64, CH], BF16),
                nc.sbuf_tensor("dacum_tm", [128, 2, 64], F32),
                nc.sbuf_tensor("expda_tm", [128, 2, 64], F32),
                nc.sbuf_tensor("zeros64", [64, CH], F32),
                nc.sbuf_tensor("segT", [128, 4, 2, CH], F32),
                nc.sbuf_tensor("mts", [128, 2, 2, CH], BF16),
                nc.sbuf_tensor("xdt", [128, 4, 2, HD], BF16),
                nc.sbuf_tensor("xdec", [128, 2, 2, HD], BF16),
                nc.sbuf_tensor("ydc", [128, 2, 2, HD], F32),
                nc.sbuf_tensor("xdd", [128, 8, 2, HD], BF16),
                nc.sbuf_tensor("y1c", [128, 2, 2, HD], F32),
                nc.sbuf_tensor("ych", [128, 2, DIN], BF16),
                nc.sbuf_tensor("ecol", [128, 64], F32),
                nc.sbuf_tensor("erow", [1, 64], F32),
                nc.sbuf_tensor("sqs", [128, DIN], BF16),
                nc.sbuf_tensor("rcol", [128, 8], F32),
                nc.sbuf_tensor("hstage", [128, 32, 128], BF16),
                nc.psum_tensor("cpsS", [128, 2, CH], F32),
                nc.psum_tensor("cpA", [128, 2, 512], F32),
                nc.psum_tensor("cpY", [128, 2, 512], F32),
                nc.psum_tensor("cpM", [128, 512], F32),
                nc.psum_tensor("cptb", [128, 2, 1024], BF16),
            ) as (xch, sgch, dacum, dachi, daclo, dacum_tm, expda_tm, zeros64, segT, mts, xdt, xdec, ydc, xdd, y1c, ych, ecol, erow, sqs, rcol, hstage, cpsS, cpA, cpY, cpM, cptb):
                def c_init(v):
                    v.wait_ge(qs.h, qs_b_end)
                    v.memset(s_all[:], 0.0)
                    v.memset(s_bf[:], 0.0)
                    dv.inc(v.memset(zeros64[:], 0.0))
                blk.vector(c_init)
                snapCini = dv.n

                qa_c0 = qa.n
                snapCgpend = [None] * NCH
                snapChgmul = [None] * NCH
                snapCsbf = [None] * NCH
                snapCh2last = [None] * NCH
                snapCecl = [None] * NCH
                snapCerow = [None] * NCH
                snapChgT = [None] * NCH
                snapCcprT = [None] * NCH
                last_hstage_dma = [None]
                cgrp_ctr = [0]
                snapCgt = {}
                snapCgc = {}

                xv = x_tm_d.rearrange("(c tt p) f -> p c tt f", p=128, tt=2)
                gv = sgate_d.rearrange("(c tt p) f -> p c tt f", p=128, tt=2)
                for c in range(NCH):
                    cp = c % 2

                    def c_sy(s, c=c, cp=cp):
                        if c == 0:
                            s.wait_ge(dv.h, snapCini)
                        if c >= 2:
                            s.wait_ge(gp.h, snapCgpend[c - 2])
                            s.wait_ge(dv.h, snapChgmul[c - 2])
                        qa.inc(s.dma_start(out=xch[:, cp, :, :], in_=xv[:, c, :, :]), 16)
                        if c >= 1:
                            s.wait_ge(dv.h, snapChgmul[c - 1])
                            s.wait_ge(pe.h, snapChgT[c - 1])
                        qa.inc(s.dma_start(out=sgch[:], in_=gv[:, c, :, :]), 16)
                    blk.sync(c_sy)

                    def c_scan(v, c=c):
                        v.wait_ge(qa.h, qa_c0 + 32 * (c + 1))
                        if c >= 1:
                            v.wait_ge(pe.h, snapCcprT[c - 1])
                        v.tensor_tensor_scan(out=dacum[:], data0=da_fm[:, c * CH:(c + 1) * CH],
                                             data1=zeros64[:], initial=0.0, op0=ALU.add, op1=ALU.add)
                        v.tensor_copy(out=dachi[:], in_=dacum[:])
                        dv.inc(v.tensor_tensor(out=daclo[:], in0=dacum[:], in1=dachi[:],
                                               op=ALU.subtract))
                    blk.vector(c_scan)
                    snapCscan = dv.n

                    def c_pe0(t, c=c, snapCscan=snapCscan):
                        t.wait_ge(dv.h, snapCscan)
                        if c == 0:
                            t.wait_ge(dv.h, snapB_dv)
                            t.wait_ge(dw.h, snapB_dw)
                            t.wait_ge(ac.h, snapB_ac)
                        if c >= 1:
                            t.wait_ge(ac.h, snapCerow[c - 1])
                        pe.inc(t.transpose(cpM[0:1, 128:192], dacum[:, CH - 1:CH], identf[0:64, 0:64]))
                    blk.tensor(c_pe0)
                    snapCcpr = pe.n

                    def c_ac0(s, c=c, snapCcpr=snapCcpr):
                        s.wait_ge(pe.h, snapCcpr)
                        ac.inc(s.activation(out=erow[:], in_=cpM[0:1, 128:192], func=AF.Exp))
                    blk.scalar(c_ac0)
                    snapCerow[c] = ac.n

                    def c_pe0b(t, c=c):
                        t.wait_ge(ac.h, snapCerow[c])
                        if c >= 1:
                            t.wait_ge(dv.h, snapCecl[c - 1])
                        pe.inc(t.matmul(cpM[:, 0:64], ones_r1[:], erow[:], start=True, stop=True))
                    blk.tensor(c_pe0b)
                    snapCpec = pe.n

                    def c_dve(v, c=c, snapCpec=snapCpec):
                        v.wait_ge(pe.h, snapCpec)
                        dv.inc(v.tensor_copy(out=ecol[:], in_=cpM[:, 0:64]))
                    blk.vector(c_dve)
                    snapCecl_pre = dv.n

                    for st in range(2):
                        dv_now = dv.n

                        def c_pt(t, c=c, st=st, snapCscan=snapCscan, dv_now=dv_now):
                            if st == 0:
                                t.wait_ge(dv.h, snapCscan)
                            t.wait_ge(dv.h, dv_now)
                            pe.inc(t.transpose(cpM[:, 64:128], dacum[:, st * 128:(st + 1) * 128],
                                               identf[0:64, 0:64]))
                        blk.tensor(c_pt)
                        sPT = pe.n

                        def c_ptc(v, c=c, st=st, sPT=sPT):
                            v.wait_ge(pe.h, sPT)
                            dv.inc(v.tensor_copy(out=dacum_tm[:, st, :], in_=cpM[:, 64:128]))
                        blk.vector(c_ptc)
                    snapCdat = dv.n

                    def c_act(s, c=c, snapCdat=snapCdat):
                        s.wait_ge(dv.h, snapCdat)
                        ac.inc(s.activation(out=expda_tm.rearrange("p a b -> p (a b)"),
                                            in_=dacum_tm.rearrange("p a b -> p (a b)"), func=AF.Exp))
                    blk.scalar(c_act)
                    snapCexp = ac.n

                    def c_psc(t, c=c, snapCdat=snapCdat):
                        t.wait_ge(dv.h, snapCdat)
                        if c >= 1:
                            t.wait_ge(dw.h, snapCh2last[c - 1])
                        for st in range(2):
                            li = t.matmul(cpsS[:, st, :],
                                          b_fm[:, c * CH + st * 128:c * CH + st * 128 + 128],
                                          c_fm[:, c * CH:(c + 1) * CH], start=True, stop=True)
                        pe.inc(li)
                    blk.tensor(c_psc)
                    snapCsc = pe.n

                    snapCh1 = {}
                    snapCh2 = {}
                    snapCh3 = {}
                    snapCpR = {}
                    snapCpY = {}
                    snapCgph = {}
                    snapCeseg = {}
                    def st_pe1(h):
                        hp = h % 2

                        def f(t, c=c, h=h, hp=hp):
                            if h == 0:
                                t.wait_ge(ac.h, snapCexp)
                                t.wait_ge(pe.h, snapCsc)
                                if c >= 1:
                                    t.wait_ge(dv.h, snapCsbf[c - 1])
                            if h >= 2:
                                t.wait_ge(dv.h, snapCh1[h - 2])
                            t.matmul(cpA[:, hp, 0:CH], bcsel[:, h, :], dachi[:],
                                     start=True, stop=False, skip_group_check=True)
                            pe.inc(t.matmul(cpA[:, hp, 0:CH], bcsel[:, h, :], daclo[:],
                                            start=False, stop=True, skip_group_check=True))
                        blk.tensor(f)
                        snapCpR[h] = pe.n

                    def st_dv1(h):
                        hp = h % 2
                        h4 = h % 4

                        def f(v, c=c, h=h, hp=hp, h4=h4):
                            v.wait_ge(pe.h, snapCpR[h])
                            if h >= 4:
                                v.wait_ge(ac.h, snapCeseg[h - 4])
                                v.wait_ge(pe.h, snapCpY[h - 4])
                            if h == 0:
                                v.drain()
                            for st in range(2):
                                v.scalar_tensor_tensor(out=segT[:, h4, st, :], in0=maskst[:, st, :],
                                                       scalar=dacum_tm[:, st, h:h + 1],
                                                       in1=cpA[:, hp, 0:CH],
                                                       op0=ALU.subtract, op1=ALU.add)
                            for st in range(2):
                                li = v.tensor_scalar(out=xdt[:, h4, st, :],
                                                     in0=xch[:, cp, st, h * HD:(h + 1) * HD],
                                                     scalar1=dt_tm[:, 2 * c + st, h:h + 1],
                                                     scalar2=None, op0=ALU.mult)
                            dv.inc(li)
                        blk.vector(f)
                        snapCh1[h] = dv.n

                    def st_ac1(h):
                        h4 = h % 4
                        h8 = h % 8

                        def f(s, c=c, h=h, h4=h4, h8=h8):
                            s.wait_ge(dv.h, snapCh1[h])
                            if h >= 4:
                                s.wait_ge(dw.h, snapCh2[h - 4])
                            if h >= 8:
                                s.wait_ge(gp.h, snapCgph[h - 8])
                            s.activation(out=xdd[:, h8, :, :],
                                         in_=xch[:, cp, :, h * HD:(h + 1) * HD],
                                         func=AF.Copy, scale=dbc[:, h:h + 1], bias=0.0)
                            ac.inc(s.activation(
                                out=segT.rearrange("p a b t -> p a (b t)")[:, h4, :],
                                in_=segT.rearrange("p a b t -> p a (b t)")[:, h4, :],
                                func=AF.Exp))
                        blk.scalar(f)
                        snapCeseg[h] = ac.n

                    def st_dv2(h):
                        hp = h % 2
                        h4 = h % 4

                        def f(v, c=c, h=h, hp=hp, h4=h4):
                            v.wait_ge(ac.h, snapCeseg[h])
                            if h >= 2:
                                v.wait_ge(pe.h, snapCpY[h - 2])
                            v.tensor_tensor(
                                out=mts.rearrange("p a b t -> p a (b t)")[:, hp, :],
                                in0=segT.rearrange("p a b t -> p a (b t)")[:, h4, :],
                                in1=cpsS.rearrange("p a t -> p (a t)"), op=ALU.mult)
                            for st in range(2):
                                li = v.tensor_scalar(out=xdec[:, hp, st, :], in0=xdt[:, h4, st, :],
                                                     scalar1=segT[:, h4, st, CH - 1:CH],
                                                     scalar2=None, op0=ALU.mult)
                            dw.inc(li)
                        blk.vector(f)
                        snapCh2[h] = dw.n

                    def st_pe2(h):
                        hp = h % 2
                        h4 = h % 4

                        def f(t, c=c, h=h, hp=hp, h4=h4):
                            t.wait_ge(dw.h, snapCh2[h])
                            if h >= 2:
                                t.wait_ge(dv.h, snapCh3[h - 2])
                            t.matmul(cpY[:, hp, 0:HD], mts[:, hp, 0, 0:128], xdt[:, h4, 0, :],
                                     start=True, stop=True, skip_group_check=True)
                            t.matmul(cpY[:, hp, HD:2 * HD], mts[:, hp, 0, 128:CH], xdt[:, h4, 0, :],
                                     start=True, stop=False, skip_group_check=True)
                            t.matmul(cpY[:, hp, HD:2 * HD], mts[:, hp, 1, 128:CH], xdt[:, h4, 1, :],
                                     start=False, stop=True, skip_group_check=True)
                            t.matmul(cpY[:, hp, 2 * HD:3 * HD], c_fm[:, c * CH:c * CH + 128],
                                     s_bf[:, h * HD:(h + 1) * HD], start=True, stop=True,
                                     skip_group_check=True)
                            t.matmul(cpY[:, hp, 3 * HD:4 * HD], c_fm[:, c * CH + 128:(c + 1) * CH],
                                     s_bf[:, h * HD:(h + 1) * HD], start=True, stop=True,
                                     skip_group_check=True)
                            t.matmul(cpA[:, hp, CH:CH + HD], b_tm[:, 2 * c, :], xdec[:, hp, 0, :],
                                     start=True, stop=False, skip_group_check=True)
                            pe.inc(t.matmul(cpA[:, hp, CH:CH + HD], b_tm[:, 2 * c + 1, :],
                                            xdec[:, hp, 1, :],
                                            start=False, stop=True, skip_group_check=True))
                        blk.tensor(f)
                        snapCpY[h] = pe.n

                    def st_dv3(h):
                        hp = h % 2

                        def f(v, c=c, h=h, hp=hp):
                            v.wait_ge(pe.h, snapCpY[h])
                            if h >= 2:
                                v.wait_ge(gp.h, snapCgph[h - 2])
                            v.tensor_copy(out=ydc.rearrange("p a b t -> p a (b t)")[:, hp, :],
                                          in_=cpY[:, hp, 0:2 * HD])
                            for st in range(2):
                                v.scalar_tensor_tensor(out=y1c[:, hp, st, :],
                                                       in0=cpY[:, hp, (2 + st) * HD:(3 + st) * HD],
                                                       scalar=expda_tm[:, st, h:h + 1],
                                                       in1=ydc[:, hp, st, :],
                                                       op0=ALU.mult, op1=ALU.add)
                            v.scalar_tensor_tensor(out=s_all[:, h * HD:(h + 1) * HD],
                                                   in0=s_all[:, h * HD:(h + 1) * HD],
                                                   scalar=ecol[:, h:h + 1],
                                                   in1=cpA[:, hp, CH:CH + HD],
                                                   op0=ALU.mult, op1=ALU.add)
                            dv.inc(v.tensor_copy(out=s_bf[:, h * HD:(h + 1) * HD],
                                                 in_=s_all[:, h * HD:(h + 1) * HD]))
                        blk.vector(f)
                        snapCh3[h] = dv.n

                    def st_gp(h):
                        hp = h % 2
                        h8 = h % 8

                        def f(g, c=c, h=h, hp=hp, h8=h8):
                            g.wait_ge(dv.h, snapCh3[h])
                            g.wait_ge(ac.h, snapCeseg[h])
                            if h == 0 and c >= 1:
                                g.wait_ge(dv.h, snapChgmul[c - 1])
                            gp.inc(g.tensor_tensor(
                                out=ych[:, :, h * HD:(h + 1) * HD],
                                in0=xdd.rearrange("p a b t -> p a (b t)")[:, h8, :],
                                in1=y1c.rearrange("p a b t -> p a (b t)")[:, hp, :],
                                op=ALU.add))
                        blk.gpsimd(f)
                        snapCgph[h] = gp.n

                    stages = [st_pe1, st_dv1, st_ac1, st_dv2, st_pe2, st_dv3, st_gp]
                    for slot in range(HEADS + len(stages) - 1):
                        for jst, fn in enumerate(stages):
                            hh = slot - jst
                            if 0 <= hh < HEADS:
                                fn(hh)
                    snapCh2last[c] = snapCh2[HEADS - 1]
                    snapCgpend[c] = gp.n
                    snapCsbf[c] = snapCh3[HEADS - 1]
                    snapCecl[c] = snapCecl_pre
                    snapCcprT[c] = snapCpR[HEADS - 1]

                    def c_hg(v, c=c, cp=cp):
                        v.wait_ge(gp.h, snapCgpend[c])
                        for st in range(2):
                            li = v.tensor_tensor(out=ych[:, st, :], in0=ych[:, st, :],
                                                 in1=sgch[:, st, :], op=ALU.mult)
                        dv.inc(li)
                    blk.vector(c_hg)
                    snapChgmul[c] = dv.n

                    def c_hga(s, c=c):
                        s.wait_ge(dv.h, snapChgmul[c])
                        for st in range(2):
                            s.activation(out=sqs[:], in_=ych[:, st, :], func=AF.Square,
                                         accum_out=rcol[:, st:st + 1])
                        for st in range(2):
                            li = s.activation(out=rcol[:, 2 + st:3 + st], in_=rcol[:, st:st + 1],
                                              func=AF.Sqrt, scale=miscc[:, 3:4], bias=miscc[:, 1:2])
                        ac.inc(li)
                    blk.scalar(c_hga)
                    snapChga = ac.n

                    def c_hg2(v, c=c, snapChga=snapChga):
                        v.wait_ge(ac.h, snapChga)
                        for st in range(2):
                            v.reciprocal(out=rcol[:, 4 + st:5 + st], in_=rcol[:, 2 + st:3 + st])
                        v.drain()
                        for st in range(2):
                            li = v.tensor_scalar(out=sgch[:, st, :], in0=ych[:, st, :],
                                                 scalar1=rcol[:, 4 + st:5 + st], scalar2=None,
                                                 op0=ALU.mult)
                        dv.inc(li)
                    blk.vector(c_hg2)
                    snapChgn = dv.n

                    snapCTg = {}
                    for st in range(2):
                        for grp in range(4):
                            G = cgrp_ctr[0]
                            cgrp_ctr[0] += 1
                            bk = G % 2
                            dw_g2 = snapCgc.get(G - 2)

                            def c_pet2(t, c=c, st=st, grp=grp, bk=bk, dw_g2=dw_g2, snapChgn=snapChgn):
                                if grp == 0:
                                    if st == 0:
                                        t.wait_ge(dv.h, snapChgn)
                                    if last_hstage_dma[0] is not None:
                                        t.wait_ge(qs.h, last_hstage_dma[0])
                                if dw_g2 is not None:
                                    t.wait_ge(dw.h, dw_g2)
                                for kk in range(8):
                                    ft = grp * 8 + kk
                                    sl = cptb[:, bk, kk * 128:kk * 128 + 128]
                                    pe.inc(t.transpose(sl, sgch[:, st, ft * 128:(ft + 1) * 128],
                                                       identb[:]))
                            blk.tensor(c_pet2)
                            snapCgt[G] = pe.n

                            def c_dvc2(v, c=c, st=st, grp=grp, bk=bk, G=G):
                                v.wait_ge(pe.h, snapCgt[G])
                                for kk in range(8):
                                    ft = grp * 8 + kk
                                    dw.inc(v.tensor_scalar(
                                        out=hstage[:, ft, :],
                                        in0=cptb[:, bk, kk * 128:kk * 128 + 128],
                                        scalar1=normw[:, ft:ft + 1], scalar2=None, op0=ALU.mult))
                            blk.vector(c_dvc2)
                            snapCgc[G] = dw.n
                        if st == 1:
                            snapChgT[c] = pe.n
                        snapChst = dw.n

                        def c_sy2(s, c=c, st=st, snapChst=snapChst):
                            s.wait_ge(dw.h, snapChst)
                            t0 = c * CH + st * 128
                            qs.inc(s.dma_start(out=hgn_d[:, :, t0:t0 + 128].rearrange("ft p t -> p ft t"),
                                               in_=hstage[:]), 16)
                        blk.sync(c_sy2)
                        last_hstage_dma[0] = qs.n
                qs_c_end = qs.n
            for cm in reversed(mres_cm):
                cm.__exit__(None, None, None)

            phase_barrier()
        if upto >= 4:
            hv = hgn_d.rearrange("ft p t -> p ft t")
            with _multi(
                nc.sbuf_tensor("wo1", [128, 32, 1024], BF16),
                nc.sbuf_tensor("hgt1", [128, 2, 32, 128], BF16),
                nc.sbuf_tensor("xr1", [128, 2, 1024], F32),
                nc.sbuf_tensor("o1t", [128, 2, 1024], F32),
                nc.psum_tensor("p1ps", [128, 2, 2, 512], F32),
            ) as (wo1, hgt1, xr1, o1t, p1ps):
                def p1_w(g):
                    g.wait_ge(qs.h, qs_c_end)
                    for gq in range(4):
                        qw.inc(g.dma_start(
                            out=wo1[:, gq * 8:(gq + 1) * 8, :],
                            in_=w_out_t[gq * 8:(gq + 1) * 8, :, 0:1024].rearrange("ft p d -> p ft d")), 16)
                blk.gpsimd(p1_w)
                qw_p1 = qw.n

                qa_p1 = qa.n
                snapP1mm = [None] * NTT
                snapP1ev = [None] * NTT
                snapP1st = [None] * NTT
                for i in range(NTT):
                    p = i % 2

                    def p1_sy(s, i=i, p=p):
                        if i == 0:
                            s.wait_ge(qs.h, qs_c_end)
                        if i >= 2:
                            s.wait_ge(pe.h, snapP1mm[i - 2])
                            s.wait_ge(dv.h, snapP1ev[i - 2])
                            s.wait_ge(qs.h, snapP1st[i - 2])
                        qa.inc(s.dma_start(out=hgt1[:, p, :, :], in_=hv[:, :, i * 128:(i + 1) * 128]), 16)
                        qa.inc(s.dma_start(out=xr1[:, p, :],
                                           in_=x_in[i * 128:(i + 1) * 128, 0:1024]), 16)
                    blk.sync(p1_sy)

                    def p1_pe(t, i=i, p=p):
                        t.wait_ge(qa.h, qa_p1 + 32 * i + 16)
                        if i == 0:
                            t.wait_ge(qw.h, qw_p1)
                        if i >= 2:
                            t.wait_ge(dv.h, snapP1ev[i - 2])
                        for ft in range(32):
                            for nd in range(2):
                                li = t.matmul(p1ps[:, p, nd, :], hgt1[:, p, ft, :],
                                              wo1[:, ft, nd * 512:(nd + 1) * 512],
                                              start=(ft == 0), stop=(ft == 31),
                                              skip_group_check=True)
                        pe.inc(li)
                    blk.tensor(p1_pe)
                    snapP1mm[i] = pe.n

                    def p1_dv(v, i=i, p=p):
                        v.wait_ge(pe.h, snapP1mm[i])
                        v.wait_ge(qa.h, qa_p1 + 32 * (i + 1))
                        for nd in range(2):
                            li = v.tensor_tensor(out=o1t[:, p, nd * 512:(nd + 1) * 512],
                                                 in0=p1ps[:, p, nd, :],
                                                 in1=xr1[:, p, nd * 512:(nd + 1) * 512], op=ALU.add)
                        dv.inc(li)
                    blk.vector(p1_dv)
                    snapP1ev[i] = dv.n

                    def p1_st(s, i=i, p=p):
                        s.wait_ge(dv.h, snapP1ev[i])
                        qs.inc(s.dma_start(out=x2_d[i * 128:(i + 1) * 128, 0:1024], in_=o1t[:, p, :]), 16)
                    blk.sync(p1_st)
                    snapP1st[i] = qs.n
                qs_p1_end = qs.n

            phase_barrier()
        if upto >= 5:
            with _multi(
                nc.sbuf_tensor("wo2", [128, 32, 1024], BF16),
                nc.sbuf_tensor("hgt2", [128, 2, 32, 128], BF16),
                nc.sbuf_tensor("xr2", [128, 2, 1024], F32),
                nc.sbuf_tensor("x2t", [128, 2, DM], F32),
                nc.sbuf_tensor("ln2f", [128, DM], F32),
                nc.sbuf_tensor("ln2h", [128, 2, DM], BF16),
                nc.sbuf_tensor("ln2lo", [128, 2, DM], BF16),
                nc.sbuf_tensor("lnT", [128, 2, NKT, 128], BF16),
                nc.sbuf_tensor("dcl", [128, 8], F32),
                nc.sbuf_tensor("lgT", [32, 128], F32),
                nc.psum_tensor("p2ps", [128, 2, 2, 512], F32),
                nc.psum_tensor("dptb", [128, 2, 1024], BF16),
                nc.psum_tensor("dplg", [32, 128], F32),
                nc.psum_tensor("dplt", [128, 32], F32),
            ) as (wo2, hgt2, xr2, x2t, ln2f, ln2h, ln2lo, lnT, dcl, lgT, p2ps, dptb, dplg, dplt):
                def p2_w(g):
                    g.wait_ge(qs.h, qs_p1_end)
                    for gq in range(4):
                        qw.inc(g.dma_start(
                            out=wo2[:, gq * 8:(gq + 1) * 8, :],
                            in_=w_out_t[gq * 8:(gq + 1) * 8, :, 1024:2048].rearrange(
                                "ft p d -> p ft d")), 16)
                blk.gpsimd(p2_w)
                qw_p2 = qw.n

                qa_p2 = qa.n
                snapP2mm = [None] * NTT
                snapP2x2 = [None] * NTT
                snapP2b = [None] * NTT
                snapP2st = [None] * NTT
                snapP2T = [None] * NTT
                snapP2r = [None] * NTT
                snapP2lg = [None] * NTT
                snapP2t2 = [None] * NTT
                snapP2l5 = [None] * NTT
                snapP2gt = {}
                snapP2gc = {}
                for i in range(NTT):
                    p = i % 2

                    def p2_sy(s, i=i, p=p):
                        if i == 0:
                            s.wait_ge(qs.h, qs_p1_end)
                        if i >= 2:
                            s.wait_ge(pe.h, snapP2mm[i - 2])
                            s.wait_ge(dv.h, snapP2x2[i - 2])
                            s.wait_ge(qs.h, snapP2st[i - 2])
                        qa.inc(s.dma_start(out=hgt2[:, p, :, :], in_=hv[:, :, i * 128:(i + 1) * 128]), 16)
                        qa.inc(s.dma_start(out=xr2[:, p, :],
                                           in_=x_in[i * 128:(i + 1) * 128, 1024:2048]), 16)
                        qa.inc(s.dma_start(out=x2t[:, p, 0:1024],
                                           in_=x2_d[i * 128:(i + 1) * 128, 0:1024]), 16)
                    blk.sync(p2_sy)

                    def p2_pe(t, i=i, p=p):
                        t.wait_ge(qa.h, qa_p2 + 48 * i + 16)
                        if i == 0:
                            t.wait_ge(qw.h, qw_p2)
                        if i >= 2:
                            t.wait_ge(dv.h, snapP2x2[i - 2])
                        for ft in range(32):
                            for nd in range(2):
                                li = t.matmul(p2ps[:, p, nd, :], hgt2[:, p, ft, :],
                                              wo2[:, ft, nd * 512:(nd + 1) * 512],
                                              start=(ft == 0), stop=(ft == 31),
                                              skip_group_check=True)
                        pe.inc(li)
                    blk.tensor(p2_pe)
                    snapP2mm[i] = pe.n

                    def p2_dv1(v, i=i, p=p):
                        v.wait_ge(pe.h, snapP2mm[i])
                        v.wait_ge(qa.h, qa_p2 + 48 * (i + 1))
                        for nd in range(2):
                            v.tensor_tensor(out=x2t[:, p, 1024 + nd * 512:1024 + (nd + 1) * 512],
                                            in0=p2ps[:, p, nd, :],
                                            in1=xr2[:, p, nd * 512:(nd + 1) * 512], op=ALU.add)
                        v.tensor_reduce(out=dcl[:, 0:1], in_=x2t[:, p, :], axis=AX.X, op=ALU.add)
                        v.drain()
                        v.tensor_scalar(out=dcl[:, 1:2], in0=dcl[:, 0:1], scalar1=1.0 / DM,
                                        scalar2=None, op0=ALU.mult)
                        v.drain()
                        dv.inc(v.tensor_scalar(out=ln2f[:], in0=x2t[:, p, :], scalar1=dcl[:, 1:2],
                                               scalar2=None, op0=ALU.subtract))
                    blk.vector(p2_dv1)
                    snapP2x2[i] = dv.n

                    def p2_ac(s, i=i, p=p):
                        s.wait_ge(dv.h, snapP2x2[i])
                        if i >= 2:
                            s.wait_ge(pe.h, snapP2T[i - 2])
                        s.activation(out=ln2lo[:, p, :], in_=ln2f[:], func=AF.Square,
                                     accum_out=dcl[:, 2:3])
                        ac.inc(s.activation(out=dcl[:, 3:4], in_=dcl[:, 2:3], func=AF.Sqrt,
                                            scale=miscc[:, 2:3], bias=miscc[:, 1:2]))
                    blk.scalar(p2_ac)
                    snapP2ac = ac.n

                    def p2_dv2(v, i=i, p=p, snapP2ac=snapP2ac):
                        v.wait_ge(ac.h, snapP2ac)
                        if i >= 2:
                            v.wait_ge(pe.h, snapP2T[i - 2])
                        v.reciprocal(out=dcl[:, 4:5], in_=dcl[:, 3:4])
                        v.drain()
                        v.tensor_scalar(out=ln2f[:], in0=ln2f[:], scalar1=dcl[:, 4:5], scalar2=None,
                                        op0=ALU.mult)
                        v.tensor_copy(out=ln2h[:, p, :], in_=ln2f[:])
                        dv.inc(v.tensor_tensor(out=ln2lo[:, p, :], in0=ln2f[:], in1=ln2h[:, p, :],
                                               op=ALU.subtract))
                    blk.vector(p2_dv2)
                    snapP2b[i] = dv.n

                    def p2_st(s, i=i, p=p):
                        s.wait_ge(dv.h, snapP2b[i])
                        qs.inc(s.dma_start(out=x2_d[i * 128:(i + 1) * 128, 1024:2048],
                                           in_=x2t[:, p, 1024:2048]), 16)
                        qs.inc(s.dma_start(out=ln2_d[i * 128:(i + 1) * 128, :], in_=ln2h[:, p, :]), 16)
                    blk.sync(p2_st)
                    snapP2st[i] = qs.n

                    for grp in range(4):
                        G = i * 4 + grp
                        bk = G % 2
                        dw_g2 = snapP2gc.get(G - 2)

                        def p2_pet(t, i=i, p=p, grp=grp, bk=bk, dw_g2=dw_g2):
                            if grp == 0:
                                t.wait_ge(dv.h, snapP2b[i])
                            if dw_g2 is not None:
                                t.wait_ge(dw.h, dw_g2)
                            for kk in range(8):
                                gg = grp * 8 + kk
                                kt, half = gg // 2, gg % 2
                                src = ln2h if half == 0 else ln2lo
                                sl = dptb[:, bk, kk * 128:kk * 128 + 128]
                                pe.inc(t.transpose(sl, src[:, p, kt * 128:(kt + 1) * 128], identb[:]))
                        blk.tensor(p2_pet)
                        snapP2gt[G] = pe.n

                        def p2_dv3(v, i=i, p=p, grp=grp, bk=bk, G=G):
                            if grp == 0 and i >= 1:
                                v.wait_ge(pe.h, snapP2r[i - 1])
                            v.wait_ge(pe.h, snapP2gt[G])
                            for kk in range(8):
                                gg = grp * 8 + kk
                                kt, half = gg // 2, gg % 2
                                dw.inc(v.tensor_copy(out=lnT[:, half, kt, :],
                                                     in_=dptb[:, bk, kk * 128:kk * 128 + 128]))
                        blk.vector(p2_dv3)
                        snapP2gc[G] = dw.n
                    snapP2T[i] = pe.n
                    snapP2c = dw.n

                    def p2_per(t, i=i, snapP2c=snapP2c):
                        t.wait_ge(dw.h, snapP2c)
                        if i >= 1:
                            t.wait_ge(dv.h, snapP2lg[i - 1])
                        first = True
                        for kt in range(NKT):
                            t.matmul(dplg[:], wg_hi[:, kt, :], lnT[:, 0, kt, :], start=first, stop=False)
                            first = False
                            t.matmul(dplg[:], wg_lo[:, kt, :], lnT[:, 0, kt, :], start=False, stop=False)
                            li = t.matmul(dplg[:], wg_hi[:, kt, :], lnT[:, 1, kt, :], start=False,
                                          stop=(kt == NKT - 1))
                        pe.inc(li)
                    blk.tensor(p2_per)
                    snapP2r[i] = pe.n

                    def p2_dv4(v, i=i):
                        v.wait_ge(pe.h, snapP2r[i])
                        if i >= 1:
                            v.wait_ge(pe.h, snapP2t2[i - 1])
                        dv.inc(v.tensor_copy(out=lgT[:], in_=dplg[:]))
                    blk.vector(p2_dv4)
                    snapP2lg[i] = dv.n

                    def p2_pe2(t, i=i):
                        t.wait_ge(dv.h, snapP2lg[i])
                        if i >= 1:
                            t.wait_ge(dv.h, snapP2l5[i - 1])
                        pe.inc(t.transpose(dplt[:], lgT[:], identf[0:32, 0:32]))
                    blk.tensor(p2_pe2)
                    snapP2t2[i] = pe.n

                    def p2_dv5(v, i=i):
                        v.wait_ge(pe.h, snapP2t2[i])
                        dv.inc(v.tensor_copy(out=logits_tm[:, i, :], in_=dplt[:]))
                    blk.vector(p2_dv5)
                    snapP2l5[i] = dv.n
                qs_d1_end = qs.n
                snapD1_dv = dv.n

            phase_barrier()
        if upto >= 6:
            with _multi(
                nc.sbuf_tensor("probs", [128, NE], F32),
                nc.sbuf_tensor("vals8", [128, TOPK], F32),
                nc.sbuf_tensor("idxu", [128, TOPK], U32),
                nc.sbuf_tensor("r2c", [128, 16], F32),
                nc.sbuf_tensor("ohp", [128, 4, NE], F32),
                nc.sbuf_tensor("ohj", [128, NE], F32),
                nc.sbuf_tensor("ohjT", [32, 128], F32),
                nc.sbuf_tensor("scj", [128, 4], F32),
                nc.sbuf_tensor("scr32", [128, NE], F32),
                nc.sbuf_tensor("bstc", [128, 8], F32),
                nc.sbuf_tensor("carry", [1, NE], F32),
                nc.sbuf_tensor("ohsi", [128, NE], F32),
                nc.sbuf_tensor("poscol", [128, 8], F32),
                nc.psum_tensor("rpt", [32, 128], F32),
                nc.psum_tensor("rps1", [128, NE], F32),
                nc.psum_tensor("rppre", [128, NE], F32),
                nc.psum_tensor("rpcar", [1, NE], F32),
            ) as (probs, vals8, idxu, r2c, ohp, ohj, ohjT, scj, scr32, bstc, carry, ohsi, poscol, rpt, rps1, rppre, rpcar):
                snap_j_pe = [None]
                snap_j_dv = [None]
                for i in range(NTT):
                    def d2_dv1(v, i=i):
                        if i == 0:
                            v.wait_ge(dv.h, snapD1_dv)
                        v.tensor_reduce(out=r2c[:, 0:1], in_=logits_tm[:, i, :], axis=AX.X, op=ALU.max)
                        v.drain()
                        dv.inc(v.tensor_scalar(out=probs[:], in0=logits_tm[:, i, :],
                                               scalar1=r2c[:, 0:1], scalar2=None, op0=ALU.subtract))
                    blk.vector(d2_dv1)
                    s1 = dv.n

                    def d2_ac(s, i=i, s1=s1):
                        s.wait_ge(dv.h, s1)
                        ac.inc(s.activation(out=probs[:], in_=probs[:], func=AF.Exp,
                                            accum_out=r2c[:, 1:2]))
                    blk.scalar(d2_ac)
                    s2 = ac.n

                    def d2_dv2(v, i=i, s2=s2):
                        v.wait_ge(ac.h, s2)
                        v.reciprocal(out=r2c[:, 2:3], in_=r2c[:, 1:2])
                        v.drain()
                        v.tensor_scalar(out=probs[:], in0=probs[:], scalar1=r2c[:, 2:3], scalar2=None,
                                        op0=ALU.mult)
                        v.drain()
                        v.max(out=vals8[:], in_=probs[:])
                        v.drain()
                        v.max_index(out=idxu[:], in_max=vals8[:], in_values=probs[:])
                        v.drain()
                        v.tensor_copy(out=idx8_all[:, i, :], in_=idxu[:])
                        v.tensor_reduce(out=r2c[:, 3:4], in_=vals8[:], axis=AX.X, op=ALU.add)
                        v.drain()
                        v.tensor_scalar(out=r2c[:, 4:5], in0=r2c[:, 3:4], scalar1=1e-12, scalar2=None,
                                        op0=ALU.max)
                        v.drain()
                        v.reciprocal(out=r2c[:, 5:6], in_=r2c[:, 4:5])
                        v.drain()
                        v.tensor_scalar(out=w8_all[:, i, :], in0=vals8[:], scalar1=r2c[:, 5:6],
                                        scalar2=None, op0=ALU.mult)
                        for k in range(4):
                            li = v.tensor_scalar(out=ohp[:, k, :], in0=iota32[:],
                                                 scalar1=idx8_all[:, i, k:k + 1], scalar2=None,
                                                 op0=ALU.is_equal)
                        dv.inc(li)
                    blk.vector(d2_dv2)
                    s3 = dv.n

                    for j in range(4):
                        def d2_dvj(v, i=i, j=j, s3=s3):
                            if j == 0:
                                v.wait_ge(dv.h, s3)
                            if snap_j_pe[0] is not None:
                                v.wait_ge(pe.h, snap_j_pe[0])
                            dw.inc(v.tensor_scalar(out=ohj[:], in0=iota32[:],
                                                   scalar1=idx8_all[:, i, SSEL + j:SSEL + j + 1],
                                                   scalar2=None, op0=ALU.is_equal))
                        blk.vector(d2_dvj)
                        s4 = dw.n

                        def d2_pej(t, i=i, j=j, s4=s4):
                            t.wait_ge(dw.h, s4)
                            if snap_j_dv[0] is not None:
                                t.wait_ge(dv.h, snap_j_dv[0])
                            pe.inc(t.transpose(rpt[:], ohj[:], identf[:]))
                        blk.tensor(d2_pej)
                        s5 = pe.n

                        def d2_dvj3(v, i=i, j=j, s5=s5):
                            v.wait_ge(pe.h, s5)
                            dw.inc(v.tensor_copy(out=ohjT[:], in_=rpt[:]))
                        blk.vector(d2_dvj3)
                        s6 = dw.n

                        def d2_pej3(t, i=i, j=j, s6=s6):
                            t.wait_ge(dw.h, s6)
                            pe.inc(t.matmul(rps1[:], ohjT[:], simd[:], start=True, stop=True))
                        blk.tensor(d2_pej3)
                        snap_j_pe[0] = pe.n
                        s7 = pe.n

                        def d2_dvj4(v, i=i, j=j, s7=s7):
                            v.wait_ge(pe.h, s7)
                            for k in range(4):
                                v.tensor_tensor(out=scr32[:], in0=rps1[:], in1=ohp[:, k, :],
                                                op=ALU.mult)
                                v.drain()
                                v.tensor_reduce(out=scj[:, k:k + 1], in_=scr32[:], axis=AX.X,
                                                op=ALU.add)
                            v.drain()
                            v.tensor_tensor(out=bstc[:, 0:1], in0=scj[:, 0:1], in1=scj[:, 1:2],
                                            op=ALU.max)
                            v.tensor_tensor(out=bstc[:, 1:2], in0=scj[:, 2:3], in1=scj[:, 3:4],
                                            op=ALU.max)
                            v.drain()
                            v.tensor_tensor(out=bstc[:, 0:1], in0=bstc[:, 0:1], in1=bstc[:, 1:2],
                                            op=ALU.max)
                            v.drain()
                            v.tensor_scalar(out=bstc[:, 2:3], in0=scj[:, 0:1], scalar1=bstc[:, 0:1],
                                            scalar2=None, op0=ALU.is_ge)
                            v.drain()
                            v.tensor_scalar(out=bstc[:, 4:5], in0=bstc[:, 2:3], scalar1=-1.0,
                                            scalar2=1.0, op0=ALU.mult, op1=ALU.add)
                            v.tensor_tensor(out=bstc[:, 6:7], in0=idx8_all[:, i, 0:1], in1=bstc[:, 2:3],
                                            op=ALU.mult)
                            for k in range(1, 4):
                                v.drain()
                                v.tensor_scalar(out=bstc[:, 2:3], in0=scj[:, k:k + 1],
                                                scalar1=bstc[:, 0:1], scalar2=None, op0=ALU.is_ge)
                                v.drain()
                                v.tensor_tensor(out=bstc[:, 3:4], in0=bstc[:, 2:3], in1=bstc[:, 4:5],
                                                op=ALU.mult)
                                v.drain()
                                v.scalar_tensor_tensor(out=bstc[:, 6:7], in0=idx8_all[:, i, k:k + 1],
                                                       scalar=bstc[:, 3:4], in1=bstc[:, 6:7],
                                                       op0=ALU.mult, op1=ALU.add)
                                v.tensor_scalar(out=bstc[:, 5:6], in0=bstc[:, 3:4], scalar1=-1.0,
                                                scalar2=1.0, op0=ALU.mult, op1=ALU.add)
                                v.drain()
                                v.tensor_tensor(out=bstc[:, 4:5], in0=bstc[:, 4:5], in1=bstc[:, 5:6],
                                                op=ALU.mult)
                            v.drain()
                            v.tensor_scalar(out=bstc[:, 7:8], in0=bstc[:, 0:1], scalar1=THR,
                                            scalar2=None, op0=ALU.is_lt)
                            v.drain()
                            v.tensor_tensor(out=scr32[:, 0:1], in0=idx8_all[:, i, SSEL + j:SSEL + j + 1],
                                            in1=bstc[:, 7:8], op=ALU.mult)
                            v.tensor_scalar(out=bstc[:, 7:8], in0=bstc[:, 7:8], scalar1=-1.0,
                                            scalar2=1.0, op0=ALU.mult, op1=ALU.add)
                            v.drain()
                            v.tensor_tensor(out=scr32[:, 1:2], in0=bstc[:, 6:7], in1=bstc[:, 7:8],
                                            op=ALU.mult)
                            v.drain()
                            dv.inc(v.tensor_tensor(out=idx8_all[:, i, SSEL + j:SSEL + j + 1],
                                                   in0=scr32[:, 0:1], in1=scr32[:, 1:2], op=ALU.add))
                        blk.vector(d2_dvj4)
                        snap_j_dv[0] = dv.n
                snapD2route = dv.n

                def d2_ci(v):
                    v.wait_ge(dv.h, snapD2route)
                    dv.inc(v.memset(carry[:], 0.0))
                blk.vector(d2_ci)
                snapPcar = dv.n

                first_carry = True
                snapPpe = None
                for s_ in range(TOPK):
                    for i in range(NTT):
                        def p_dv1(v, s_=s_, i=i, snapPpe=snapPpe):
                            if snapPpe is not None:
                                v.wait_ge(pe.h, snapPpe)
                            dw.inc(v.tensor_scalar(out=ohsi[:], in0=iota32[:],
                                                   scalar1=idx8_all[:, i, s_:s_ + 1], scalar2=None,
                                                   op0=ALU.is_equal))
                        blk.vector(p_dv1)
                        sP1 = dw.n

                        def p_pe(t, s_=s_, i=i, sP1=sP1, fc=first_carry, snapPcar=snapPcar):
                            t.wait_ge(dw.h, sP1)
                            t.wait_ge(dv.h, snapPcar)
                            t.matmul(rppre[:], ustrict[:], ohsi[:], start=True, stop=False,
                                     skip_group_check=True)
                            t.matmul(rppre[:], ones_r1[:], carry[:], start=False, stop=True,
                                     skip_group_check=True)
                            pe.inc(t.matmul(rpcar[:], ones_c[:], ohsi[:], start=fc, stop=True,
                                            skip_group_check=True))
                        blk.tensor(p_pe)
                        first_carry = False
                        snapPpe = pe.n
                        sP2 = pe.n

                        def p_dv2(v, s_=s_, i=i, sP2=sP2):
                            v.wait_ge(pe.h, sP2)
                            v.tensor_tensor(out=scr32[:], in0=rppre[:], in1=ohsi[:], op=ALU.mult)
                            v.tensor_copy(out=carry[:], in_=rpcar[:])
                            v.tensor_reduce(out=poscol[:, 0:1], in_=scr32[:], axis=AX.X, op=ALU.add)
                            v.drain()
                            v.tensor_scalar(out=poscol[:, 1:2], in0=idx8_all[:, i, s_:s_ + 1],
                                            scalar1=float(CAP), scalar2=poscol[:, 0:1],
                                            op0=ALU.mult, op1=ALU.add)
                            v.tensor_scalar(out=poscol[:, 2:3], in0=poscol[:, 0:1], scalar1=float(CAP),
                                            scalar2=None, op0=ALU.is_ge)
                            v.drain()
                            v.tensor_scalar(out=poscol[:, 3:4], in0=poscol[:, 2:3], scalar1=-1.0,
                                            scalar2=1.0, op0=ALU.mult, op1=ALU.add)
                            v.tensor_tensor(out=poscol[:, 5:6], in0=trash[:], in1=poscol[:, 2:3],
                                            op=ALU.mult)
                            v.drain()
                            v.tensor_tensor(out=poscol[:, 4:5], in0=poscol[:, 1:2], in1=poscol[:, 3:4],
                                            op=ALU.mult)
                            v.drain()
                            v.tensor_tensor(out=poscol[:, 6:7], in0=poscol[:, 4:5], in1=poscol[:, 5:6],
                                            op=ALU.add)
                            v.drain()
                            dv.inc(v.tensor_copy(out=posu_all[:, s_ * NTT + i:s_ * NTT + i + 1],
                                                 in_=poscol[:, 6:7]))
                        blk.vector(p_dv2)
                        snapPcar = dv.n
                snapD2pos = dv.n

            phase_barrier()
        if upto >= 7:
            with nc.sbuf_tensor("lnrow", [128, 2, DM], BF16) as lnrow:
                qa_d3 = qa.n
                snapD3sc = [None] * NTT
                for i in range(NTT):
                    def d3_sy(s, i=i):
                        if i == 0:
                            s.wait_ge(dv.h, snapD2pos)
                            s.wait_ge(qs.h, qs_d1_end)
                        if i >= 2:
                            s.wait_ge(qi.h, snapD3sc[i - 2])
                        qa.inc(s.dma_start(out=lnrow[:, i % 2, :],
                                           in_=ln2_d[i * 128:(i + 1) * 128, :]), 16)
                    blk.sync(d3_sy)

                    def d3_gp(g, i=i):
                        g.wait_ge(qa.h, qa_d3 + 16 * (i + 1))
                        if i == 0:
                            g.wait_ge(dv.h, snapD2pos)
                        for s_ in range(TOPK):
                            qi.inc(g.indirect_dma_start(
                                out=xdisp_d[:],
                                out_offset=IOA(ap=posu_all[:, s_ * NTT + i:s_ * NTT + i + 1], axis=0),
                                in_=lnrow[:, i % 2, :], in_offset=None), 16)
                    blk.gpsimd(d3_gp)
                    snapD3sc[i] = qi.n
                qi_d3_end = qi.n

            phase_barrier()
        if upto >= 8:
            with _multi(
                nc.sbuf_tensor("xg", [128, CAP // 128, DM], BF16),
                nc.sbuf_tensor("xgT", [128, NKT, CAP], BF16),
                nc.sbuf_tensor("wgu", [128, 2, NKT, 128], BF16),
                nc.sbuf_tensor("wdt", [128, 6, DM], BF16),
                nc.sbuf_tensor("sg6", [128, 6, CAP], F32),
                nc.sbuf_tensor("htil", [128, 6, CAP], BF16),
                nc.sbuf_tensor("ytile", [128, 2, DM], BF16),
                nc.psum_tensor("gps", [128, 2, 512], F32),
                nc.psum_tensor("dnps", [128, 4, 512], F32),
                nc.psum_tensor("gptb", [128, 2, 1024], BF16),
            ) as (xg, xgT, wgu, wdt, sg6, htil, ytile, gps, dnps, gptb):
                def d4_z(v):
                    dv.inc(v.memset(ytile[:, 0, :], 0.0))
                blk.vector(d4_z)
                sZ = dv.n

                def d4_zs(s, sZ=sZ):
                    s.wait_ge(dv.h, sZ)
                    qs.inc(s.dma_start(out=ydisp_d[NDISP:NDISP + 128, :], in_=ytile[:, 0, :]), 16)
                blk.sync(d4_zs)
                snapD4zs = qs.n

                qa_d4 = qa.n
                qwsnapG = {}
                qwsnapD = {}
                snapD4xT = [None] * NE
                snapD4gu = {}
                snapD4guE = [None] * NE
                snapD4dnE = [None] * NE
                snapD4uend = [None] * NE
                snapD4sg = {}
                snapD4xc = [None] * NE
                snapD4y = {}
                snapD4st = {}
                snapD4gt = {}
                snapD4gc = {}
                evD4_before = {}

                def d4_gp0(g):
                    g.wait_ge(qi.h, qi_d3_end)
                    for mt in range(2):
                        qw.inc(g.dma_start(out=wgu[:, mt % 2, :, :],
                                           in_=w_gu_t[0, mt].rearrange("(kt kp) m -> kp kt m",
                                                                       kp=128)), 16)
                        qwsnapG[(0, mt)] = qw.n
                    qw.inc(g.dma_start(out=wdt[:],
                                       in_=w_d_t[0].rearrange("(kt kp) d -> kp kt d", kp=128)), 16)
                    qwsnapD[0] = qw.n
                blk.gpsimd(d4_gp0)

                for e in range(NE):
                    ep = e % 2

                    def d4_syl(s, e=e):
                        if e == 0:
                            s.wait_ge(qi.h, qi_d3_end)
                        else:
                            s.wait_ge(pe.h, snapD4xT[e - 1])
                        iv = xdisp_d[e * CAP:(e + 1) * CAP, :].rearrange(
                            "(rt p) d -> p rt d", p=128)
                        qa.inc(s.dma_start(out=xg[:], in_=iv), 16)
                    blk.sync(d4_syl)

                    for rt in range(CAP // 128):
                        for grp in range(2):
                            G = e * 16 + rt * 2 + grp
                            bk = G % 2
                            dw_g2 = snapD4gc.get(G - 2)

                            def d4_pex(t, e=e, ep=ep, rt=rt, grp=grp, bk=bk, dw_g2=dw_g2):
                                if rt == 0 and grp == 0:
                                    t.wait_ge(qa.h, qa_d4 + 16 * (e + 1))
                                if dw_g2 is not None:
                                    t.wait_ge(dw.h, dw_g2)
                                for kk in range(8):
                                    kt = grp * 8 + kk
                                    sl = gptb[:, bk, kk * 128:kk * 128 + 128]
                                    pe.inc(t.transpose(sl, xg[:, rt, kt * 128:(kt + 1) * 128],
                                                       identb[:]))
                            blk.tensor(d4_pex)
                            snapD4gt[G] = pe.n

                            def d4_dvx(v, e=e, ep=ep, rt=rt, grp=grp, bk=bk, G=G):
                                if rt == 0 and grp == 0 and e >= 1:
                                    v.wait_ge(pe.h, snapD4guE[e - 1])
                                v.wait_ge(pe.h, snapD4gt[G])
                                for kk in range(8):
                                    kt = grp * 8 + kk
                                    dw.inc(v.tensor_copy(
                                        out=xgT[:, kt, rt * 128:(rt + 1) * 128],
                                        in_=gptb[:, bk, kk * 128:kk * 128 + 128]))
                            blk.vector(d4_dvx)
                            snapD4gc[G] = dw.n
                    snapD4xT[e] = pe.n
                    snapD4xc[e] = dw.n

                    evD4_before[e] = ev.n
                    for mt in range(12):
                        def d4_pg(t, e=e, ep=ep, mt=mt):
                            t.wait_ge(qw.h, qwsnapG[(e, mt)])
                            if mt == 0:
                                t.wait_ge(dw.h, snapD4xc[e])
                            if mt >= 1:
                                t.wait_ge(ev.h, evD4_before[e] + 2 * mt)
                            for kt in range(NKT):
                                for nr in range(2):
                                    n0, n1 = (0, 512) if nr == 0 else (512, CAP)
                                    li = t.matmul(gps[:, nr, 0:n1 - n0], wgu[:, (e * 12 + mt) % 2, kt, :],
                                                  xgT[:, kt, n0:n1],
                                                  start=(kt == 0), stop=(kt == NKT - 1),
                                                  skip_group_check=True)
                            pe.inc(li)
                        blk.tensor(d4_pg)
                        snapD4gu[(e, mt)] = pe.n

                        def d4_gpw(g, e=e, mt=mt):
                            g.wait_ge(pe.h, snapD4gu[(e, mt)])
                            if mt + 2 < 12:
                                e2, m2 = e, mt + 2
                            else:
                                e2, m2 = e + 1, mt + 2 - 12
                            if e2 < NE:
                                qw.inc(g.dma_start(out=wgu[:, (e2 * 12 + m2) % 2, :, :],
                                                   in_=w_gu_t[e2, m2].rearrange(
                                                       "(kt kp) m -> kp kt m", kp=128)), 16)
                                qwsnapG[(e2, m2)] = qw.n
                        blk.gpsimd(d4_gpw)

                        if mt < 6:
                            def d4_ev1(s, e=e, mt=mt):
                                s.wait_ge(pe.h, snapD4gu[(e, mt)])
                                if e >= 1 and mt == 0:
                                    s.wait_ge(ev.h, evD4_before[e - 1] + 24)
                                for nr in range(2):
                                    n0, n1 = (0, 512) if nr == 0 else (512, CAP)
                                    i2 = s.activation(out=sg6[:, mt, n0:n1],
                                                      in_=gps[:, nr, 0:n1 - n0], func=AF.Silu)
                                    ev.inc(i2)
                            blk.scalar(d4_ev1)
                            snapD4sg[(e, mt)] = ev.n
                        else:
                            def d4_ev2(v, e=e, mt=mt):
                                v.wait_ge(pe.h, snapD4gu[(e, mt)])
                                v.wait_ge(ev.h, snapD4sg[(e, mt - 6)])
                                if e >= 1 and mt == 6:
                                    v.wait_ge(pe.h, snapD4dnE[e - 1])
                                for nr in range(2):
                                    n0, n1 = (0, 512) if nr == 0 else (512, CAP)
                                    i2 = v.tensor_tensor(out=htil[:, mt - 6, n0:n1],
                                                         in0=gps[:, nr, 0:n1 - n0],
                                                         in1=sg6[:, mt - 6, n0:n1],
                                                         op=ALU.mult)
                                    ev.inc(i2)
                            blk.vector(d4_ev2)
                    snapD4guE[e] = pe.n
                    snapD4uend[e] = ev.n

                    for rt in range(CAP // 128):
                        def d4_pd(t, e=e, ep=ep, rt=rt):
                            if rt == 0:
                                t.wait_ge(ev.h, snapD4uend[e])
                                t.wait_ge(qw.h, qwsnapD[e])
                                if e == 0:
                                    t.wait_ge(qs.h, snapD4zs)
                            if rt >= 1:
                                t.wait_ge(dv.h, snapD4y[(e, rt - 1)])
                            for kt in range(6):
                                for nd in range(4):
                                    li = t.matmul(dnps[:, nd, :], htil[:, kt, rt * 128:(rt + 1) * 128],
                                                  wdt[:, kt, nd * 512:(nd + 1) * 512],
                                                  start=(kt == 0), stop=(kt == 5),
                                                  skip_group_check=True)
                            pe.inc(li)
                        blk.tensor(d4_pd)
                        snapD4dn_rt = pe.n

                        def d4_dy(v, e=e, rt=rt, snapD4dn_rt=snapD4dn_rt):
                            v.wait_ge(pe.h, snapD4dn_rt)
                            rpe = CAP // 128
                            idx2 = e * rpe + rt
                            if idx2 >= 2:
                                pidx = idx2 - 2
                                v.wait_ge(qs.h, snapD4st[(pidx // rpe, pidx % rpe)])
                            for nd in range(4):
                                ci = v.tensor_copy(out=ytile[:, rt % 2, nd * 512:(nd + 1) * 512],
                                                   in_=dnps[:, nd, :])
                            dv.inc(ci)
                        blk.vector(d4_dy)
                        snapD4y[(e, rt)] = dv.n

                        def d4_sy(s, e=e, rt=rt):
                            s.wait_ge(dv.h, snapD4y[(e, rt)])
                            r0 = e * CAP + rt * 128
                            qs.inc(s.dma_start(out=ydisp_d[r0:r0 + 128, :], in_=ytile[:, rt % 2, :]), 16)
                        blk.sync(d4_sy)
                        snapD4st[(e, rt)] = qs.n
                    snapD4dnE[e] = pe.n

                    if e + 1 < NE:
                        def d4_gpd(g, e=e):
                            g.wait_ge(pe.h, snapD4dnE[e])
                            qw.inc(g.dma_start(out=wdt[:],
                                               in_=w_d_t[e + 1].rearrange(
                                                   "(kt kp) d -> kp kt d", kp=128)), 16)
                            qwsnapD[e + 1] = qw.n
                        blk.gpsimd(d4_gpd)
                qs_d4_end = qs.n

            phase_barrier()
        if upto >= 9:
            with _multi(
                nc.sbuf_tensor("x2acc", [128, 2, DM], F32),
                nc.sbuf_tensor("ygt", [128, 4, DM], BF16),
            ) as (x2acc, ygt):
                qa_d5 = qa.n
                qi_d5 = qi.n
                snapD5stt = {}
                snapD5out = {}
                for i in range(NTT):
                    def d5_sy(s, i=i):
                        if i == 0:
                            s.wait_ge(qs.h, qs_d4_end)
                        if i >= 2:
                            s.wait_ge(qs.h, snapD5out[i - 2])
                        qa.inc(s.dma_start(out=x2acc[:, i % 2, :],
                                           in_=x2_d[i * 128:(i + 1) * 128, :]), 16)
                    blk.sync(d5_sy)

                    for s_ in range(TOPK):
                        it = i * TOPK + s_

                        def d5_gp(g, i=i, s_=s_, it=it):
                            if it == 0:
                                g.wait_ge(qs.h, qs_d4_end)
                            if it >= 4:
                                g.wait_ge(dv.h, snapD5stt[it - 4])
                            qi.inc(g.indirect_dma_start(
                                out=ygt[:, it % 4, :], out_offset=None,
                                in_=ydisp_d[:],
                                in_offset=IOA(ap=posu_all[:, s_ * NTT + i:s_ * NTT + i + 1],
                                              axis=0)), 16)
                        blk.gpsimd(d5_gp)

                        def d5_dv(v, i=i, s_=s_, it=it):
                            if s_ == 0:
                                v.wait_ge(qa.h, qa_d5 + 16 * (i + 1))
                            v.wait_ge(qi.h, qi_d5 + 16 * (it + 1))
                            dv.inc(v.scalar_tensor_tensor(out=x2acc[:, i % 2, :],
                                                          in0=ygt[:, it % 4, :],
                                                          scalar=w8_all[:, i, s_:s_ + 1],
                                                          in1=x2acc[:, i % 2, :],
                                                          op0=ALU.mult, op1=ALU.add))
                            snapD5stt[it] = dv.n
                        blk.vector(d5_dv)
                    sD5 = dv.n

                    def d5_out(s, i=i, sD5=sD5):
                        s.wait_ge(dv.h, sD5)
                        qs.inc(s.dma_start(out=out_d[i * 128:(i + 1) * 128, :],
                                           in_=x2acc[:, i % 2, :]), 16)
                    blk.sync(d5_out)
                    snapD5out[i] = qs.n

                def d5_fin(s):
                    s.wait_ge(qs.h, qs.n)
                blk.sync(d5_fin)


        if 1 <= upto < 2:
            ln1_cm.__exit__(None, None, None)
        if upto < 3:
            for cm in reversed(mres_cm):
                cm.__exit__(None, None, None)

        def _tail_fin(s):
            for s2 in (qs, qa, qw, qi, qc):
                s.wait_ge(s2.h, s2.n)
        blk.sync(_tail_fin)

        if dbg:
            scr = {"x_tm_d": x_tm_d, "sgate_d": sgate_d, "hgn_d": hgn_d,
                   "ln2_d": ln2_d, "x2_d": x2_d, "xdisp_d": xdisp_d, "ydisp_d": ydisp_d}

            def dbg_sync(s):
                s.wait_ge(qs.h, qs.n)
                s.wait_ge(dv.h, dv.n)
                s.wait_ge(ac.h, ac.n)
                s.wait_ge(gp.h, gp.n)
                s.wait_ge(qi.h, qi.n)
                for name, shape, dt in dbg:
                    if name == "ln1xT":
                        continue
                    if name == "posu_all":
                        qs.inc(s.dma_start(out=dbg_outs[name][:], in_=posu_all[:]), 16)
                        continue
                    if name == "w8_all":
                        qs.inc(s.dma_start(out=dbg_outs[name][:],
                                           in_=w8_all.rearrange("p a b -> p (a b)")), 16)
                        continue
                    if name == "idx8_all":
                        qs.inc(s.dma_start(out=dbg_outs[name][:],
                                           in_=idx8_all.rearrange("p a b -> p (a b)")), 16)
                        continue
                    src = scr[name]
                    sl = tuple(slice(0, z) for z in shape)
                    qs.inc(s.dma_start(out=dbg_outs[name][:], in_=src.__getitem__(sl)), 16)
                s.wait_ge(qs.h, qs.n)
            blk.sync(dbg_sync)

    return nc


def _consts():
    c = {}
    c["ident_bf"] = np.eye(128).astype(ml_dtypes.bfloat16)
    c["ident_f"] = np.eye(128, dtype=np.float32)
    c["iota32"] = np.broadcast_to(np.arange(NE, dtype=np.float32), (128, NE)).copy()
    c["trash_c"] = (NDISP + np.arange(128, dtype=np.float32)).reshape(128, 1)
    m = np.empty((2, 128, CH), np.float32)
    for st in range(2):
        s_glob = st * 128 + np.arange(128)[:, None]
        t_glob = np.arange(CH)[None, :]
        m[st] = np.where(t_glob >= s_glob, 0.0, MASKNEG)
    c["mask_st"] = m
    c["ustrict"] = np.triu(np.ones((128, 128), np.float32), 1)
    c["ones_r1"] = np.ones((1, 128), np.float32)
    c["ones_c"] = np.ones((128, 1), np.float32)
    bc = np.zeros((64, HEADS, 128), np.float32)
    for h in range(HEADS):
        bc[h, h, :] = 1.0
    c["bcsel"] = bc.astype(ml_dtypes.bfloat16)
    c["noti"] = (1.0 - np.eye(NE)).astype(np.float32)
    c["i32d"] = np.eye(NE, dtype=np.float32)
    mc = np.zeros((128, 8), np.float32)
    mc[:, 0] = 1.0
    mc[:, 1] = EPS
    mc[:, 2] = 1.0 / DM
    mc[:, 3] = 1.0 / DIN
    mc[:, 4] = -1.0
    c["misc_c"] = mc
    return c


def _prep_weights(inp):
    d = {}
    wt = np.ascontiguousarray(inp["L0_attn_in_proj_w"].T)
    pad = np.zeros((DM, NMT * 128 - PROJ), np.float32)
    wt = np.concatenate([wt, pad], axis=1)
    d["w_in_t"] = np.ascontiguousarray(wt.reshape(DM, NMT, 128).transpose(1, 0, 2))
    d["conv_w_t"] = np.ascontiguousarray(inp["L0_attn_conv_w"].reshape(34, 128, KCONV).transpose(1, 0, 2))
    d["conv_b_t"] = np.ascontiguousarray(inp["L0_attn_conv_b"].reshape(34, 128).T)
    d["dt_bias_c"] = np.ascontiguousarray(inp["L0_attn_dt_bias"].reshape(64, 1))
    d["a_log_c"] = np.ascontiguousarray(inp["L0_attn_A_log"].reshape(64, 1))
    d["d_row"] = np.ascontiguousarray(inp["L0_attn_D"].reshape(1, 64))
    d["norm_w_t"] = np.ascontiguousarray(inp["L0_attn_norm_w"].reshape(32, 128).T)
    d["w_out_t"] = np.ascontiguousarray(inp["L0_attn_out_proj_w"].T.reshape(32, 128, DM))
    d["w_g_t"] = np.ascontiguousarray(inp["L0_ffn_gate_weight"].T.reshape(NKT, 128, NE))
    wgu = inp["L0_ffn_gate_up_proj"]
    d["w_gu_t"] = np.ascontiguousarray(
        wgu.transpose(0, 2, 1).reshape(NE, DM, 12, 128).transpose(0, 2, 1, 3))
    d["w_d_t"] = np.ascontiguousarray(inp["L0_ffn_down_proj"].transpose(0, 2, 1))
    d["sim"] = np.ascontiguousarray(inp["L0_ffn_similarity"])
    return d


def kernel(**inputs):
    x = inputs["L0_attn_hidden_states"]
    shared = _prep_weights(inputs)
    shared.update(_consts())
    in_maps = []
    for c in range(8):
        m = dict(shared)
        m["x"] = np.ascontiguousarray(x[c])
        in_maps.append(m)
    nc = build_nc()
    res = run_bass_kernel_spmd(nc, in_maps, core_ids=list(range(8)))
    return np.stack([r["out"] for r in res.results]).astype(np.float32)




# version 05
# speedup vs baseline: 1.2459x; 1.0290x over previous
import numpy as np
import ml_dtypes

import concourse.bass as bass
import concourse.mybir as mybir
from concourse.bass_utils import run_bass_kernel_spmd

F32 = mybir.dt.float32
BF16 = mybir.dt.bfloat16
U32 = mybir.dt.uint32
AF = mybir.ActivationFunctionType
ALU = mybir.AluOpType
AX = mybir.AxisListType
IOA = bass.IndirectOffsetOnAxis

L = 2048
DM = 2048
DIN = 4096
HEADS = 64
HD = 64
STATE = 128
KCONV = 4
CD = DIN + 2 * STATE
PROJ = DIN + CD + HEADS
NMT = 67
CH = 256
NCH = L // CH
EPS = 1e-5
NE = 32
TOPK = 8
SSEL = 4
THR = 0.5
DFF = 768
CAP = 896
NDISP = NE * CAP
NTT = L // 128
NKT = DM // 128
MASKNEG = -1e30



from contextlib import contextmanager, ExitStack


@contextmanager
def _multi(*cms):
    with ExitStack() as _st:
        yield [_st.enter_context(cm) for cm in cms]


class Sem:
    __slots__ = ("h", "n")

    def __init__(self, stack, nc, name):
        self.h = stack.enter_context(nc.semaphore(name))
        self.n = 0

    def inc(self, instr, k=1):
        instr.then_inc(self.h, k)
        self.n += k
        return instr


def build_nc(dbg=None, upto=99, serial_b=False):
    nc = bass.Bass("TRN2", target_bir_lowering=False)

    def dram(name, shape, dt, kind):
        return nc.dram_tensor(name, shape, dt, kind=kind).ap()

    x_in = dram("x", [L, DM], F32, "ExternalInput")
    w_in_t = dram("w_in_t", [NMT, DM, 128], F32, "ExternalInput")
    conv_w_t = dram("conv_w_t", [128, 34, KCONV], F32, "ExternalInput")
    conv_b_t = dram("conv_b_t", [128, 34], F32, "ExternalInput")
    dt_bias_c = dram("dt_bias_c", [64, 1], F32, "ExternalInput")
    a_log_c = dram("a_log_c", [64, 1], F32, "ExternalInput")
    d_row_i = dram("d_row", [1, 64], F32, "ExternalInput")
    norm_w_t = dram("norm_w_t", [128, 32], F32, "ExternalInput")
    w_out_t = dram("w_out_t", [32, 128, DM], F32, "ExternalInput")
    w_g_t = dram("w_g_t", [NKT, 128, NE], F32, "ExternalInput")
    w_gu_t = dram("w_gu_t", [NE, 12, DM, 128], F32, "ExternalInput")
    w_d_t = dram("w_d_t", [NE, DFF, DM], F32, "ExternalInput")
    sim_i = dram("sim", [NE, NE], F32, "ExternalInput")
    ident_bf_i = dram("ident_bf", [128, 128], BF16, "ExternalInput")
    ident_f_i = dram("ident_f", [128, 128], F32, "ExternalInput")
    iota32_i = dram("iota32", [128, NE], F32, "ExternalInput")
    trash_i = dram("trash_c", [128, 1], F32, "ExternalInput")
    mask_i = dram("mask_st", [2, 128, CH], F32, "ExternalInput")
    ustrict_i = dram("ustrict", [128, 128], F32, "ExternalInput")
    ones_r1_i = dram("ones_r1", [1, 128], F32, "ExternalInput")
    ones_c_i = dram("ones_c", [128, 1], F32, "ExternalInput")
    bcsel_i = dram("bcsel", [64, HEADS, 128], BF16, "ExternalInput")
    noti_i = dram("noti", [NE, NE], F32, "ExternalInput")
    i32_i = dram("i32d", [NE, NE], F32, "ExternalInput")
    misc_i = dram("misc_c", [128, 8], F32, "ExternalInput")

    x_tm_d = dram("x_tm_d", [L, DIN], BF16, "Internal")
    sgate_d = dram("sgate_d", [L, DIN], BF16, "Internal")
    hgn_d = dram("hgn_d", [32, 128, L], BF16, "Internal")
    ln2_d = dram("ln2_d", [L, DM], BF16, "Internal")
    x2_d = dram("x2_d", [L, DM], F32, "Internal")
    xdisp_d = dram("xdisp_d", [NDISP + 128, DM], BF16, "Internal")
    ydisp_d = dram("ydisp_d", [NDISP + 128, DM], BF16, "Internal")

    out_d = dram("out", [L, DM], F32, "ExternalOutput")
    dbg_outs = {}
    if dbg:
        for name, shape, dt in dbg:
            dbg_outs[name] = dram("dbg_" + name, shape, dt, "ExternalOutput")

    es = ExitStack()
    with es:
        def sb(name, shape, dt):
            return es.enter_context(nc.sbuf_tensor(name, shape, dt))

        identb = sb("identb", [128, 128], BF16)
        identf = sb("identf", [128, 128], F32)
        iota32 = sb("iota32s", [128, NE], F32)
        trash = sb("trashs", [128, 1], F32)
        maskst = sb("masksts", [128, 2, CH], F32)
        ustrict = sb("ustricts", [128, 128], F32)
        ones_r1 = sb("ones_r1s", [1, 128], F32)
        ones_c = sb("ones_cs", [128, 1], F32)
        convw = sb("convws", [128, 34, KCONV], F32)
        convb = sb("convbs", [128, 34], F32)
        dtbias = sb("dtbiass", [64, 1], F32)
        alogc = sb("alogcs", [64, 1], F32)
        acol = sb("acols", [64, 1], F32)
        drow = sb("drows", [1, 64], F32)
        dbc = sb("dbcs", [128, 64], F32)
        normw = sb("normws", [128, 32], F32)
        simd = sb("simds", [NE, NE], F32)
        simraw = sb("simraws", [NE, NE], F32)
        noti = sb("notis", [NE, NE], F32)
        i32d = sb("i32ds", [NE, NE], F32)
        miscc = sb("misccs", [128, 8], F32)
        wg_hi = sb("wg_hi", [128, NKT, NE], BF16)
        wg_lo = sb("wg_lo", [128, NKT, NE], BF16)
        wg_f = sb("wg_f", [128, NKT, NE], F32)
        bcsel = sb("bcsels", [64, HEADS, 128], BF16)

        logits_tm = sb("logits_tm", [128, NTT, NE], F32)
        w8_all = sb("w8_all", [128, NTT, TOPK], F32)
        idx8_all = sb("idx8_all", [128, NTT, TOPK], F32)
        posu_all = sb("posu_all", [128, TOPK * NTT], U32)

        names = ["qc", "qa", "qw", "qs", "qi", "pe", "dv", "dw", "ev", "ac", "gp"]
        S = {k: Sem(es, nc, "sem_" + k) for k in names}
        qc, qa, qw, qs, qi, pe, dv, dw, ev, ac, gp = (S[k] for k in names)

        blk = es.enter_context(nc.Block())


        def phase_barrier():
            bar = [(dv, dv.n), (dw, dw.n), (ev, ev.n), (ac, ac.n), (gp, gp.n),
                   (pe, pe.n), (qs, qs.n), (qi, qi.n), (qa, qa.n), (qw, qw.n),
                   (qc, qc.n)]

            def mk(eng):
                for s2, v2 in bar:
                    eng.wait_ge(s2.h, v2)
            blk.tensor(mk)
            blk.vector(mk)
            blk.scalar(mk)
            blk.gpsimd(mk)
            blk.sync(mk)

        def p0_sync(s):
            for dst, src in [
                (identb, ident_bf_i), (identf, ident_f_i), (iota32, iota32_i),
                (trash, trash_i), (ustrict, ustrict_i),
                (ones_r1, ones_r1_i), (ones_c, ones_c_i), (bcsel, bcsel_i),
                (dtbias, dt_bias_c), (alogc, a_log_c), (drow, d_row_i),
                (simraw, sim_i), (noti, noti_i), (i32d, i32_i), (miscc, misc_i),
            ]:
                qc.inc(s.dma_start(out=dst[:], in_=src[:]), 16)
            qc.inc(s.dma_start(out=maskst[:], in_=mask_i.rearrange("a p c -> p a c")), 16)
            qc.inc(s.dma_start(out=convw[:], in_=conv_w_t[:]), 16)
            qc.inc(s.dma_start(out=convb[:], in_=conv_b_t[:]), 16)
            qc.inc(s.dma_start(out=normw[:], in_=norm_w_t[:]), 16)
            qc.inc(s.dma_start(out=wg_f[:], in_=w_g_t.rearrange("a p c -> p a c")), 16)
        blk.sync(p0_sync)
        qc_all = qc.n

        p0ps_cm = nc.psum_tensor("p0ps", [128, 64], F32)
        p0ps = p0ps_cm.__enter__()

        def p0_pe(t):
            t.wait_ge(qc.h, qc_all)
            pe.inc(t.matmul(p0ps[:], ones_r1[:], drow[:], start=True, stop=True))
        blk.tensor(p0_pe)
        snap_p0pe = pe.n

        def p0_dv(v):
            v.wait_ge(qc.h, qc_all)
            v.wait_ge(pe.h, snap_p0pe)
            v.tensor_copy(out=dbc[:], in_=p0ps[:])
            v.tensor_tensor(out=simd[:], in0=simraw[:], in1=noti[:], op=ALU.mult)
            v.tensor_tensor(out=simd[:], in0=simd[:], in1=i32d[:], op=ALU.add)
            v.tensor_copy(out=wg_hi[:], in_=wg_f[:])
            dv.inc(v.tensor_tensor(out=wg_lo[:], in0=wg_f[:], in1=wg_hi[:], op=ALU.subtract))
        blk.vector(p0_dv)
        snap_p0dv = dv.n

        def p0_ac(s):
            s.wait_ge(qc.h, qc_all)
            s.activation(out=acol[:], in_=alogc[:], func=AF.Exp)
            ac.inc(s.mul(acol[:], acol[:], -1.0))
        blk.scalar(p0_ac)
        snap_p0ac = ac.n
        p0ps_cm.__exit__(None, None, None)

        mres_cm = [
            nc.sbuf_tensor("b_fm", [128, L], BF16),
            nc.sbuf_tensor("c_fm", [128, L], BF16),
            nc.sbuf_tensor("b_tm", [128, NTT, 128], BF16),
            nc.sbuf_tensor("dt_fm", [64, L], F32),
            nc.sbuf_tensor("da_fm", [64, L], F32),
            nc.sbuf_tensor("dt_tm", [128, NTT, 64], F32),
            nc.sbuf_tensor("s_all", [128, HEADS * HD], F32),
            nc.sbuf_tensor("s_bf", [128, HEADS * HD], BF16),
        ]
        b_fm, c_fm, b_tm, dt_fm, da_fm, dt_tm, s_all, s_bf = [cm.__enter__() for cm in mres_cm]

        if upto >= 1:
            ln1_cm = nc.sbuf_tensor("ln1xT", [128, NKT, L], BF16)
            ln1xT = ln1_cm.__enter__()
            with _multi(
                nc.sbuf_tensor("axt", [128, 2, DM], F32),
                nc.sbuf_tensor("axc", [128, 2, DM], F32),
                nc.sbuf_tensor("acl", [128, 8], F32),
                nc.sbuf_tensor("alnb", [128, 2, DM], BF16),
                nc.psum_tensor("aptb", [128, 2, 1024], BF16),
            ) as (axt, axc, acl, alnb, aptb):
                qa_a0 = qa.n
                snapA1 = [None] * NTT
                snapAT = [None] * NTT
                snapApt = {}
                snapApc = {}
                snapApc = {}
                for i in range(NTT):
                    p = i % 2

                    def a_sy(s, i=i, p=p):
                        if i >= 2:
                            s.wait_ge(dv.h, snapA1[i - 2])
                        qa.inc(s.dma_start(out=axt[:, p, :], in_=x_in[i * 128:(i + 1) * 128, :]), 16)
                    blk.sync(a_sy)

                    def a_dv1(v, i=i, p=p):
                        v.wait_ge(qa.h, qa_a0 + 16 * (i + 1))
                        v.tensor_reduce(out=acl[:, 0:1], in_=axt[:, p, :], axis=AX.X, op=ALU.add)
                        v.drain()
                        v.tensor_scalar(out=acl[:, 1:2], in0=acl[:, 0:1], scalar1=1.0 / DM,
                                        scalar2=None, op0=ALU.mult)
                        v.drain()
                        dv.inc(v.tensor_scalar(out=axc[:, p, :], in0=axt[:, p, :], scalar1=acl[:, 1:2],
                                               scalar2=None, op0=ALU.subtract))
                    blk.vector(a_dv1)
                    snapA1[i] = dv.n

                    def a_ac(s, i=i, p=p):
                        s.wait_ge(dv.h, snapA1[i])
                        if i >= 2:
                            s.wait_ge(pe.h, snapAT[i - 2])
                        s.activation(out=alnb[:, p, :], in_=axc[:, p, :], func=AF.Square,
                                     accum_out=acl[:, 4:5])
                        ac.inc(s.activation(out=acl[:, 5:6], in_=acl[:, 4:5], func=AF.Sqrt,
                                            scale=miscc[:, 2:3], bias=miscc[:, 1:2]))
                    blk.scalar(a_ac)
                    snapAac = ac.n

                    def a_dv2(v, i=i, p=p, snapAac=snapAac):
                        v.wait_ge(ac.h, snapAac)
                        if i >= 2:
                            v.wait_ge(pe.h, snapAT[i - 2])
                        v.reciprocal(out=acl[:, 6:7], in_=acl[:, 5:6])
                        v.drain()
                        dv.inc(v.tensor_scalar(out=alnb[:, p, :], in0=axc[:, p, :], scalar1=acl[:, 6:7],
                                               scalar2=None, op0=ALU.mult))
                    blk.vector(a_dv2)
                    snapA2 = dv.n

                    for grp in range(2):
                        G = i * 2 + grp
                        bk = G % 2
                        dw_g2 = snapApc.get(G - 2)

                        def a_pe(t, i=i, p=p, grp=grp, bk=bk, snapA2=snapA2, dw_g2=dw_g2):
                            if grp == 0:
                                t.wait_ge(dv.h, snapA2)
                            if dw_g2 is not None:
                                t.wait_ge(dw.h, dw_g2)
                            for kk in range(8):
                                kt = grp * 8 + kk
                                sl = aptb[:, bk, kk * 128:kk * 128 + 128]
                                pe.inc(t.transpose(sl, alnb[:, p, kt * 128:(kt + 1) * 128], identb[:]))
                        blk.tensor(a_pe)
                        snapApt[G] = pe.n

                        def a_dv3(v, i=i, p=p, grp=grp, bk=bk, G=G):
                            v.wait_ge(pe.h, snapApt[G])
                            for kk in range(8):
                                kt = grp * 8 + kk
                                dw.inc(v.tensor_copy(out=ln1xT[:, kt, i * 128:(i + 1) * 128],
                                                     in_=aptb[:, bk, kk * 128:kk * 128 + 128]))
                        blk.vector(a_dv3)
                        snapApc[G] = dw.n
                    snapAT[i] = pe.n
                snapA_dv = dv.n
                snapA_dw = dw.n
                if dbg and any(n == "ln1xT" for n, _, _ in dbg):
                    def a_dump(s):
                        s.wait_ge(dw.h, dw.n)
                        s.wait_ge(dv.h, dv.n)
                        s.wait_ge(pe.h, pe.n)
                        qs.inc(s.dma_start(out=dbg_outs["ln1xT"][:],
                                           in_=ln1xT.rearrange("p a b -> p (a b)")), 16)
                        s.wait_ge(qs.h, qs.n)
                    blk.sync(a_dump)

            phase_barrier()
        if upto >= 2:
            with _multi(
                nc.sbuf_tensor("wbuf", [128, 2, NKT, 128], BF16),
                nc.sbuf_tensor("gcx", [128, DM], BF16),
                nc.sbuf_tensor("stage", [128, 2, NKT, 128], BF16),
                nc.sbuf_tensor("convin", [128, 3 + L], F32),
                nc.sbuf_tensor("cacc", [128, L], F32),
                nc.psum_tensor("bps", [128, 4, 512], F32),
                nc.psum_tensor("bptb", [128, 2, 1024], BF16),
                nc.psum_tensor("bptf", [128, 2, 512], F32),
            ) as (wbuf, gcx, stage, convin, cacc, bps, bptb, bptf):
                dtr = convin[0:64, 3:3 + L]
                dtt1 = cacc[0:64, 0:L]

                def b_init(v):
                    dv.inc(v.memset(convin[:, 0:3], 0.0))
                blk.vector(b_init)
                snapBz = dv.n

                qw_b0 = qw.n
                qwsnapB = {}
                bgrp_ctr = [0]
                snapBgt = {}
                snapBgc = {}
                ev_b0 = ev.n
                evB_before = {}
                snapBmm = [None] * NMT
                snapBsilu = {}
                snapBT = {}
                snapBC = {}
                last_stage_dma = [None, None]
                last_gcx_T = [None]

                def sec_of(m):
                    return "gate" if m < 32 else ("xbc" if m < 66 else "dt")

                for m0 in range(2):
                    def b_gp0(g, m0=m0):
                        if m0 == 0:
                            g.wait_ge(qc.h, qc_all)
                        qw.inc(g.dma_start(out=wbuf[:, m0 % 2, :, :],
                                           in_=w_in_t[m0].rearrange("(kt kp) m -> kp kt m", kp=128)), 16)
                        qwsnapB[m0] = qw.n
                    blk.gpsimd(b_gp0)

                for m in range(NMT):
                    if serial_b:
                        phase_barrier()
                    sec = sec_of(m)
                    evB_before[m] = ev.n

                    def b_pe(t, m=m):
                        t.wait_ge(qw.h, qwsnapB[m])
                        if m == 0:
                            t.wait_ge(dv.h, snapA_dv)
                            t.wait_ge(dw.h, snapA_dw)
                        if m >= 1:
                            t.wait_ge(ev.h, evB_before[m - 1] + 4)
                        for kt in range(NKT):
                            for nt in range(4):
                                li = t.matmul(bps[:, nt, :], wbuf[:, m % 2, kt, :],
                                              ln1xT[:, kt, nt * 512:(nt + 1) * 512],
                                              start=(kt == 0), stop=(kt == NKT - 1),
                                              skip_group_check=True)
                        pe.inc(li)
                    blk.tensor(b_pe)
                    snapBmm[m] = pe.n

                    if m + 2 < NMT:
                        def b_gp(g, m=m):
                            g.wait_ge(pe.h, snapBmm[m])
                            qw.inc(g.dma_start(out=wbuf[:, (m + 2) % 2, :, :],
                                               in_=w_in_t[m + 2].rearrange("(kt kp) m -> kp kt m",
                                                                           kp=128)), 16)
                            qwsnapB[m + 2] = qw.n
                        blk.gpsimd(b_gp)

                    if sec == "gate":
                        def b_ac(s, m=m):
                            s.wait_ge(pe.h, snapBmm[m])
                            if last_gcx_T[0] is not None:
                                s.wait_ge(pe.h, last_gcx_T[0])
                            for nt in range(4):
                                i2 = s.activation(out=gcx[:, nt * 512:(nt + 1) * 512],
                                                  in_=bps[:, nt, :], func=AF.Silu)
                                ev.inc(i2)
                        blk.scalar(b_ac)
                        snapBsilu_ev = ev.n

                        for grp in range(2):
                            G = bgrp_ctr[0]
                            bgrp_ctr[0] += 1
                            bk = G % 2
                            dw_g2 = snapBgc.get(G - 2)

                            def b_pet(t, m=m, grp=grp, bk=bk, dw_g2=dw_g2, snapBsilu_ev=snapBsilu_ev):
                                if grp == 0:
                                    t.wait_ge(ev.h, snapBsilu_ev)
                                    if last_stage_dma[m % 2] is not None:
                                        t.wait_ge(qs.h, last_stage_dma[m % 2])
                                if dw_g2 is not None:
                                    t.wait_ge(dw.h, dw_g2)
                                for kk in range(8):
                                    tt = grp * 8 + kk
                                    sl = bptb[:, bk, kk * 128:kk * 128 + 128]
                                    pe.inc(t.transpose(sl, gcx[:, tt * 128:(tt + 1) * 128], identb[:]))
                            blk.tensor(b_pet)
                            snapBgt[G] = pe.n

                            def b_dvc(v, m=m, grp=grp, bk=bk, G=G):
                                v.wait_ge(pe.h, snapBgt[G])
                                for kk in range(8):
                                    tt = grp * 8 + kk
                                    dw.inc(v.tensor_copy(out=stage[:, m % 2, tt, :],
                                                         in_=bptb[:, bk, kk * 128:kk * 128 + 128]))
                            blk.vector(b_dvc)
                            snapBgc[G] = dw.n
                        last_gcx_T[0] = pe.n
                        snapBcp = dw.n

                        def b_sy(s, m=m, snapBcp=snapBcp):
                            s.wait_ge(dw.h, snapBcp)
                            od = sgate_d.rearrange("(tt p) f -> p tt f", p=128)
                            qs.inc(s.dma_start(out=od[:, :, m * 128:(m + 1) * 128],
                                               in_=stage[:, m % 2, :, :]), 16)
                        blk.sync(b_sy)
                        last_stage_dma[m % 2] = qs.n

                    elif sec == "xbc":
                        ct = m - 32

                        def b_dv(v, m=m, ct=ct):
                            v.wait_ge(pe.h, snapBmm[m])
                            if m == 32:
                                v.wait_ge(dv.h, snapBz)
                            for nt in range(4):
                                ev.inc(v.tensor_copy(out=convin[:, 3 + nt * 512:3 + (nt + 1) * 512],
                                                     in_=bps[:, nt, :]))
                            if ct > 0:
                                v.wait_ge(ac.h, snapBsilu[m - 1])
                            v.tensor_scalar(out=cacc[:], in0=convin[:, 0:L],
                                            scalar1=convw[:, ct, 0:1], scalar2=convb[:, ct:ct + 1],
                                            op0=ALU.mult, op1=ALU.add)
                            for kk in range(1, KCONV):
                                li = v.scalar_tensor_tensor(out=cacc[:], in0=convin[:, kk:kk + L],
                                                            scalar=convw[:, ct, kk:kk + 1], in1=cacc[:],
                                                            op0=ALU.mult, op1=ALU.add)
                            dv.inc(li)
                        blk.vector(b_dv)
                        snapBcv = dv.n

                        def b_ac(s, m=m, ct=ct, snapBcv=snapBcv):
                            s.wait_ge(dv.h, snapBcv)
                            if ct < 32 and last_gcx_T[0] is not None:
                                s.wait_ge(pe.h, last_gcx_T[0])
                            dst = b_fm if ct == 32 else (c_fm if ct == 33 else gcx)
                            ac.inc(s.activation(out=dst[:], in_=cacc[:], func=AF.Silu))
                        blk.scalar(b_ac)
                        snapBsilu[m] = ac.n

                        if ct <= 32:
                            for grp in range(2):
                                G = bgrp_ctr[0]
                                bgrp_ctr[0] += 1
                                bk = G % 2
                                dw_g2 = snapBgc.get(G - 2)

                                def b_pet(t, m=m, ct=ct, grp=grp, bk=bk, dw_g2=dw_g2):
                                    if grp == 0:
                                        t.wait_ge(ac.h, snapBsilu[m])
                                        if ct < 32 and last_stage_dma[m % 2] is not None:
                                            t.wait_ge(qs.h, last_stage_dma[m % 2])
                                    if dw_g2 is not None:
                                        t.wait_ge(dw.h, dw_g2)
                                    src = b_fm if ct == 32 else gcx
                                    for kk in range(8):
                                        tt = grp * 8 + kk
                                        sl = bptb[:, bk, kk * 128:kk * 128 + 128]
                                        pe.inc(t.transpose(sl, src[:, tt * 128:(tt + 1) * 128],
                                                           identb[:]))
                                blk.tensor(b_pet)
                                snapBgt[G] = pe.n

                                def b_dvc(v, m=m, ct=ct, grp=grp, bk=bk, G=G):
                                    v.wait_ge(pe.h, snapBgt[G])
                                    for kk in range(8):
                                        tt = grp * 8 + kk
                                        dst = b_tm[:, tt, :] if ct == 32 else stage[:, m % 2, tt, :]
                                        dw.inc(v.tensor_copy(
                                            out=dst, in_=bptb[:, bk, kk * 128:kk * 128 + 128]))
                                blk.vector(b_dvc)
                                snapBgc[G] = dw.n
                            if ct < 32:
                                last_gcx_T[0] = pe.n
                            snapBcp = dw.n

                            if ct < 32:
                                def b_sy(s, m=m, ct=ct, snapBcp=snapBcp):
                                    s.wait_ge(dw.h, snapBcp)
                                    od = x_tm_d.rearrange("(tt p) f -> p tt f", p=128)
                                    qs.inc(s.dma_start(out=od[:, :, ct * 128:(ct + 1) * 128],
                                                       in_=stage[:, m % 2, :, :]), 16)
                                blk.sync(b_sy)
                                last_stage_dma[m % 2] = qs.n
                    else:
                        def b_dv(v, m=m):
                            v.wait_ge(pe.h, snapBmm[m])
                            v.wait_ge(ac.h, snapBsilu[65])
                            for nt in range(4):
                                ev.inc(v.tensor_copy(out=dtr[:, nt * 512:(nt + 1) * 512],
                                                     in_=bps[0:64, nt, :]))
                            dv.inc(v.tensor_scalar(out=dtr[:], in0=dtr[:], scalar1=dtbias[:],
                                                   scalar2=None, op0=ALU.add))
                        blk.vector(b_dv)
                        snapBdt1 = dv.n

                        def b_ac(s, m=m, snapBdt1=snapBdt1):
                            s.wait_ge(dv.h, snapBdt1)
                            s.activation(out=dtt1[:], in_=dtr[:], func=AF.Abs)
                            s.activation(out=dtt1[:], in_=dtt1[:], func=AF.Exp, scale=miscc[0:64, 4:5])
                            ac.inc(s.activation(out=dtt1[:], in_=dtt1[:], func=AF.Ln,
                                                bias=miscc[0:64, 0:1]))
                        blk.scalar(b_ac)
                        snapBdt2 = ac.n

                        def b_dv2(v, m=m, snapBdt2=snapBdt2):
                            v.wait_ge(ac.h, snapBdt2)
                            v.wait_ge(ac.h, snap_p0ac)
                            v.tensor_scalar(out=dt_fm[:], in0=dtr[:], scalar1=0.0, scalar2=None,
                                            op0=ALU.max)
                            v.tensor_tensor(out=dt_fm[:], in0=dt_fm[:], in1=dtt1[:], op=ALU.add)
                            dv.inc(v.tensor_scalar(out=da_fm[:], in0=dt_fm[:], scalar1=acol[:],
                                                   scalar2=None, op0=ALU.mult))
                        blk.vector(b_dv2)
                        snapBdt3 = dv.n

                        for grp in range(2):
                            bk = grp % 2

                            def b_pe2(t, m=m, grp=grp, bk=bk, snapBdt3=snapBdt3):
                                if grp == 0:
                                    t.wait_ge(dv.h, snapBdt3)
                                for kk in range(8):
                                    tt = grp * 8 + kk
                                    pe.inc(t.transpose(bptf[:, bk, kk * 64:kk * 64 + 64],
                                                       dt_fm[:, tt * 128:(tt + 1) * 128],
                                                       identf[0:64, 0:64]))
                            blk.tensor(b_pe2)
                            sBt = pe.n

                            def b_dv3(v, m=m, grp=grp, bk=bk, sBt=sBt):
                                v.wait_ge(pe.h, sBt)
                                for kk in range(8):
                                    tt = grp * 8 + kk
                                    dw.inc(v.tensor_copy(out=dt_tm[:, tt, :],
                                                         in_=bptf[:, bk, kk * 64:kk * 64 + 64]))
                            blk.vector(b_dv3)

                snapB_dv = dv.n
                snapB_dw = dw.n
                snapB_ac = ac.n
                qs_b_end = qs.n
            ln1_cm.__exit__(None, None, None)

            phase_barrier()
        if upto >= 3:
            with _multi(
                nc.sbuf_tensor("xch", [128, 2, 2, DIN], BF16),
                nc.sbuf_tensor("sgch", [128, 2, DIN], BF16),
                nc.sbuf_tensor("dacum", [64, CH], F32),
                nc.sbuf_tensor("dachi", [64, CH], BF16),
                nc.sbuf_tensor("daclo", [64, CH], BF16),
                nc.sbuf_tensor("dacum_tm", [128, 2, 64], F32),
                nc.sbuf_tensor("expda_tm", [128, 2, 64], F32),
                nc.sbuf_tensor("zeros64", [64, CH], F32),
                nc.sbuf_tensor("segT", [128, 4, 2, CH], F32),
                nc.sbuf_tensor("mts", [128, 2, 2, CH], BF16),
                nc.sbuf_tensor("xdt", [128, 4, 2, HD], BF16),
                nc.sbuf_tensor("xdec", [128, 2, 2, HD], BF16),
                nc.sbuf_tensor("ydc", [128, 2, 2, HD], F32),
                nc.sbuf_tensor("xdd", [128, 8, 2, HD], BF16),
                nc.sbuf_tensor("y1c", [128, 2, 2, HD], F32),
                nc.sbuf_tensor("ych", [128, 2, DIN], BF16),
                nc.sbuf_tensor("ecol", [128, 64], F32),
                nc.sbuf_tensor("erow", [1, 64], F32),
                nc.sbuf_tensor("sqs", [128, DIN], BF16),
                nc.sbuf_tensor("rcol", [128, 8], F32),
                nc.sbuf_tensor("hstage", [128, 32, 128], BF16),
                nc.psum_tensor("cpsS", [128, 2, CH], F32),
                nc.psum_tensor("cpA", [128, 2, 512], F32),
                nc.psum_tensor("cpY", [128, 2, 512], F32),
                nc.psum_tensor("cpM", [128, 512], F32),
                nc.psum_tensor("cptb", [128, 2, 1024], BF16),
            ) as (xch, sgch, dacum, dachi, daclo, dacum_tm, expda_tm, zeros64, segT, mts, xdt, xdec, ydc, xdd, y1c, ych, ecol, erow, sqs, rcol, hstage, cpsS, cpA, cpY, cpM, cptb):
                def c_init(v):
                    v.wait_ge(qs.h, qs_b_end)
                    v.memset(s_all[:], 0.0)
                    v.memset(s_bf[:], 0.0)
                    dv.inc(v.memset(zeros64[:], 0.0))
                blk.vector(c_init)
                snapCini = dv.n

                qa_c0 = qa.n
                snapCgpend = [None] * NCH
                snapChgmul = [None] * NCH
                snapCsbf = [None] * NCH
                snapCh2last = [None] * NCH
                snapCecl = [None] * NCH
                snapCerow = [None] * NCH
                snapChgT = [None] * NCH
                snapCcprT = [None] * NCH
                last_hstage_dma = [None]
                cgrp_ctr = [0]
                snapCgt = {}
                snapCgc = {}

                xv = x_tm_d.rearrange("(c tt p) f -> p c tt f", p=128, tt=2)
                gv = sgate_d.rearrange("(c tt p) f -> p c tt f", p=128, tt=2)
                for c in range(NCH):
                    cp = c % 2

                    def c_sy(s, c=c, cp=cp):
                        if c == 0:
                            s.wait_ge(dv.h, snapCini)
                        if c >= 2:
                            s.wait_ge(gp.h, snapCgpend[c - 2])
                            s.wait_ge(dv.h, snapChgmul[c - 2])
                        qa.inc(s.dma_start(out=xch[:, cp, :, :], in_=xv[:, c, :, :]), 16)
                        if c >= 1:
                            s.wait_ge(dv.h, snapChgmul[c - 1])
                            s.wait_ge(pe.h, snapChgT[c - 1])
                        qa.inc(s.dma_start(out=sgch[:], in_=gv[:, c, :, :]), 16)
                    blk.sync(c_sy)

                    def c_scan(v, c=c):
                        v.wait_ge(qa.h, qa_c0 + 32 * (c + 1))
                        if c >= 1:
                            v.wait_ge(pe.h, snapCcprT[c - 1])
                        v.tensor_tensor_scan(out=dacum[:], data0=da_fm[:, c * CH:(c + 1) * CH],
                                             data1=zeros64[:], initial=0.0, op0=ALU.add, op1=ALU.add)
                        v.tensor_copy(out=dachi[:], in_=dacum[:])
                        dv.inc(v.tensor_tensor(out=daclo[:], in0=dacum[:], in1=dachi[:],
                                               op=ALU.subtract))
                    blk.vector(c_scan)
                    snapCscan = dv.n

                    def c_pe0(t, c=c, snapCscan=snapCscan):
                        t.wait_ge(dv.h, snapCscan)
                        if c == 0:
                            t.wait_ge(dv.h, snapB_dv)
                            t.wait_ge(dw.h, snapB_dw)
                            t.wait_ge(ac.h, snapB_ac)
                        if c >= 1:
                            t.wait_ge(ac.h, snapCerow[c - 1])
                        pe.inc(t.transpose(cpM[0:1, 128:192], dacum[:, CH - 1:CH], identf[0:64, 0:64]))
                    blk.tensor(c_pe0)
                    snapCcpr = pe.n

                    def c_ac0(s, c=c, snapCcpr=snapCcpr):
                        s.wait_ge(pe.h, snapCcpr)
                        ac.inc(s.activation(out=erow[:], in_=cpM[0:1, 128:192], func=AF.Exp))
                    blk.scalar(c_ac0)
                    snapCerow[c] = ac.n

                    def c_pe0b(t, c=c):
                        t.wait_ge(ac.h, snapCerow[c])
                        if c >= 1:
                            t.wait_ge(dv.h, snapCecl[c - 1])
                        pe.inc(t.matmul(cpM[:, 0:64], ones_r1[:], erow[:], start=True, stop=True))
                    blk.tensor(c_pe0b)
                    snapCpec = pe.n

                    def c_dve(v, c=c, snapCpec=snapCpec):
                        v.wait_ge(pe.h, snapCpec)
                        dv.inc(v.tensor_copy(out=ecol[:], in_=cpM[:, 0:64]))
                    blk.vector(c_dve)
                    snapCecl_pre = dv.n

                    for st in range(2):
                        dv_now = dv.n

                        def c_pt(t, c=c, st=st, snapCscan=snapCscan, dv_now=dv_now):
                            if st == 0:
                                t.wait_ge(dv.h, snapCscan)
                            t.wait_ge(dv.h, dv_now)
                            pe.inc(t.transpose(cpM[:, 64:128], dacum[:, st * 128:(st + 1) * 128],
                                               identf[0:64, 0:64]))
                        blk.tensor(c_pt)
                        sPT = pe.n

                        def c_ptc(v, c=c, st=st, sPT=sPT):
                            v.wait_ge(pe.h, sPT)
                            dv.inc(v.tensor_copy(out=dacum_tm[:, st, :], in_=cpM[:, 64:128]))
                        blk.vector(c_ptc)
                    snapCdat = dv.n

                    def c_act(s, c=c, snapCdat=snapCdat):
                        s.wait_ge(dv.h, snapCdat)
                        ac.inc(s.activation(out=expda_tm.rearrange("p a b -> p (a b)"),
                                            in_=dacum_tm.rearrange("p a b -> p (a b)"), func=AF.Exp))
                    blk.scalar(c_act)
                    snapCexp = ac.n

                    def c_psc(t, c=c, snapCdat=snapCdat):
                        t.wait_ge(dv.h, snapCdat)
                        if c >= 1:
                            t.wait_ge(dw.h, snapCh2last[c - 1])
                        for st in range(2):
                            li = t.matmul(cpsS[:, st, :],
                                          b_fm[:, c * CH + st * 128:c * CH + st * 128 + 128],
                                          c_fm[:, c * CH:(c + 1) * CH], start=True, stop=True)
                        pe.inc(li)
                    blk.tensor(c_psc)
                    snapCsc = pe.n

                    snapCh1 = {}
                    snapCh2 = {}
                    snapCh3 = {}
                    snapCpR = {}
                    snapCpY = {}
                    snapCgph = {}
                    snapCeseg = {}
                    def st_pe1(h):
                        hp = h % 2

                        def f(t, c=c, h=h, hp=hp):
                            if h == 0:
                                t.wait_ge(ac.h, snapCexp)
                                t.wait_ge(pe.h, snapCsc)
                                if c >= 1:
                                    t.wait_ge(dv.h, snapCsbf[c - 1])
                            if h >= 2:
                                t.wait_ge(dv.h, snapCh1[h - 2])
                            t.matmul(cpA[:, hp, 0:CH], bcsel[:, h, :], dachi[:],
                                     start=True, stop=False, skip_group_check=True)
                            pe.inc(t.matmul(cpA[:, hp, 0:CH], bcsel[:, h, :], daclo[:],
                                            start=False, stop=True, skip_group_check=True))
                        blk.tensor(f)
                        snapCpR[h] = pe.n

                    def st_dv1(h):
                        hp = h % 2
                        h4 = h % 4

                        def f(v, c=c, h=h, hp=hp, h4=h4):
                            v.wait_ge(pe.h, snapCpR[h])
                            if h >= 4:
                                v.wait_ge(ac.h, snapCeseg[h - 4])
                                v.wait_ge(pe.h, snapCpY[h - 4])
                            if h == 0:
                                v.drain()
                            for st in range(2):
                                v.scalar_tensor_tensor(out=segT[:, h4, st, :], in0=maskst[:, st, :],
                                                       scalar=dacum_tm[:, st, h:h + 1],
                                                       in1=cpA[:, hp, 0:CH],
                                                       op0=ALU.subtract, op1=ALU.add)
                            for st in range(2):
                                li = v.tensor_scalar(out=xdt[:, h4, st, :],
                                                     in0=xch[:, cp, st, h * HD:(h + 1) * HD],
                                                     scalar1=dt_tm[:, 2 * c + st, h:h + 1],
                                                     scalar2=None, op0=ALU.mult)
                            dv.inc(li)
                        blk.vector(f)
                        snapCh1[h] = dv.n

                    def st_ac1(h):
                        h4 = h % 4
                        h8 = h % 8

                        def f(s, c=c, h=h, h4=h4, h8=h8):
                            s.wait_ge(dv.h, snapCh1[h])
                            if h >= 4:
                                s.wait_ge(dw.h, snapCh2[h - 4])
                            if h >= 8:
                                s.wait_ge(gp.h, snapCgph[h - 8])
                            s.activation(out=xdd[:, h8, :, :],
                                         in_=xch[:, cp, :, h * HD:(h + 1) * HD],
                                         func=AF.Copy, scale=dbc[:, h:h + 1], bias=0.0)
                            ac.inc(s.activation(
                                out=segT.rearrange("p a b t -> p a (b t)")[:, h4, :],
                                in_=segT.rearrange("p a b t -> p a (b t)")[:, h4, :],
                                func=AF.Exp))
                        blk.scalar(f)
                        snapCeseg[h] = ac.n

                    def st_dv2(h):
                        hp = h % 2
                        h4 = h % 4

                        def f(v, c=c, h=h, hp=hp, h4=h4):
                            v.wait_ge(ac.h, snapCeseg[h])
                            if h >= 2:
                                v.wait_ge(pe.h, snapCpY[h - 2])
                            v.tensor_tensor(
                                out=mts.rearrange("p a b t -> p a (b t)")[:, hp, :],
                                in0=segT.rearrange("p a b t -> p a (b t)")[:, h4, :],
                                in1=cpsS.rearrange("p a t -> p (a t)"), op=ALU.mult)
                            for st in range(2):
                                li = v.tensor_scalar(out=xdec[:, hp, st, :], in0=xdt[:, h4, st, :],
                                                     scalar1=segT[:, h4, st, CH - 1:CH],
                                                     scalar2=None, op0=ALU.mult)
                            dw.inc(li)
                        blk.vector(f)
                        snapCh2[h] = dw.n

                    def st_pe2(h):
                        hp = h % 2
                        h4 = h % 4

                        def f(t, c=c, h=h, hp=hp, h4=h4):
                            t.wait_ge(dw.h, snapCh2[h])
                            if h >= 2:
                                t.wait_ge(dv.h, snapCh3[h - 2])
                            t.matmul(cpY[:, hp, 0:HD], mts[:, hp, 0, 0:128], xdt[:, h4, 0, :],
                                     start=True, stop=True, skip_group_check=True)
                            t.matmul(cpY[:, hp, HD:2 * HD], mts[:, hp, 0, 128:CH], xdt[:, h4, 0, :],
                                     start=True, stop=False, skip_group_check=True)
                            t.matmul(cpY[:, hp, HD:2 * HD], mts[:, hp, 1, 128:CH], xdt[:, h4, 1, :],
                                     start=False, stop=True, skip_group_check=True)
                            t.matmul(cpY[:, hp, 2 * HD:3 * HD], c_fm[:, c * CH:c * CH + 128],
                                     s_bf[:, h * HD:(h + 1) * HD], start=True, stop=True,
                                     skip_group_check=True)
                            t.matmul(cpY[:, hp, 3 * HD:4 * HD], c_fm[:, c * CH + 128:(c + 1) * CH],
                                     s_bf[:, h * HD:(h + 1) * HD], start=True, stop=True,
                                     skip_group_check=True)
                            t.matmul(cpA[:, hp, CH:CH + HD], b_tm[:, 2 * c, :], xdec[:, hp, 0, :],
                                     start=True, stop=False, skip_group_check=True)
                            pe.inc(t.matmul(cpA[:, hp, CH:CH + HD], b_tm[:, 2 * c + 1, :],
                                            xdec[:, hp, 1, :],
                                            start=False, stop=True, skip_group_check=True))
                        blk.tensor(f)
                        snapCpY[h] = pe.n

                    def st_dv3(h):
                        hp = h % 2

                        def f(v, c=c, h=h, hp=hp):
                            v.wait_ge(pe.h, snapCpY[h])
                            if h >= 2:
                                v.wait_ge(gp.h, snapCgph[h - 2])
                            v.tensor_copy(out=ydc.rearrange("p a b t -> p a (b t)")[:, hp, :],
                                          in_=cpY[:, hp, 0:2 * HD])
                            for st in range(2):
                                v.scalar_tensor_tensor(out=y1c[:, hp, st, :],
                                                       in0=cpY[:, hp, (2 + st) * HD:(3 + st) * HD],
                                                       scalar=expda_tm[:, st, h:h + 1],
                                                       in1=ydc[:, hp, st, :],
                                                       op0=ALU.mult, op1=ALU.add)
                            v.scalar_tensor_tensor(out=s_all[:, h * HD:(h + 1) * HD],
                                                   in0=s_all[:, h * HD:(h + 1) * HD],
                                                   scalar=ecol[:, h:h + 1],
                                                   in1=cpA[:, hp, CH:CH + HD],
                                                   op0=ALU.mult, op1=ALU.add)
                            dv.inc(v.tensor_copy(out=s_bf[:, h * HD:(h + 1) * HD],
                                                 in_=s_all[:, h * HD:(h + 1) * HD]))
                        blk.vector(f)
                        snapCh3[h] = dv.n

                    def st_gp(h):
                        hp = h % 2
                        h8 = h % 8

                        def f(g, c=c, h=h, hp=hp, h8=h8):
                            g.wait_ge(dv.h, snapCh3[h])
                            g.wait_ge(ac.h, snapCeseg[h])
                            if h == 0 and c >= 1:
                                g.wait_ge(dv.h, snapChgmul[c - 1])
                            gp.inc(g.tensor_tensor(
                                out=ych[:, :, h * HD:(h + 1) * HD],
                                in0=xdd.rearrange("p a b t -> p a (b t)")[:, h8, :],
                                in1=y1c.rearrange("p a b t -> p a (b t)")[:, hp, :],
                                op=ALU.add))
                        blk.gpsimd(f)
                        snapCgph[h] = gp.n

                    stages = [st_pe1, st_dv1, st_ac1, st_dv2, st_pe2, st_dv3, st_gp]
                    for slot in range(HEADS + len(stages) - 1):
                        for jst, fn in enumerate(stages):
                            hh = slot - jst
                            if 0 <= hh < HEADS:
                                fn(hh)
                    snapCh2last[c] = snapCh2[HEADS - 1]
                    snapCgpend[c] = gp.n
                    snapCsbf[c] = snapCh3[HEADS - 1]
                    snapCecl[c] = snapCecl_pre
                    snapCcprT[c] = snapCpR[HEADS - 1]

                    def c_hg(v, c=c, cp=cp):
                        v.wait_ge(gp.h, snapCgpend[c])
                        for st in range(2):
                            li = v.tensor_tensor(out=ych[:, st, :], in0=ych[:, st, :],
                                                 in1=sgch[:, st, :], op=ALU.mult)
                        dv.inc(li)
                    blk.vector(c_hg)
                    snapChgmul[c] = dv.n

                    def c_hga(s, c=c):
                        s.wait_ge(dv.h, snapChgmul[c])
                        for st in range(2):
                            s.activation(out=sqs[:], in_=ych[:, st, :], func=AF.Square,
                                         accum_out=rcol[:, st:st + 1])
                        for st in range(2):
                            li = s.activation(out=rcol[:, 2 + st:3 + st], in_=rcol[:, st:st + 1],
                                              func=AF.Sqrt, scale=miscc[:, 3:4], bias=miscc[:, 1:2])
                        ac.inc(li)
                    blk.scalar(c_hga)
                    snapChga = ac.n

                    def c_hg2(v, c=c, snapChga=snapChga):
                        v.wait_ge(ac.h, snapChga)
                        for st in range(2):
                            v.reciprocal(out=rcol[:, 4 + st:5 + st], in_=rcol[:, 2 + st:3 + st])
                        v.drain()
                        for st in range(2):
                            li = v.tensor_scalar(out=sgch[:, st, :], in0=ych[:, st, :],
                                                 scalar1=rcol[:, 4 + st:5 + st], scalar2=None,
                                                 op0=ALU.mult)
                        dv.inc(li)
                    blk.vector(c_hg2)
                    snapChgn = dv.n

                    snapCTg = {}
                    for st in range(2):
                        for grp in range(4):
                            G = cgrp_ctr[0]
                            cgrp_ctr[0] += 1
                            bk = G % 2
                            dw_g2 = snapCgc.get(G - 2)

                            def c_pet2(t, c=c, st=st, grp=grp, bk=bk, dw_g2=dw_g2, snapChgn=snapChgn):
                                if grp == 0:
                                    if st == 0:
                                        t.wait_ge(dv.h, snapChgn)
                                    if last_hstage_dma[0] is not None:
                                        t.wait_ge(qs.h, last_hstage_dma[0])
                                if dw_g2 is not None:
                                    t.wait_ge(dw.h, dw_g2)
                                for kk in range(8):
                                    ft = grp * 8 + kk
                                    sl = cptb[:, bk, kk * 128:kk * 128 + 128]
                                    pe.inc(t.transpose(sl, sgch[:, st, ft * 128:(ft + 1) * 128],
                                                       identb[:]))
                            blk.tensor(c_pet2)
                            snapCgt[G] = pe.n

                            def c_dvc2(v, c=c, st=st, grp=grp, bk=bk, G=G):
                                v.wait_ge(pe.h, snapCgt[G])
                                for kk in range(8):
                                    ft = grp * 8 + kk
                                    dw.inc(v.tensor_scalar(
                                        out=hstage[:, ft, :],
                                        in0=cptb[:, bk, kk * 128:kk * 128 + 128],
                                        scalar1=normw[:, ft:ft + 1], scalar2=None, op0=ALU.mult))
                            blk.vector(c_dvc2)
                            snapCgc[G] = dw.n
                        if st == 1:
                            snapChgT[c] = pe.n
                        snapChst = dw.n

                        def c_sy2(s, c=c, st=st, snapChst=snapChst):
                            s.wait_ge(dw.h, snapChst)
                            t0 = c * CH + st * 128
                            qs.inc(s.dma_start(out=hgn_d[:, :, t0:t0 + 128].rearrange("ft p t -> p ft t"),
                                               in_=hstage[:]), 16)
                        blk.sync(c_sy2)
                        last_hstage_dma[0] = qs.n
                qs_c_end = qs.n
            for cm in reversed(mres_cm):
                cm.__exit__(None, None, None)

            phase_barrier()
        if upto >= 4:
            hv = hgn_d.rearrange("ft p t -> p ft t")
            with _multi(
                nc.sbuf_tensor("wo1", [128, 32, 1024], BF16),
                nc.sbuf_tensor("hgt1", [128, 2, 32, 128], BF16),
                nc.sbuf_tensor("xr1", [128, 2, 1024], F32),
                nc.sbuf_tensor("o1t", [128, 2, 1024], F32),
                nc.psum_tensor("p1ps", [128, 2, 2, 512], F32),
            ) as (wo1, hgt1, xr1, o1t, p1ps):
                def p1_w(g):
                    g.wait_ge(qs.h, qs_c_end)
                    for gq in range(4):
                        qw.inc(g.dma_start(
                            out=wo1[:, gq * 8:(gq + 1) * 8, :],
                            in_=w_out_t[gq * 8:(gq + 1) * 8, :, 0:1024].rearrange("ft p d -> p ft d")), 16)
                blk.gpsimd(p1_w)
                qw_p1 = qw.n

                qa_p1 = qa.n
                snapP1mm = [None] * NTT
                snapP1ev = [None] * NTT
                snapP1st = [None] * NTT
                for i in range(NTT):
                    p = i % 2

                    def p1_sy(s, i=i, p=p):
                        if i == 0:
                            s.wait_ge(qs.h, qs_c_end)
                        if i >= 2:
                            s.wait_ge(pe.h, snapP1mm[i - 2])
                            s.wait_ge(dv.h, snapP1ev[i - 2])
                            s.wait_ge(qs.h, snapP1st[i - 2])
                        qa.inc(s.dma_start(out=hgt1[:, p, :, :], in_=hv[:, :, i * 128:(i + 1) * 128]), 16)
                        qa.inc(s.dma_start(out=xr1[:, p, :],
                                           in_=x_in[i * 128:(i + 1) * 128, 0:1024]), 16)
                    blk.sync(p1_sy)

                    def p1_pe(t, i=i, p=p):
                        t.wait_ge(qa.h, qa_p1 + 32 * i + 16)
                        if i == 0:
                            t.wait_ge(qw.h, qw_p1)
                        if i >= 2:
                            t.wait_ge(dv.h, snapP1ev[i - 2])
                        for ft in range(32):
                            for nd in range(2):
                                li = t.matmul(p1ps[:, p, nd, :], hgt1[:, p, ft, :],
                                              wo1[:, ft, nd * 512:(nd + 1) * 512],
                                              start=(ft == 0), stop=(ft == 31),
                                              skip_group_check=True)
                        pe.inc(li)
                    blk.tensor(p1_pe)
                    snapP1mm[i] = pe.n

                    def p1_dv(v, i=i, p=p):
                        v.wait_ge(pe.h, snapP1mm[i])
                        v.wait_ge(qa.h, qa_p1 + 32 * (i + 1))
                        for nd in range(2):
                            li = v.tensor_tensor(out=o1t[:, p, nd * 512:(nd + 1) * 512],
                                                 in0=p1ps[:, p, nd, :],
                                                 in1=xr1[:, p, nd * 512:(nd + 1) * 512], op=ALU.add)
                        dv.inc(li)
                    blk.vector(p1_dv)
                    snapP1ev[i] = dv.n

                    def p1_st(s, i=i, p=p):
                        s.wait_ge(dv.h, snapP1ev[i])
                        qs.inc(s.dma_start(out=x2_d[i * 128:(i + 1) * 128, 0:1024], in_=o1t[:, p, :]), 16)
                    blk.sync(p1_st)
                    snapP1st[i] = qs.n
                qs_p1_end = qs.n

            phase_barrier()
        if upto >= 5:
            with _multi(
                nc.sbuf_tensor("wo2", [128, 32, 1024], BF16),
                nc.sbuf_tensor("hgt2", [128, 2, 32, 128], BF16),
                nc.sbuf_tensor("xr2", [128, 2, 1024], F32),
                nc.sbuf_tensor("x2t", [128, 2, DM], F32),
                nc.sbuf_tensor("ln2f", [128, DM], F32),
                nc.sbuf_tensor("ln2h", [128, 2, DM], BF16),
                nc.sbuf_tensor("ln2lo", [128, 2, DM], BF16),
                nc.sbuf_tensor("lnT", [128, 2, NKT, 128], BF16),
                nc.sbuf_tensor("dcl", [128, 8], F32),
                nc.sbuf_tensor("lgT", [32, 128], F32),
                nc.psum_tensor("p2ps", [128, 2, 2, 512], F32),
                nc.psum_tensor("dptb", [128, 2, 1024], BF16),
                nc.psum_tensor("dplg", [32, 128], F32),
                nc.psum_tensor("dplt", [128, 32], F32),
            ) as (wo2, hgt2, xr2, x2t, ln2f, ln2h, ln2lo, lnT, dcl, lgT, p2ps, dptb, dplg, dplt):
                def p2_w(g):
                    g.wait_ge(qs.h, qs_p1_end)
                    for gq in range(4):
                        qw.inc(g.dma_start(
                            out=wo2[:, gq * 8:(gq + 1) * 8, :],
                            in_=w_out_t[gq * 8:(gq + 1) * 8, :, 1024:2048].rearrange(
                                "ft p d -> p ft d")), 16)
                blk.gpsimd(p2_w)
                qw_p2 = qw.n

                qa_p2 = qa.n
                snapP2mm = [None] * NTT
                snapP2x2 = [None] * NTT
                snapP2b = [None] * NTT
                snapP2st = [None] * NTT
                snapP2T = [None] * NTT
                snapP2r = [None] * NTT
                snapP2lg = [None] * NTT
                snapP2t2 = [None] * NTT
                snapP2l5 = [None] * NTT
                snapP2gt = {}
                snapP2gc = {}
                for i in range(NTT):
                    p = i % 2

                    def p2_sy(s, i=i, p=p):
                        if i == 0:
                            s.wait_ge(qs.h, qs_p1_end)
                        if i >= 2:
                            s.wait_ge(pe.h, snapP2mm[i - 2])
                            s.wait_ge(dv.h, snapP2x2[i - 2])
                            s.wait_ge(qs.h, snapP2st[i - 2])
                        qa.inc(s.dma_start(out=hgt2[:, p, :, :], in_=hv[:, :, i * 128:(i + 1) * 128]), 16)
                        qa.inc(s.dma_start(out=xr2[:, p, :],
                                           in_=x_in[i * 128:(i + 1) * 128, 1024:2048]), 16)
                        qa.inc(s.dma_start(out=x2t[:, p, 0:1024],
                                           in_=x2_d[i * 128:(i + 1) * 128, 0:1024]), 16)
                    blk.sync(p2_sy)

                    def p2_pe(t, i=i, p=p):
                        t.wait_ge(qa.h, qa_p2 + 48 * i + 16)
                        if i == 0:
                            t.wait_ge(qw.h, qw_p2)
                        if i >= 2:
                            t.wait_ge(dv.h, snapP2x2[i - 2])
                        for ft in range(32):
                            for nd in range(2):
                                li = t.matmul(p2ps[:, p, nd, :], hgt2[:, p, ft, :],
                                              wo2[:, ft, nd * 512:(nd + 1) * 512],
                                              start=(ft == 0), stop=(ft == 31),
                                              skip_group_check=True)
                        pe.inc(li)
                    blk.tensor(p2_pe)
                    snapP2mm[i] = pe.n

                    def p2_dv1(v, i=i, p=p):
                        v.wait_ge(pe.h, snapP2mm[i])
                        v.wait_ge(qa.h, qa_p2 + 48 * (i + 1))
                        for nd in range(2):
                            v.tensor_tensor(out=x2t[:, p, 1024 + nd * 512:1024 + (nd + 1) * 512],
                                            in0=p2ps[:, p, nd, :],
                                            in1=xr2[:, p, nd * 512:(nd + 1) * 512], op=ALU.add)
                        v.tensor_reduce(out=dcl[:, 0:1], in_=x2t[:, p, :], axis=AX.X, op=ALU.add)
                        v.drain()
                        v.tensor_scalar(out=dcl[:, 1:2], in0=dcl[:, 0:1], scalar1=1.0 / DM,
                                        scalar2=None, op0=ALU.mult)
                        v.drain()
                        dv.inc(v.tensor_scalar(out=ln2f[:], in0=x2t[:, p, :], scalar1=dcl[:, 1:2],
                                               scalar2=None, op0=ALU.subtract))
                    blk.vector(p2_dv1)
                    snapP2x2[i] = dv.n

                    def p2_ac(s, i=i, p=p):
                        s.wait_ge(dv.h, snapP2x2[i])
                        if i >= 2:
                            s.wait_ge(pe.h, snapP2T[i - 2])
                        s.activation(out=ln2lo[:, p, :], in_=ln2f[:], func=AF.Square,
                                     accum_out=dcl[:, 2:3])
                        ac.inc(s.activation(out=dcl[:, 3:4], in_=dcl[:, 2:3], func=AF.Sqrt,
                                            scale=miscc[:, 2:3], bias=miscc[:, 1:2]))
                    blk.scalar(p2_ac)
                    snapP2ac = ac.n

                    def p2_dv2(v, i=i, p=p, snapP2ac=snapP2ac):
                        v.wait_ge(ac.h, snapP2ac)
                        if i >= 2:
                            v.wait_ge(pe.h, snapP2T[i - 2])
                        v.reciprocal(out=dcl[:, 4:5], in_=dcl[:, 3:4])
                        v.drain()
                        v.tensor_scalar(out=ln2f[:], in0=ln2f[:], scalar1=dcl[:, 4:5], scalar2=None,
                                        op0=ALU.mult)
                        v.tensor_copy(out=ln2h[:, p, :], in_=ln2f[:])
                        dv.inc(v.tensor_tensor(out=ln2lo[:, p, :], in0=ln2f[:], in1=ln2h[:, p, :],
                                               op=ALU.subtract))
                    blk.vector(p2_dv2)
                    snapP2b[i] = dv.n

                    def p2_st(s, i=i, p=p):
                        s.wait_ge(dv.h, snapP2b[i])
                        qs.inc(s.dma_start(out=x2_d[i * 128:(i + 1) * 128, 1024:2048],
                                           in_=x2t[:, p, 1024:2048]), 16)
                        qs.inc(s.dma_start(out=ln2_d[i * 128:(i + 1) * 128, :], in_=ln2h[:, p, :]), 16)
                    blk.sync(p2_st)
                    snapP2st[i] = qs.n

                    for grp in range(4):
                        G = i * 4 + grp
                        bk = G % 2
                        dw_g2 = snapP2gc.get(G - 2)

                        def p2_pet(t, i=i, p=p, grp=grp, bk=bk, dw_g2=dw_g2):
                            if grp == 0:
                                t.wait_ge(dv.h, snapP2b[i])
                            if dw_g2 is not None:
                                t.wait_ge(dw.h, dw_g2)
                            for kk in range(8):
                                gg = grp * 8 + kk
                                kt, half = gg // 2, gg % 2
                                src = ln2h if half == 0 else ln2lo
                                sl = dptb[:, bk, kk * 128:kk * 128 + 128]
                                pe.inc(t.transpose(sl, src[:, p, kt * 128:(kt + 1) * 128], identb[:]))
                        blk.tensor(p2_pet)
                        snapP2gt[G] = pe.n

                        def p2_dv3(v, i=i, p=p, grp=grp, bk=bk, G=G):
                            if grp == 0 and i >= 1:
                                v.wait_ge(pe.h, snapP2r[i - 1])
                            v.wait_ge(pe.h, snapP2gt[G])
                            for kk in range(8):
                                gg = grp * 8 + kk
                                kt, half = gg // 2, gg % 2
                                dw.inc(v.tensor_copy(out=lnT[:, half, kt, :],
                                                     in_=dptb[:, bk, kk * 128:kk * 128 + 128]))
                        blk.vector(p2_dv3)
                        snapP2gc[G] = dw.n
                    snapP2T[i] = pe.n
                    snapP2c = dw.n

                    def p2_per(t, i=i, snapP2c=snapP2c):
                        t.wait_ge(dw.h, snapP2c)
                        if i >= 1:
                            t.wait_ge(dv.h, snapP2lg[i - 1])
                        first = True
                        for kt in range(NKT):
                            t.matmul(dplg[:], wg_hi[:, kt, :], lnT[:, 0, kt, :], start=first, stop=False)
                            first = False
                            t.matmul(dplg[:], wg_lo[:, kt, :], lnT[:, 0, kt, :], start=False, stop=False)
                            li = t.matmul(dplg[:], wg_hi[:, kt, :], lnT[:, 1, kt, :], start=False,
                                          stop=(kt == NKT - 1))
                        pe.inc(li)
                    blk.tensor(p2_per)
                    snapP2r[i] = pe.n

                    def p2_dv4(v, i=i):
                        v.wait_ge(pe.h, snapP2r[i])
                        if i >= 1:
                            v.wait_ge(pe.h, snapP2t2[i - 1])
                        dv.inc(v.tensor_copy(out=lgT[:], in_=dplg[:]))
                    blk.vector(p2_dv4)
                    snapP2lg[i] = dv.n

                    def p2_pe2(t, i=i):
                        t.wait_ge(dv.h, snapP2lg[i])
                        if i >= 1:
                            t.wait_ge(dv.h, snapP2l5[i - 1])
                        pe.inc(t.transpose(dplt[:], lgT[:], identf[0:32, 0:32]))
                    blk.tensor(p2_pe2)
                    snapP2t2[i] = pe.n

                    def p2_dv5(v, i=i):
                        v.wait_ge(pe.h, snapP2t2[i])
                        dv.inc(v.tensor_copy(out=logits_tm[:, i, :], in_=dplt[:]))
                    blk.vector(p2_dv5)
                    snapP2l5[i] = dv.n
                qs_d1_end = qs.n
                snapD1_dv = dv.n

            phase_barrier()
        if upto >= 6:
            with _multi(
                nc.sbuf_tensor("probs", [128, NE], F32),
                nc.sbuf_tensor("vals8", [128, TOPK], F32),
                nc.sbuf_tensor("idxu", [128, TOPK], U32),
                nc.sbuf_tensor("r2c", [128, 16], F32),
                nc.sbuf_tensor("ohp", [128, 4, NE], F32),
                nc.sbuf_tensor("ohj", [128, NE], F32),
                nc.sbuf_tensor("ohjT", [32, 128], F32),
                nc.sbuf_tensor("scj", [128, 4], F32),
                nc.sbuf_tensor("scr32", [128, NE], F32),
                nc.sbuf_tensor("bstc", [128, 8], F32),
                nc.sbuf_tensor("carry", [1, NE], F32),
                nc.sbuf_tensor("ohsi", [128, NE], F32),
                nc.sbuf_tensor("poscol", [128, 8], F32),
                nc.psum_tensor("rpt", [32, 128], F32),
                nc.psum_tensor("rps1", [128, NE], F32),
                nc.psum_tensor("rppre", [128, NE], F32),
                nc.psum_tensor("rpcar", [1, NE], F32),
            ) as (probs, vals8, idxu, r2c, ohp, ohj, ohjT, scj, scr32, bstc, carry, ohsi, poscol, rpt, rps1, rppre, rpcar):
                snap_j_pe = [None]
                snap_j_dv = [None]
                for i in range(NTT):
                    def d2_dv1(v, i=i):
                        if i == 0:
                            v.wait_ge(dv.h, snapD1_dv)
                        v.tensor_reduce(out=r2c[:, 0:1], in_=logits_tm[:, i, :], axis=AX.X, op=ALU.max)
                        v.drain()
                        dv.inc(v.tensor_scalar(out=probs[:], in0=logits_tm[:, i, :],
                                               scalar1=r2c[:, 0:1], scalar2=None, op0=ALU.subtract))
                    blk.vector(d2_dv1)
                    s1 = dv.n

                    def d2_ac(s, i=i, s1=s1):
                        s.wait_ge(dv.h, s1)
                        ac.inc(s.activation(out=probs[:], in_=probs[:], func=AF.Exp,
                                            accum_out=r2c[:, 1:2]))
                    blk.scalar(d2_ac)
                    s2 = ac.n

                    def d2_dv2(v, i=i, s2=s2):
                        v.wait_ge(ac.h, s2)
                        v.reciprocal(out=r2c[:, 2:3], in_=r2c[:, 1:2])
                        v.drain()
                        v.tensor_scalar(out=probs[:], in0=probs[:], scalar1=r2c[:, 2:3], scalar2=None,
                                        op0=ALU.mult)
                        v.drain()
                        v.max(out=vals8[:], in_=probs[:])
                        v.drain()
                        v.max_index(out=idxu[:], in_max=vals8[:], in_values=probs[:])
                        v.drain()
                        v.tensor_copy(out=idx8_all[:, i, :], in_=idxu[:])
                        v.tensor_reduce(out=r2c[:, 3:4], in_=vals8[:], axis=AX.X, op=ALU.add)
                        v.drain()
                        v.tensor_scalar(out=r2c[:, 4:5], in0=r2c[:, 3:4], scalar1=1e-12, scalar2=None,
                                        op0=ALU.max)
                        v.drain()
                        v.reciprocal(out=r2c[:, 5:6], in_=r2c[:, 4:5])
                        v.drain()
                        v.tensor_scalar(out=w8_all[:, i, :], in0=vals8[:], scalar1=r2c[:, 5:6],
                                        scalar2=None, op0=ALU.mult)
                        for k in range(4):
                            li = v.tensor_scalar(out=ohp[:, k, :], in0=iota32[:],
                                                 scalar1=idx8_all[:, i, k:k + 1], scalar2=None,
                                                 op0=ALU.is_equal)
                        dv.inc(li)
                    blk.vector(d2_dv2)
                    s3 = dv.n

                    for j in range(4):
                        def d2_dvj(v, i=i, j=j, s3=s3):
                            if j == 0:
                                v.wait_ge(dv.h, s3)
                            if snap_j_pe[0] is not None:
                                v.wait_ge(pe.h, snap_j_pe[0])
                            dw.inc(v.tensor_scalar(out=ohj[:], in0=iota32[:],
                                                   scalar1=idx8_all[:, i, SSEL + j:SSEL + j + 1],
                                                   scalar2=None, op0=ALU.is_equal))
                        blk.vector(d2_dvj)
                        s4 = dw.n

                        def d2_pej(t, i=i, j=j, s4=s4):
                            t.wait_ge(dw.h, s4)
                            if snap_j_dv[0] is not None:
                                t.wait_ge(dv.h, snap_j_dv[0])
                            pe.inc(t.transpose(rpt[:], ohj[:], identf[:]))
                        blk.tensor(d2_pej)
                        s5 = pe.n

                        def d2_dvj3(v, i=i, j=j, s5=s5):
                            v.wait_ge(pe.h, s5)
                            dw.inc(v.tensor_copy(out=ohjT[:], in_=rpt[:]))
                        blk.vector(d2_dvj3)
                        s6 = dw.n

                        def d2_pej3(t, i=i, j=j, s6=s6):
                            t.wait_ge(dw.h, s6)
                            pe.inc(t.matmul(rps1[:], ohjT[:], simd[:], start=True, stop=True))
                        blk.tensor(d2_pej3)
                        snap_j_pe[0] = pe.n
                        s7 = pe.n

                        def d2_dvj4(v, i=i, j=j, s7=s7):
                            v.wait_ge(pe.h, s7)
                            for k in range(4):
                                v.tensor_tensor(out=scr32[:], in0=rps1[:], in1=ohp[:, k, :],
                                                op=ALU.mult)
                                v.drain()
                                v.tensor_reduce(out=scj[:, k:k + 1], in_=scr32[:], axis=AX.X,
                                                op=ALU.add)
                            v.drain()
                            v.tensor_tensor(out=bstc[:, 0:1], in0=scj[:, 0:1], in1=scj[:, 1:2],
                                            op=ALU.max)
                            v.tensor_tensor(out=bstc[:, 1:2], in0=scj[:, 2:3], in1=scj[:, 3:4],
                                            op=ALU.max)
                            v.drain()
                            v.tensor_tensor(out=bstc[:, 0:1], in0=bstc[:, 0:1], in1=bstc[:, 1:2],
                                            op=ALU.max)
                            v.drain()
                            v.tensor_scalar(out=bstc[:, 2:3], in0=scj[:, 0:1], scalar1=bstc[:, 0:1],
                                            scalar2=None, op0=ALU.is_ge)
                            v.drain()
                            v.tensor_scalar(out=bstc[:, 4:5], in0=bstc[:, 2:3], scalar1=-1.0,
                                            scalar2=1.0, op0=ALU.mult, op1=ALU.add)
                            v.tensor_tensor(out=bstc[:, 6:7], in0=idx8_all[:, i, 0:1], in1=bstc[:, 2:3],
                                            op=ALU.mult)
                            for k in range(1, 4):
                                v.drain()
                                v.tensor_scalar(out=bstc[:, 2:3], in0=scj[:, k:k + 1],
                                                scalar1=bstc[:, 0:1], scalar2=None, op0=ALU.is_ge)
                                v.drain()
                                v.tensor_tensor(out=bstc[:, 3:4], in0=bstc[:, 2:3], in1=bstc[:, 4:5],
                                                op=ALU.mult)
                                v.drain()
                                v.scalar_tensor_tensor(out=bstc[:, 6:7], in0=idx8_all[:, i, k:k + 1],
                                                       scalar=bstc[:, 3:4], in1=bstc[:, 6:7],
                                                       op0=ALU.mult, op1=ALU.add)
                                v.tensor_scalar(out=bstc[:, 5:6], in0=bstc[:, 3:4], scalar1=-1.0,
                                                scalar2=1.0, op0=ALU.mult, op1=ALU.add)
                                v.drain()
                                v.tensor_tensor(out=bstc[:, 4:5], in0=bstc[:, 4:5], in1=bstc[:, 5:6],
                                                op=ALU.mult)
                            v.drain()
                            v.tensor_scalar(out=bstc[:, 7:8], in0=bstc[:, 0:1], scalar1=THR,
                                            scalar2=None, op0=ALU.is_lt)
                            v.drain()
                            v.tensor_tensor(out=scr32[:, 0:1], in0=idx8_all[:, i, SSEL + j:SSEL + j + 1],
                                            in1=bstc[:, 7:8], op=ALU.mult)
                            v.tensor_scalar(out=bstc[:, 7:8], in0=bstc[:, 7:8], scalar1=-1.0,
                                            scalar2=1.0, op0=ALU.mult, op1=ALU.add)
                            v.drain()
                            v.tensor_tensor(out=scr32[:, 1:2], in0=bstc[:, 6:7], in1=bstc[:, 7:8],
                                            op=ALU.mult)
                            v.drain()
                            dv.inc(v.tensor_tensor(out=idx8_all[:, i, SSEL + j:SSEL + j + 1],
                                                   in0=scr32[:, 0:1], in1=scr32[:, 1:2], op=ALU.add))
                        blk.vector(d2_dvj4)
                        snap_j_dv[0] = dv.n
                snapD2route = dv.n

                def d2_ci(v):
                    v.wait_ge(dv.h, snapD2route)
                    dv.inc(v.memset(carry[:], 0.0))
                blk.vector(d2_ci)
                snapPcar = dv.n

                first_carry = True
                snapPpe = None
                for s_ in range(TOPK):
                    for i in range(NTT):
                        def p_dv1(v, s_=s_, i=i, snapPpe=snapPpe):
                            if snapPpe is not None:
                                v.wait_ge(pe.h, snapPpe)
                            dw.inc(v.tensor_scalar(out=ohsi[:], in0=iota32[:],
                                                   scalar1=idx8_all[:, i, s_:s_ + 1], scalar2=None,
                                                   op0=ALU.is_equal))
                        blk.vector(p_dv1)
                        sP1 = dw.n

                        def p_pe(t, s_=s_, i=i, sP1=sP1, fc=first_carry, snapPcar=snapPcar):
                            t.wait_ge(dw.h, sP1)
                            t.wait_ge(dv.h, snapPcar)
                            t.matmul(rppre[:], ustrict[:], ohsi[:], start=True, stop=False,
                                     skip_group_check=True)
                            t.matmul(rppre[:], ones_r1[:], carry[:], start=False, stop=True,
                                     skip_group_check=True)
                            pe.inc(t.matmul(rpcar[:], ones_c[:], ohsi[:], start=fc, stop=True,
                                            skip_group_check=True))
                        blk.tensor(p_pe)
                        first_carry = False
                        snapPpe = pe.n
                        sP2 = pe.n

                        def p_dv2(v, s_=s_, i=i, sP2=sP2):
                            v.wait_ge(pe.h, sP2)
                            v.tensor_tensor(out=scr32[:], in0=rppre[:], in1=ohsi[:], op=ALU.mult)
                            v.tensor_copy(out=carry[:], in_=rpcar[:])
                            v.tensor_reduce(out=poscol[:, 0:1], in_=scr32[:], axis=AX.X, op=ALU.add)
                            v.drain()
                            v.tensor_scalar(out=poscol[:, 1:2], in0=idx8_all[:, i, s_:s_ + 1],
                                            scalar1=float(CAP), scalar2=poscol[:, 0:1],
                                            op0=ALU.mult, op1=ALU.add)
                            v.tensor_scalar(out=poscol[:, 2:3], in0=poscol[:, 0:1], scalar1=float(CAP),
                                            scalar2=None, op0=ALU.is_ge)
                            v.drain()
                            v.tensor_scalar(out=poscol[:, 3:4], in0=poscol[:, 2:3], scalar1=-1.0,
                                            scalar2=1.0, op0=ALU.mult, op1=ALU.add)
                            v.tensor_tensor(out=poscol[:, 5:6], in0=trash[:], in1=poscol[:, 2:3],
                                            op=ALU.mult)
                            v.drain()
                            v.tensor_tensor(out=poscol[:, 4:5], in0=poscol[:, 1:2], in1=poscol[:, 3:4],
                                            op=ALU.mult)
                            v.drain()
                            v.tensor_tensor(out=poscol[:, 6:7], in0=poscol[:, 4:5], in1=poscol[:, 5:6],
                                            op=ALU.add)
                            v.drain()
                            dv.inc(v.tensor_copy(out=posu_all[:, s_ * NTT + i:s_ * NTT + i + 1],
                                                 in_=poscol[:, 6:7]))
                        blk.vector(p_dv2)
                        snapPcar = dv.n
                snapD2pos = dv.n

            phase_barrier()
        if upto >= 7:
            with nc.sbuf_tensor("lnrow", [128, 2, DM], BF16) as lnrow:
                qa_d3 = qa.n
                snapD3sc = [None] * NTT
                for i in range(NTT):
                    def d3_sy(s, i=i):
                        if i == 0:
                            s.wait_ge(dv.h, snapD2pos)
                            s.wait_ge(qs.h, qs_d1_end)
                        if i >= 2:
                            s.wait_ge(qi.h, snapD3sc[i - 2])
                        qa.inc(s.dma_start(out=lnrow[:, i % 2, :],
                                           in_=ln2_d[i * 128:(i + 1) * 128, :]), 16)
                    blk.sync(d3_sy)

                    def d3_gp(g, i=i):
                        g.wait_ge(qa.h, qa_d3 + 16 * (i + 1))
                        if i == 0:
                            g.wait_ge(dv.h, snapD2pos)
                        for s_ in range(TOPK):
                            qi.inc(g.indirect_dma_start(
                                out=xdisp_d[:],
                                out_offset=IOA(ap=posu_all[:, s_ * NTT + i:s_ * NTT + i + 1], axis=0),
                                in_=lnrow[:, i % 2, :], in_offset=None), 16)
                    blk.gpsimd(d3_gp)
                    snapD3sc[i] = qi.n
                qi_d3_end = qi.n

            phase_barrier()
        if upto >= 8:
            with _multi(
                nc.sbuf_tensor("xgT", [128, 2, NKT, CAP], BF16),
                nc.sbuf_tensor("wgu", [128, 2, NKT, 128], BF16),
                nc.sbuf_tensor("wdt", [128, 6, DM], BF16),
                nc.sbuf_tensor("sg6", [128, 6, CAP], F32),
                nc.sbuf_tensor("htil", [128, 6, CAP], BF16),
                nc.sbuf_tensor("ytile", [128, 2, DM], BF16),
                nc.psum_tensor("gps", [128, 2, 512], F32),
                nc.psum_tensor("dnps", [128, 4, 512], F32),
                nc.psum_tensor("gptb", [128, 2, 1024], BF16),
            ) as (xgT, wgu, wdt, sg6, htil, ytile, gps, dnps, gptb):
                def d4_z(v):
                    dv.inc(v.memset(ytile[:, 0, :], 0.0))
                blk.vector(d4_z)
                sZ = dv.n

                def d4_zs(s, sZ=sZ):
                    s.wait_ge(dv.h, sZ)
                    qs.inc(s.dma_start(out=ydisp_d[NDISP:NDISP + 128, :], in_=ytile[:, 0, :]), 16)
                blk.sync(d4_zs)
                snapD4zs = qs.n

                qa_d4 = qa.n
                qwsnapG = {}
                qwsnapD = {}
                snapD4xT = [None] * NE
                snapD4gu = {}
                snapD4guE = [None] * NE
                snapD4dnE = [None] * NE
                snapD4uend = [None] * NE
                snapD4sg = {}
                snapD4xc = [None] * NE
                snapD4y = {}
                snapD4st = {}
                snapD4gt = {}
                snapD4gc = {}
                evD4_before = {}

                def d4_gp0(g):
                    g.wait_ge(qi.h, qi_d3_end)
                    for mt in range(2):
                        qw.inc(g.dma_start(out=wgu[:, mt % 2, :, :],
                                           in_=w_gu_t[0, mt].rearrange("(kt kp) m -> kp kt m",
                                                                       kp=128)), 16)
                        qwsnapG[(0, mt)] = qw.n
                    qw.inc(g.dma_start(out=wdt[:],
                                       in_=w_d_t[0].rearrange("(kt kp) d -> kp kt d", kp=128)), 16)
                    qwsnapD[0] = qw.n
                blk.gpsimd(d4_gp0)

                for e in range(NE):
                    ep = e % 2

                    def d4_syl(s, e=e, ep=ep):
                        if e == 0:
                            s.wait_ge(qi.h, qi_d3_end)
                        if e >= 2:
                            s.wait_ge(pe.h, snapD4guE[e - 2])
                        qa.inc(s.dma_start(out=xgT[:, ep, :, :],
                                           in_=xdisp_d[e * CAP:(e + 1) * CAP, :],
                                           transpose=True), 16)
                    blk.sync(d4_syl)

                    evD4_before[e] = ev.n
                    for mt in range(12):
                        def d4_pg(t, e=e, ep=ep, mt=mt):
                            t.wait_ge(qw.h, qwsnapG[(e, mt)])
                            if mt == 0:
                                t.wait_ge(qa.h, qa_d4 + 16 * (e + 1))
                            if mt >= 1:
                                t.wait_ge(ev.h, evD4_before[e] + 2 * mt)
                            for kt in range(NKT):
                                for nr in range(2):
                                    n0, n1 = (0, 512) if nr == 0 else (512, CAP)
                                    li = t.matmul(gps[:, nr, 0:n1 - n0], wgu[:, (e * 12 + mt) % 2, kt, :],
                                                  xgT[:, ep, kt, n0:n1],
                                                  start=(kt == 0), stop=(kt == NKT - 1),
                                                  skip_group_check=True)
                            pe.inc(li)
                        blk.tensor(d4_pg)
                        snapD4gu[(e, mt)] = pe.n

                        def d4_gpw(g, e=e, mt=mt):
                            g.wait_ge(pe.h, snapD4gu[(e, mt)])
                            if mt + 2 < 12:
                                e2, m2 = e, mt + 2
                            else:
                                e2, m2 = e + 1, mt + 2 - 12
                            if e2 < NE:
                                qw.inc(g.dma_start(out=wgu[:, (e2 * 12 + m2) % 2, :, :],
                                                   in_=w_gu_t[e2, m2].rearrange(
                                                       "(kt kp) m -> kp kt m", kp=128)), 16)
                                qwsnapG[(e2, m2)] = qw.n
                        blk.gpsimd(d4_gpw)

                        if mt < 6:
                            def d4_ev1(s, e=e, mt=mt):
                                s.wait_ge(pe.h, snapD4gu[(e, mt)])
                                if e >= 1 and mt == 0:
                                    s.wait_ge(ev.h, evD4_before[e - 1] + 24)
                                for nr in range(2):
                                    n0, n1 = (0, 512) if nr == 0 else (512, CAP)
                                    i2 = s.activation(out=sg6[:, mt, n0:n1],
                                                      in_=gps[:, nr, 0:n1 - n0], func=AF.Silu)
                                    ev.inc(i2)
                            blk.scalar(d4_ev1)
                            snapD4sg[(e, mt)] = ev.n
                        else:
                            def d4_ev2(v, e=e, mt=mt):
                                v.wait_ge(pe.h, snapD4gu[(e, mt)])
                                v.wait_ge(ev.h, snapD4sg[(e, mt - 6)])
                                if e >= 1 and mt == 6:
                                    v.wait_ge(pe.h, snapD4dnE[e - 1])
                                for nr in range(2):
                                    n0, n1 = (0, 512) if nr == 0 else (512, CAP)
                                    i2 = v.tensor_tensor(out=htil[:, mt - 6, n0:n1],
                                                         in0=gps[:, nr, 0:n1 - n0],
                                                         in1=sg6[:, mt - 6, n0:n1],
                                                         op=ALU.mult)
                                    ev.inc(i2)
                            blk.vector(d4_ev2)
                    snapD4guE[e] = pe.n
                    snapD4uend[e] = ev.n

                    for rt in range(CAP // 128):
                        def d4_pd(t, e=e, ep=ep, rt=rt):
                            if rt == 0:
                                t.wait_ge(ev.h, snapD4uend[e])
                                t.wait_ge(qw.h, qwsnapD[e])
                                if e == 0:
                                    t.wait_ge(qs.h, snapD4zs)
                            if rt >= 1:
                                t.wait_ge(dv.h, snapD4y[(e, rt - 1)])
                            for kt in range(6):
                                for nd in range(4):
                                    li = t.matmul(dnps[:, nd, :], htil[:, kt, rt * 128:(rt + 1) * 128],
                                                  wdt[:, kt, nd * 512:(nd + 1) * 512],
                                                  start=(kt == 0), stop=(kt == 5),
                                                  skip_group_check=True)
                            pe.inc(li)
                        blk.tensor(d4_pd)
                        snapD4dn_rt = pe.n

                        def d4_dy(v, e=e, rt=rt, snapD4dn_rt=snapD4dn_rt):
                            v.wait_ge(pe.h, snapD4dn_rt)
                            rpe = CAP // 128
                            idx2 = e * rpe + rt
                            if idx2 >= 2:
                                pidx = idx2 - 2
                                v.wait_ge(qs.h, snapD4st[(pidx // rpe, pidx % rpe)])
                            for nd in range(4):
                                ci = v.tensor_copy(out=ytile[:, rt % 2, nd * 512:(nd + 1) * 512],
                                                   in_=dnps[:, nd, :])
                            dv.inc(ci)
                        blk.vector(d4_dy)
                        snapD4y[(e, rt)] = dv.n

                        def d4_sy(s, e=e, rt=rt):
                            s.wait_ge(dv.h, snapD4y[(e, rt)])
                            r0 = e * CAP + rt * 128
                            qs.inc(s.dma_start(out=ydisp_d[r0:r0 + 128, :], in_=ytile[:, rt % 2, :]), 16)
                        blk.sync(d4_sy)
                        snapD4st[(e, rt)] = qs.n
                    snapD4dnE[e] = pe.n

                    if e + 1 < NE:
                        def d4_gpd(g, e=e):
                            g.wait_ge(pe.h, snapD4dnE[e])
                            qw.inc(g.dma_start(out=wdt[:],
                                               in_=w_d_t[e + 1].rearrange(
                                                   "(kt kp) d -> kp kt d", kp=128)), 16)
                            qwsnapD[e + 1] = qw.n
                        blk.gpsimd(d4_gpd)
                qs_d4_end = qs.n

            phase_barrier()
        if upto >= 9:
            with _multi(
                nc.sbuf_tensor("x2acc", [128, 2, DM], F32),
                nc.sbuf_tensor("ygt", [128, 4, DM], BF16),
            ) as (x2acc, ygt):
                qa_d5 = qa.n
                qi_d5 = qi.n
                snapD5stt = {}
                snapD5out = {}
                for i in range(NTT):
                    def d5_sy(s, i=i):
                        if i == 0:
                            s.wait_ge(qs.h, qs_d4_end)
                        if i >= 2:
                            s.wait_ge(qs.h, snapD5out[i - 2])
                        qa.inc(s.dma_start(out=x2acc[:, i % 2, :],
                                           in_=x2_d[i * 128:(i + 1) * 128, :]), 16)
                    blk.sync(d5_sy)

                    for s_ in range(TOPK):
                        it = i * TOPK + s_

                        def d5_gp(g, i=i, s_=s_, it=it):
                            if it == 0:
                                g.wait_ge(qs.h, qs_d4_end)
                            if it >= 4:
                                g.wait_ge(dv.h, snapD5stt[it - 4])
                            qi.inc(g.indirect_dma_start(
                                out=ygt[:, it % 4, :], out_offset=None,
                                in_=ydisp_d[:],
                                in_offset=IOA(ap=posu_all[:, s_ * NTT + i:s_ * NTT + i + 1],
                                              axis=0)), 16)
                        blk.gpsimd(d5_gp)

                        def d5_dv(v, i=i, s_=s_, it=it):
                            if s_ == 0:
                                v.wait_ge(qa.h, qa_d5 + 16 * (i + 1))
                            v.wait_ge(qi.h, qi_d5 + 16 * (it + 1))
                            dv.inc(v.scalar_tensor_tensor(out=x2acc[:, i % 2, :],
                                                          in0=ygt[:, it % 4, :],
                                                          scalar=w8_all[:, i, s_:s_ + 1],
                                                          in1=x2acc[:, i % 2, :],
                                                          op0=ALU.mult, op1=ALU.add))
                            snapD5stt[it] = dv.n
                        blk.vector(d5_dv)
                    sD5 = dv.n

                    def d5_out(s, i=i, sD5=sD5):
                        s.wait_ge(dv.h, sD5)
                        qs.inc(s.dma_start(out=out_d[i * 128:(i + 1) * 128, :],
                                           in_=x2acc[:, i % 2, :]), 16)
                    blk.sync(d5_out)
                    snapD5out[i] = qs.n

                def d5_fin(s):
                    s.wait_ge(qs.h, qs.n)
                blk.sync(d5_fin)


        if 1 <= upto < 2:
            ln1_cm.__exit__(None, None, None)
        if upto < 3:
            for cm in reversed(mres_cm):
                cm.__exit__(None, None, None)

        def _tail_fin(s):
            for s2 in (qs, qa, qw, qi, qc):
                s.wait_ge(s2.h, s2.n)
        blk.sync(_tail_fin)

        if dbg:
            scr = {"x_tm_d": x_tm_d, "sgate_d": sgate_d, "hgn_d": hgn_d,
                   "ln2_d": ln2_d, "x2_d": x2_d, "xdisp_d": xdisp_d, "ydisp_d": ydisp_d}

            def dbg_sync(s):
                s.wait_ge(qs.h, qs.n)
                s.wait_ge(dv.h, dv.n)
                s.wait_ge(ac.h, ac.n)
                s.wait_ge(gp.h, gp.n)
                s.wait_ge(qi.h, qi.n)
                for name, shape, dt in dbg:
                    if name == "ln1xT":
                        continue
                    if name == "posu_all":
                        qs.inc(s.dma_start(out=dbg_outs[name][:], in_=posu_all[:]), 16)
                        continue
                    if name == "w8_all":
                        qs.inc(s.dma_start(out=dbg_outs[name][:],
                                           in_=w8_all.rearrange("p a b -> p (a b)")), 16)
                        continue
                    if name == "idx8_all":
                        qs.inc(s.dma_start(out=dbg_outs[name][:],
                                           in_=idx8_all.rearrange("p a b -> p (a b)")), 16)
                        continue
                    src = scr[name]
                    sl = tuple(slice(0, z) for z in shape)
                    qs.inc(s.dma_start(out=dbg_outs[name][:], in_=src.__getitem__(sl)), 16)
                s.wait_ge(qs.h, qs.n)
            blk.sync(dbg_sync)

    return nc


def _consts():
    c = {}
    c["ident_bf"] = np.eye(128).astype(ml_dtypes.bfloat16)
    c["ident_f"] = np.eye(128, dtype=np.float32)
    c["iota32"] = np.broadcast_to(np.arange(NE, dtype=np.float32), (128, NE)).copy()
    c["trash_c"] = (NDISP + np.arange(128, dtype=np.float32)).reshape(128, 1)
    m = np.empty((2, 128, CH), np.float32)
    for st in range(2):
        s_glob = st * 128 + np.arange(128)[:, None]
        t_glob = np.arange(CH)[None, :]
        m[st] = np.where(t_glob >= s_glob, 0.0, MASKNEG)
    c["mask_st"] = m
    c["ustrict"] = np.triu(np.ones((128, 128), np.float32), 1)
    c["ones_r1"] = np.ones((1, 128), np.float32)
    c["ones_c"] = np.ones((128, 1), np.float32)
    bc = np.zeros((64, HEADS, 128), np.float32)
    for h in range(HEADS):
        bc[h, h, :] = 1.0
    c["bcsel"] = bc.astype(ml_dtypes.bfloat16)
    c["noti"] = (1.0 - np.eye(NE)).astype(np.float32)
    c["i32d"] = np.eye(NE, dtype=np.float32)
    mc = np.zeros((128, 8), np.float32)
    mc[:, 0] = 1.0
    mc[:, 1] = EPS
    mc[:, 2] = 1.0 / DM
    mc[:, 3] = 1.0 / DIN
    mc[:, 4] = -1.0
    c["misc_c"] = mc
    return c


def _prep_weights(inp):
    d = {}
    wt = np.ascontiguousarray(inp["L0_attn_in_proj_w"].T)
    pad = np.zeros((DM, NMT * 128 - PROJ), np.float32)
    wt = np.concatenate([wt, pad], axis=1)
    d["w_in_t"] = np.ascontiguousarray(wt.reshape(DM, NMT, 128).transpose(1, 0, 2))
    d["conv_w_t"] = np.ascontiguousarray(inp["L0_attn_conv_w"].reshape(34, 128, KCONV).transpose(1, 0, 2))
    d["conv_b_t"] = np.ascontiguousarray(inp["L0_attn_conv_b"].reshape(34, 128).T)
    d["dt_bias_c"] = np.ascontiguousarray(inp["L0_attn_dt_bias"].reshape(64, 1))
    d["a_log_c"] = np.ascontiguousarray(inp["L0_attn_A_log"].reshape(64, 1))
    d["d_row"] = np.ascontiguousarray(inp["L0_attn_D"].reshape(1, 64))
    d["norm_w_t"] = np.ascontiguousarray(inp["L0_attn_norm_w"].reshape(32, 128).T)
    d["w_out_t"] = np.ascontiguousarray(inp["L0_attn_out_proj_w"].T.reshape(32, 128, DM))
    d["w_g_t"] = np.ascontiguousarray(inp["L0_ffn_gate_weight"].T.reshape(NKT, 128, NE))
    wgu = inp["L0_ffn_gate_up_proj"]
    d["w_gu_t"] = np.ascontiguousarray(
        wgu.transpose(0, 2, 1).reshape(NE, DM, 12, 128).transpose(0, 2, 1, 3))
    d["w_d_t"] = np.ascontiguousarray(inp["L0_ffn_down_proj"].transpose(0, 2, 1))
    d["sim"] = np.ascontiguousarray(inp["L0_ffn_similarity"])
    return d


def kernel(**inputs):
    x = inputs["L0_attn_hidden_states"]
    shared = _prep_weights(inputs)
    shared.update(_consts())
    in_maps = []
    for c in range(8):
        m = dict(shared)
        m["x"] = np.ascontiguousarray(x[c])
        in_maps.append(m)
    nc = build_nc()
    res = run_bass_kernel_spmd(nc, in_maps, core_ids=list(range(8)))
    return np.stack([r["out"] for r in res.results]).astype(np.float32)


